# version 06
# baseline (speedup 1.0000x reference)
import numpy as np
import ml_dtypes
from contextlib import ExitStack

import concourse.bass as bass
import concourse.mybir as mybir
from concourse.bass_utils import run_bass_kernel_spmd

F32 = mybir.dt.float32
BF16 = mybir.dt.bfloat16
AF = mybir.ActivationFunctionType
ALU = mybir.AluOpType
AX = mybir.AxisListType
BF = ml_dtypes.bfloat16
DS = bass.ds

D = 1024
DC = 8
EPS = 1e-5

ENGS = ("tensor", "vector", "scalar", "gpsimd", "sync")


class Prog:

    def __init__(self, nc, stack):
        self.nc = nc
        self.stack = stack
        self.sems = {}
        for name in ("tensor", "vector", "scalar", "gpsimd"):
            self.sems[name] = stack.enter_context(nc.semaphore("sem_" + name))
        self.cnt = {k: 0 for k in self.sems}
        self.items = {e: [] for e in ENGS}
        self.scope = None

    def _stream_sem(self, stream):
        if stream not in self.sems:
            self.sems[stream] = self.stack.enter_context(
                self.nc.semaphore("sem_" + stream))
            self.cnt[stream] = 0
        return self.sems[stream]

    def op(self, eng, fn):
        self.cnt[eng] += 1
        self.items[eng].append(("op_mark", fn, self.sems[eng], 1, self.scope))
        return (eng, self.cnt[eng])

    def dma(self, eng, out, in_, stream="misc", **kw):
        sem = self._stream_sem("dq_" + stream)
        self.cnt["dq_" + stream] += 16
        self.items[eng].append(
            ("op_mark", lambda h, o=out, i=in_, k=kw: h.dma_start(out=o, in_=i, **k),
             sem, 16, self.scope))
        return ("dq_" + stream, self.cnt["dq_" + stream])

    def wait(self, eng, *toks):
        best = {}
        for t in toks:
            if t is None:
                continue
            best[t[0]] = max(best.get(t[0], 0), t[1])
        for name, val in best.items():
            self.items[eng].append(("wait", self.sems[name], val))

    def fence(self, eng):
        self.items[eng].append(("wait", self.sems[eng], self.cnt[eng]))

    def last(self, eng):
        return (eng, self.cnt[eng])

    def barrier(self):
        snap = dict(self.cnt)
        for e in ENGS:
            for name, val in snap.items():
                if val > 0:
                    self.items[e].append(("wait", self.sems[name], val))

    def emit(self):
        block = self.stack.enter_context(self.nc.Block())

        def runner(name):
            def run(handle):
                cur = [None, None]
                for item in self.items[name]:
                    if item[0] == "op_mark":
                        sc = item[4]
                        if sc != cur[0]:
                            if cur[0] is not None:
                                self.nc.leave_named_scope(cur[0], cur[1], False)
                            if sc is not None:
                                cur[1], _ = self.nc.enter_named_scope(sc, False)
                            cur[0] = sc
                        inst = item[1](handle)
                        inst.then_inc(item[2], item[3])
                    else:
                        handle.wait_ge(item[1], item[2])
                if cur[0] is not None:
                    self.nc.leave_named_scope(cur[0], cur[1], False)
            return run

        block.tensor(runner("tensor"))
        block.vector(runner("vector"))
        block.scalar(runner("scalar"))
        block.gpsimd(runner("gpsimd"))
        block.sync(runner("sync"))


def ln_stats(p, s, xa_ap, j, rl=None):
    sq, s1, s2 = s[f"sq{j}"], s[f"s1{j}"], s[f"s2{j}"]
    var, sd = s[f"var{j}"], s[f"sd{j}"]
    negm, rstd, m2n = s[f"negm{j}"], s[f"rstd{j}"], s[f"m2n{j}"]
    p.wait("scalar", s.get(("g", j)))
    p.op("scalar", lambda h: h.activation(out=sq[:, :], in_=xa_ap,
                                          func=AF.Square, accum_out=s2[:, :]))
    p.fence("scalar")
    p.op("vector", lambda h: h.reduce_sum(s1[:, :], xa_ap, axis=AX.X))
    p.fence("vector")
    p.op("vector", lambda h: h.tensor_scalar_mul(negm[:, :], s1[:, :], -1.0 / D))
    p.fence("vector")
    tm2 = p.op("vector", lambda h: h.scalar_tensor_tensor(
        out=m2n[:, :], in0=negm[:, :], scalar=-1.0, in1=negm[:, :],
        op0=ALU.mult, op1=ALU.mult))
    p.fence("vector")
    p.wait("scalar", tm2)
    p.op("scalar", lambda h: h.mul(var[:, :], s2[:, :], s["cInvD"][:, :]))
    p.fence("scalar")
    p.op("scalar", lambda h: h.add(var[:, :], var[:, :], m2n[:, :]))
    p.fence("scalar")
    if rl is not None:
        p.op("scalar", lambda h: h.mul(var[:, :], var[:, :], rl))
        p.fence("scalar")
        p.op("scalar", lambda h: h.mul(var[:, :], var[:, :], rl))
        p.fence("scalar")
    p.op("scalar", lambda h: h.add(var[:, :], var[:, :], s["cEps"][:, :]))
    p.fence("scalar")
    tsd = p.op("scalar", lambda h: h.activation(out=sd[:, :], in_=var[:, :],
                                                func=AF.Sqrt))
    p.wait("vector", tsd)
    p.op("vector", lambda h: h.reciprocal(rstd[:, :], sd[:, :]))
    p.fence("vector")
    if rl is not None:
        p.op("vector", lambda h: h.tensor_mul(rstd[:, :], rstd[:, :], rl))
        p.fence("vector")
    tok = p.op("vector", lambda h: h.engine_nop())
    s[("g", j)] = tok
    return negm, rstd, tok


def alloc_ln_scratch(nc, stack, s, tag=""):
    for j in (0, 1):
        s[f"sq{j}"] = _sb(nc, stack, f"lnsq{tag}{j}", [128, D], F32)
        s[f"s1{j}"] = _sb(nc, stack, f"lns1{tag}{j}", [128, 1], F32)
        s[f"s2{j}"] = _sb(nc, stack, f"lns2{tag}{j}", [128, 1], F32)
        s[f"var{j}"] = _sb(nc, stack, f"lnvar{tag}{j}", [128, 1], F32)
        s[f"sd{j}"] = _sb(nc, stack, f"lnsd{tag}{j}", [128, 1], F32)
        s[f"negm{j}"] = _sb(nc, stack, f"lnnegm{tag}{j}", [128, 1], F32)
        s[f"rstd{j}"] = _sb(nc, stack, f"lnrstd{tag}{j}", [128, 1], F32)
        s[f"m2n{j}"] = _sb(nc, stack, f"lnm2n{tag}{j}", [128, 1], F32)
    s["cInvD"] = _sb(nc, stack, f"lncid{tag}", [128, 1], F32)
    s["cEps"] = _sb(nc, stack, f"lnceps{tag}", [128, 1], F32)


def _sb(nc, stack, name, shape, dtype):
    return stack.enter_context(nc.sbuf_tensor(name, shape, dtype))


def _psum(nc, stack, name, shape, dtype):
    return stack.enter_context(nc.psum_tensor(name, shape, dtype))


def build_launch1(T=4096, do_ffn=True, dbg_x1=False, phases=3, p1_parts=("ln", "qkv", "conv")):
    TC = T // 512
    QT = T // 128
    nc = bass.Bass("TRN2", target_bir_lowering=False)

    def din(name, shape, dt=F32):
        return nc.dram_tensor(name, shape, dt, kind="ExternalInput").ap()

    def dout(name, shape, dt=F32):
        return nc.dram_tensor(name, shape, dt, kind="ExternalOutput").ap()

    def dint(name, shape, dt):
        return nc.dram_tensor(name, shape, dt, kind="Internal").ap()

    x_in = din("x", [T, D])
    rn_cols = din("rn_cols", [128, T // 128])
    Wqkv = din("Wqkv", [D, 3 * D])
    bq_cols = din("bq_cols", [128, 24])
    wv_cols = din("wv_cols", [128, 24])
    identb = din("identb", [128, 128], BF16)
    identf = din("identf", [128, 128], F32)
    maskadd = din("maskadd", [128, 128])
    x2_out = dout("x2", [T, D])
    x1_out = dout("x1dbg", [T, D]) if dbg_x1 else None

    Wqkv_bf = dint("Wqkv_bf", [D, 3 * D], BF16)
    q_sp = dint("q_sp", [DC, 128, T], BF16)
    wvn_sp = dint("wvn_sp", [QT, 128, D], BF16)
    x1h = dint("x1h", [T, D], F32)

    if do_ffn:
        WQT = din("WQT", [D, D]); WKT = din("WKT", [D, D])
        WinvT = din("WinvT", [D, D]); rw1T = din("rw1T", [D, D])
        rw2T = din("rw2T", [D, 8]); WOT = din("WOT", [D, D])
        Varr_f = din("Varr_f", [D, 512]); Warr_f = din("Warr_f", [D, 512])
        Varr_i = din("Varr_i", [D, 512]); Warr_i = din("Warr_i", [D, 512])
        UT_f = din("UT_f", [512, D]); UT_i = din("UT_i", [512, D])
        b_f = din("b_f", [8, D]); b_i = din("b_i", [8, D])
        rb1_cols = din("rb1_cols", [128, 8])
        rb2 = din("rb2", [1, 8]); eb = din("eb", [1, 8])
        alpha_in = din("alpha", [128, 1])
        Ecat = din("Ecat", [8, 512], BF16)
        onesrow = din("onesrow", [1, 128], BF16)
        ones512 = din("ones512", [128, 512])
        ffn_bf = {}
        for nm, src, shp in (
            ("WQT", WQT, [D, D]), ("WKT", WKT, [D, D]), ("WinvT", WinvT, [D, D]),
            ("rw1T", rw1T, [D, D]), ("rw2T", rw2T, [D, 8]), ("WOT", WOT, [D, D]),
            ("Varr_f", Varr_f, [D, 512]), ("Warr_f", Warr_f, [D, 512]),
            ("Varr_i", Varr_i, [D, 512]), ("Warr_i", Warr_i, [D, 512]),
            ("UT_f", UT_f, [512, D]), ("UT_i", UT_i, [512, D]),
            ("b_f", b_f, [8, D]), ("b_i", b_i, [8, D]),
        ):
            ffn_bf[nm] = (dint(nm + "_bf", shp, BF16), src)

    stack = ExitStack()
    with stack:
        p = Prog(nc, stack)

        tident = _sb(nc, stack, "tident", [128, 128], BF16)
        tidentf = _sb(nc, stack, "tidentf", [128, 128], F32)
        tmask = _sb(nc, stack, "tmask", [128, 128], F32)
        tbqc = _sb(nc, stack, "tbqc", [128, 24], F32)
        tbqs = _sb(nc, stack, "tbqs", [128, 8], F32)
        twvc = _sb(nc, stack, "twvc", [128, 24], F32)
        c32 = _sb(nc, stack, "c32", [128, 1], F32)
        trn = _sb(nc, stack, "trn", [128, T // 128], F32)
        s = {}
        alloc_ln_scratch(nc, stack, s)
        p.op("vector", lambda h: h.memset(s["cInvD"][:, :], 1.0 / D))
        t_lnc = p.op("vector", lambda h: h.memset(s["cEps"][:, :], EPS))
        p.fence("vector")
        s[("g", 0)] = t_lnc
        s[("g", 1)] = t_lnc

        p.scope = "l1p0"
        t_ident = p.dma("sync", tident[:, :], identb, stream="ic0")
        p.dma("sync", tidentf[:, :], identf, stream="ic1")
        t_mask = p.dma("sync", tmask[:, :], maskadd, stream="ic2")
        t_bqc = p.dma("sync", tbqc[:, :], bq_cols, stream="ic3")
        t_wvc = p.dma("sync", twvc[:, :], wv_cols, stream="ic4")
        t_rn = p.dma("sync", trn[:, :], rn_cols, stream="ic5")
        t_Wq = p.dma("gpsimd", Wqkv_bf, Wqkv, stream="gpq")
        if do_ffn:
            for nm, (dst, src) in ffn_bf.items():
                p.dma("gpsimd", dst, src, stream="gpc")
        p.wait("vector", t_bqc)
        p.op("vector", lambda h: h.tensor_scalar_mul(tbqs[:, :], tbqc[:, 16:24], 1.0 / 32.0))
        p.op("vector", lambda h: h.memset(c32[:, :], 1.0 / 32.0))
        p.fence("vector")

        if phases == 0:
            tcp = p.dma("sync", x2_out, x_in, stream="fin")
            if dbg_x1:
                p.dma("sync", x1_out, x_in, stream="fin")
            p.barrier()
            p.emit()
            print("launch1 built (phase0)")
            return nc
        p.scope = "l1p1"
        ph12 = ExitStack()
        kfm = _sb(nc, ph12, "kfm", [128, DC, T], BF16)
        Ctm = _sb(nc, ph12, "Ctm", [128, QT, D], BF16)
        with ExitStack() as ph1:
            xtb = [_sb(nc, ph1, f"xtb{i}", [128, D], F32) for i in range(2)]
            htm = _sb(nc, ph1, "htm", [128, D], BF16)
            hfm = _sb(nc, ph1, "hfm", [128, DC, 512], BF16)
            Wb = [_sb(nc, ph1, f"Wb{i}", [128, DC, 384], BF16) for i in range(2)]
            vbuf = [_sb(nc, ph1, f"vbuf{i}", [128, DC, 512], BF16) for i in range(2)]
            vlast = _sb(nc, ph1, "vlast", [128, DC, TC], BF16)
            qsb = [_sb(nc, ph1, f"qsb{i}", [128, 512], BF16) for i in range(2)]
            Cfm = _sb(nc, ph1, "Cfm", [128, DC, 512], BF16)
            wvfm = _sb(nc, ph1, "wvfm", [128, DC, 512], BF16)
            wsb = [_sb(nc, ph1, f"wsb{i}", [128, D], BF16) for i in range(2)]
            psQ = [_psum(nc, ph1, f"psQ{i}", [128, 512], F32) for i in range(2)]
            psTrH = _psum(nc, ph1, "psTrH", [128, D], BF16)
            psTrC = _psum(nc, ph1, "psTrC", [128, D], BF16)

            WqkvR = Wqkv_bf.rearrange("(c p) n -> p c n", p=128)
            t_vev = {}
            t_conv = {}
            t_psev = [None, None]
            t_qspill = [None, None]
            t_wspill = [None, None]
            t_hread = {}

            def emit_conv(ct, last):
                vb = vbuf[ct % 2]
                vnb = vbuf[(ct + 1) % 2]
                p.wait("vector", t_vev[ct], t_wvc)
                if not last:
                    p.wait("vector", t_vev[ct + 1])
                for dc in range(DC):
                    w0 = twvc[:, dc:dc + 1]
                    w1 = twvc[:, 8 + dc:9 + dc]
                    w2 = twvc[:, 16 + dc:17 + dc]
                    p.op("vector", lambda h, dc=dc, w1=w1, vb=vb: h.tensor_scalar_mul(
                        Cfm[:, dc, :], vb[:, dc, :], w1))
                    p.fence("vector")
                    p.op("vector", lambda h, dc=dc, w0=w0, vb=vb: h.scalar_tensor_tensor(
                        out=Cfm[:, dc, 1:512], in0=vb[:, dc, 0:511], scalar=w0,
                        in1=Cfm[:, dc, 1:512], op0=ALU.mult, op1=ALU.add))
                    p.fence("vector")
                    if ct > 0:
                        p.op("vector", lambda h, dc=dc, w0=w0, ct=ct: h.scalar_tensor_tensor(
                            out=Cfm[:, dc, 0:1], in0=vlast[:, dc, ct - 1:ct],
                            scalar=w0, in1=Cfm[:, dc, 0:1], op0=ALU.mult, op1=ALU.add))
                        p.fence("vector")
                    p.op("vector", lambda h, dc=dc, w2=w2, vb=vb: h.scalar_tensor_tensor(
                        out=Cfm[:, dc, 0:511], in0=vb[:, dc, 1:512], scalar=w2,
                        in1=Cfm[:, dc, 0:511], op0=ALU.mult, op1=ALU.add))
                    p.fence("vector")
                    p.op("vector", lambda h, dc=dc, w2=w2, vb=vb: h.tensor_scalar_mul(
                        wvfm[:, dc, 0:511], vb[:, dc, 1:512], w2))
                    p.fence("vector")
                    if not last:
                        p.op("vector", lambda h, dc=dc, w2=w2, vnb=vnb: h.scalar_tensor_tensor(
                            out=Cfm[:, dc, 511:512], in0=vnb[:, dc, 0:1], scalar=w2,
                            in1=Cfm[:, dc, 511:512], op0=ALU.mult, op1=ALU.add))
                        p.fence("vector")
                        p.op("vector", lambda h, dc=dc, w2=w2, vnb=vnb: h.tensor_scalar_mul(
                            wvfm[:, dc, 511:512], vnb[:, dc, 0:1], w2))
                    else:
                        p.op("vector", lambda h, dc=dc: h.memset(wvfm[:, dc, 511:512], 0.0))
                    p.fence("vector")
                tcv = p.op("vector", lambda h: h.tensor_copy(Cfm[:, 0, 0:1], Cfm[:, 0, 0:1]))
                t_conv[ct] = tcv
                t_prev_ev = tcv
                for kcl in range(4):
                    kc = ct * 4 + kcl
                    p.wait("tensor", t_prev_ev, t_ident)
                    for dc in range(DC):
                        p.op("tensor", lambda h, dc=dc, kcl=kcl: h.transpose(
                            psTrC[:, dc * 128:(dc + 1) * 128],
                            Cfm[:, dc, kcl * 128:(kcl + 1) * 128], tident[:, :]))
                    tt = p.last("tensor")
                    p.wait("vector", tt)
                    tctm = p.op("vector", lambda h, kc=kc: h.tensor_copy(
                        Ctm[:, kc, :], psTrC[:, :]))
                    p.wait("tensor", tctm)
                    for dc in range(DC):
                        p.op("tensor", lambda h, dc=dc, kcl=kcl: h.transpose(
                            psTrC[:, dc * 128:(dc + 1) * 128],
                            wvfm[:, dc, kcl * 128:(kcl + 1) * 128], tident[:, :]))
                    tt = p.last("tensor")
                    p.wait("vector", tt, t_wspill[kcl % 2])
                    tw = p.op("vector", lambda h, kcl=kcl: h.tensor_copy(
                        wsb[kcl % 2][:, :], psTrC[:, :]))
                    t_prev_ev = tw
                    p.wait("sync", tw)
                    t_wspill[kcl % 2] = p.dma("sync", wvn_sp[kc, :, :], wsb[kcl % 2][:, :], stream=f"wv{kcl % 2}")

            OGS = [(tc, og) for tc in range(TC) for og in range(8)]
            t_wdma = {}
            p.wait("sync", t_Wq)
            for j in (range(2) if "qkv" in p1_parts else range(0)):
                tcj, ogj = OGS[j]
                t_wdma[OGS[j]] = p.dma("sync", Wb[ogj % 2][:, :, :],
                                       WqkvR[:, :, ogj * 384:(ogj + 1) * 384],
                                       stream=f"wb{ogj % 2}")

            t_xtb_rd = [None, None]
            t_htr = [None]
            for tc in range(TC):
                ttrs = []
                for st_ in range(4):
                    g = (tc * 4 + st_) % 2
                    r0 = tc * 512 + st_ * 128
                    p.wait("sync", t_xtb_rd[g])
                    tx = p.dma("sync", xtb[g][:, :], x_in[r0:r0 + 128, :],
                               stream=f"xt{g}")
                    xa = xtb[g][:, :]
                    p.wait("scalar", tx)
                    p.wait("vector", tx)
                    negm, rstd, tln = ln_stats(p, s, xa, st_ % 2)
                    p.wait("vector", t_htr[0])
                    tok = p.op("vector", lambda h, xa=xa, negm=negm, rstd=rstd:
                               h.tensor_scalar(out=htm[:, :], in0=xa,
                                               scalar1=negm[:, :], scalar2=rstd[:, :],
                                               op0=ALU.add, op1=ALU.mult))
                    p.fence("vector")
                    p.wait("tensor", tok, t_ident)
                    if tc > 0:
                        p.wait("tensor", t_hread[tc - 1])
                    if ttrs:
                        p.wait("tensor", ttrs[-1])
                    for dc in range(DC):
                        p.op("tensor", lambda h, dc=dc: h.transpose(
                            psTrH[:, dc * 128:(dc + 1) * 128],
                            htm[:, dc * 128:(dc + 1) * 128], tident[:, :]))
                    ttr = p.last("tensor")
                    t_htr[0] = ttr
                    p.wait("vector", ttr)
                    tev = p.op("vector", lambda h, st_=st_: h.tensor_copy(
                        hfm[:, :, st_ * 128:(st_ + 1) * 128],
                        psTrH[:, :].rearrange("p (c t) -> p c t", c=DC)))
                    ttrs.append(tev)
                    t_xtb_rd[g] = tok
                t_hfm_done = ttrs[-1]

                p.wait("tensor", t_hfm_done)
                t_read_og = None
                for og in (range(8) if "qkv" in p1_parts else range(0)):
                    gi = tc * 8 + og
                    p.wait("tensor", t_wdma[OGS[gi]], t_Wq)
                    for oc in range(3):
                        ocg = og * 3 + oc
                        slot = ocg % 2
                        p.wait("tensor", t_psev[slot])
                        for dc in range(DC):
                            p.op("tensor", lambda h, slot=slot, og=og, oc=oc, dc=dc:
                                 h.matmul(psQ[slot][:, :],
                                          Wb[og % 2][:, dc, oc * 128:(oc + 1) * 128],
                                          hfm[:, dc, :],
                                          start=(dc == 0), stop=(dc == DC - 1)))
                        tmm = p.last("tensor")
                        if "qkvcopy" in p1_parts:
                            p.wait("vector", tmm)
                            t_psev[slot] = p.op("vector", lambda h, slot=slot: h.tensor_copy(
                                qsb[slot][:, :], psQ[slot][:, :]))
                            t_vev[tc] = t_psev[slot]
                            continue
                        p.wait("scalar", tmm)
                        if ocg >= 16:
                            p.wait("scalar", t_qspill[slot])
                            tev = p.op("scalar", lambda h, slot=slot, ocg=ocg: h.activation(
                                out=qsb[slot][:, :], in_=psQ[slot][:, :], func=AF.Identity,
                                bias=tbqs[:, ocg - 16:ocg - 15], scale=c32[:, :]))
                            p.fence("scalar")
                            if "nospill" not in p1_parts:
                                t_qspill[slot] = p.dma(
                                    "scalar", q_sp[ocg - 16, :, tc * 512:(tc + 1) * 512],
                                    qsb[slot][:, :], stream=f"qs{slot}")
                            t_psev[slot] = tev
                        elif ocg >= 8:
                            tev = p.op("scalar", lambda h, slot=slot, ocg=ocg, tc=tc: h.activation(
                                out=kfm[:, ocg - 8, tc * 512:(tc + 1) * 512],
                                in_=psQ[slot][:, :], func=AF.Identity,
                                bias=tbqc[:, ocg:ocg + 1]))
                            t_psev[slot] = tev
                        else:
                            if tc >= 2:
                                p.wait("scalar", t_conv[tc - 2])
                            tev = p.op("scalar", lambda h, slot=slot, ocg=ocg, tc=tc: h.activation(
                                out=vbuf[tc % 2][:, ocg, :],
                                in_=psQ[slot][:, :], func=AF.Identity,
                                bias=tbqc[:, ocg:ocg + 1]))
                            t_psev[slot] = tev
                            t_vev[tc] = tev
                    t_read_og = p.last("tensor")
                    if gi + 2 < len(OGS):
                        tcn, ogn = OGS[gi + 2]
                        p.wait("sync", t_read_og)
                        t_wdma[OGS[gi + 2]] = p.dma(
                            "sync", Wb[ogn % 2][:, :, :],
                            WqkvR[:, :, ogn * 384:(ogn + 1) * 384],
                            stream=f"wb{ogn % 2}")
                    if og == 2 and "conv" in p1_parts:
                        p.wait("vector", t_vev[tc])
                        p.op("vector", lambda h, tc=tc: h.tensor_copy(
                            vlast[:, :, tc:tc + 1], vbuf[tc % 2][:, :, 511:512]))
                        p.fence("vector")
                        if tc >= 1:
                            emit_conv(tc - 1, last=False)
                t_hread[tc] = t_read_og
            if "conv" in p1_parts:
                emit_conv(TC - 1, last=True)
            p.barrier()
        if phases <= 1:
            ph12.close()
            tcp = p.dma("sync", x2_out, x_in, stream="fin")
            if dbg_x1:
                p.dma("sync", x1_out, x_in, stream="fin")
            p.barrier()
            p.emit()
            print("launch1 built (phase1)")
            return nc

        p.scope = "l1p2"
        with ExitStack() as ph2:
            Srow = _sb(nc, ph2, "Srow", [128, T], F32)
            Prow = _sb(nc, ph2, "Prow", [128, T], BF16)
            PTb = _sb(nc, ph2, "PTb", [128, T], BF16)
            qb = [_sb(nc, ph2, f"qb{i}", [128, DC, 128], BF16) for i in range(2)]
            wvt = [_sb(nc, ph2, f"wvt{i}", [128, D], BF16) for i in range(2)]
            xb2 = [_sb(nc, ph2, f"xb2{i}", [128, D], F32) for i in range(2)]
            ytile = _sb(nc, ph2, "ytile", [128, D], F32)
            ytmp = _sb(nc, ph2, "ytmp", [128, D], F32)
            xot = [_sb(nc, ph2, f"xot{i}", [128, D], F32) for i in range(2)]
            negmS = _sb(nc, ph2, "negmS", [128, 1], F32)
            sumP = _sb(nc, ph2, "sumP", [128, 1], F32)
            rlP = _sb(nc, ph2, "rlP", [128, 1], F32)
            dtmp = _sb(nc, ph2, "dtmp", [128, 128], F32)
            diagP = _sb(nc, ph2, "diagP", [128, 1], F32)
            negdiag = _sb(nc, ph2, "negdiag", [128, 1], F32)
            psS = [_psum(nc, ph2, f"psS{i}", [128, 512], F32) for i in range(4)]
            psO = _psum(nc, ph2, "psO", [128, D], F32)
            psTrP = _psum(nc, ph2, "psTrP", [128, 512], BF16)

            q_spR = q_sp.transpose([1, 0, 2])
            t_qld = {}
            t_wld = {}
            t_xld = {}
            for qi in range(min(2, QT)):
                t_qld[qi] = p.dma("sync", qb[qi % 2][:, :, :],
                                  q_spR[:, :, qi * 128:(qi + 1) * 128],
                                  stream=f"qb{qi % 2}")
                t_wld[qi] = p.dma("sync", wvt[qi % 2][:, :], wvn_sp[qi, :, :], stream=f"wt{qi % 2}")
                t_xld[qi] = p.dma("sync", xb2[qi % 2][:, :],
                                  x_in[qi * 128:(qi + 1) * 128, :],
                                  stream=f"xb{qi % 2}")
            t_sev = [None] * 4
            t_smm = {}
            t_y = None
            t_xst = [None, None]

            for qi in range(QT):
                width = (qi + 1) * 128
                nkb = (width + 511) // 512
                p.wait("tensor", t_qld[qi])
                for kb in range(nkb):
                    cols = min(512, width - kb * 512)
                    slot = kb % 4
                    p.wait("tensor", t_sev[slot])
                    for dc in range(DC):
                        p.op("tensor", lambda h, slot=slot, dc=dc, kb=kb, cols=cols, qi=qi:
                             h.matmul(psS[slot][:, 0:cols],
                                      qb[qi % 2][:, dc, :],
                                      kfm[:, dc, kb * 512:kb * 512 + cols],
                                      start=(dc == 0), stop=(dc == DC - 1)))
                    tmm = p.last("tensor")
                    p.wait("scalar", tmm)
                    t_sev[slot] = p.op("scalar", lambda h, slot=slot, kb=kb, cols=cols:
                                       h.activation(out=Srow[:, kb * 512:kb * 512 + cols],
                                                    in_=psS[slot][:, 0:cols],
                                                    func=AF.Identity))
                t_smm[qi] = tmm
                p.wait("vector", t_sev[slot], t_mask)
                p.op("vector", lambda h, qi=qi: h.tensor_add(
                    Srow[:, qi * 128:(qi + 1) * 128],
                    Srow[:, qi * 128:(qi + 1) * 128], tmask[:, :]))
                p.fence("vector")
                tmax = p.op("vector", lambda h, width=width: h.reduce_max(
                    negmS[:, :], Srow[:, 0:width], axis=AX.X, negate=True))
                p.fence("vector")
                p.wait("scalar", tmax)
                texp = p.op("scalar", lambda h, width=width: h.activation(
                    out=Prow[:, 0:width], in_=Srow[:, 0:width], func=AF.Exp,
                    bias=negmS[:, :], accum_out=sumP[:, :]))
                p.wait("vector", texp)
                p.op("vector", lambda h, qi=qi: h.tensor_mul(
                    dtmp[:, :], Prow[:, qi * 128:(qi + 1) * 128], tident[:, :]))
                p.fence("vector")
                p.op("vector", lambda h: h.reduce_sum(diagP[:, :], dtmp[:, :], axis=AX.X))
                p.fence("vector")
                tnd = p.op("vector", lambda h: h.tensor_scalar_mul(
                    negdiag[:, :], diagP[:, :], -1.0))
                p.fence("vector")
                t_ptev = {}
                ngrp = (qi + 1 + 3) // 4
                prev_ev = None
                for g in range(ngrp):
                    kcs = range(g * 4, min(qi + 1, g * 4 + 4))
                    p.wait("tensor", texp if g == 0 else None, prev_ev)
                    for j, kc in enumerate(kcs):
                        p.op("tensor", lambda h, j=j, kc=kc: h.transpose(
                            psTrP[:, j * 128:(j + 1) * 128],
                            Prow[:, kc * 128:(kc + 1) * 128], tident[:, :]))
                    tt = p.last("tensor")
                    p.wait("vector", tt)
                    prev_ev = p.op("vector", lambda h, g=g, n=len(list(kcs)): h.tensor_copy(
                        PTb[:, g * 512:g * 512 + n * 128], psTrP[:, 0:n * 128]))
                    t_ptev[g] = prev_ev
                p.wait("tensor", t_y, t_ptev[ngrp - 1])
                for kc in range(qi + 1):
                    p.op("tensor", lambda h, kc=kc, qi=qi: h.matmul(
                        psO[:, 0:512], PTb[:, kc * 128:(kc + 1) * 128],
                        Ctm[:, kc, 0:512], start=(kc == 0), stop=(kc == qi)))
                    p.op("tensor", lambda h, kc=kc, qi=qi: h.matmul(
                        psO[:, 512:1024], PTb[:, kc * 128:(kc + 1) * 128],
                        Ctm[:, kc, 512:1024], start=(kc == 0), stop=(kc == qi)))
                tpv = p.last("tensor")
                p.wait("vector", tpv, t_wld[qi], tnd)
                ty = p.op("vector", lambda h, qi=qi: h.scalar_tensor_tensor(
                    out=ytile[:, :], in0=wvt[qi % 2][:, :], scalar=negdiag[:, :],
                    in1=psO[:, :], op0=ALU.mult, op1=ALU.add))
                t_y = ty
                p.fence("vector")
                if qi + 2 < QT:
                    p.wait("sync", t_smm[qi])
                    t_qld[qi + 2] = p.dma("sync", qb[qi % 2][:, :, :],
                                          q_spR[:, :, (qi + 2) * 128:(qi + 3) * 128],
                                          stream=f"qb{qi % 2}")
                    p.wait("sync", ty)
                    t_wld[qi + 2] = p.dma("sync", wvt[qi % 2][:, :], wvn_sp[qi + 2, :, :], stream=f"wt{qi % 2}")
                p.op("vector", lambda h: h.reciprocal(rlP[:, :], sumP[:, :]))
                p.fence("vector")
                p.wait("scalar", ty)
                negm, rstd, tln = ln_stats(p, s, ytile[:, :], 0, rl=rlP[:, :])
                p.op("vector", lambda h, negm=negm, rstd=rstd: h.tensor_scalar(
                    out=ytmp[:, :], in0=ytile[:, :], scalar1=negm[:, :],
                    scalar2=rstd[:, :], op0=ALU.add, op1=ALU.mult))
                p.fence("vector")
                p.wait("vector", t_xld[qi], t_xst[qi % 2])
                tres = p.op("vector", lambda h, qi=qi: h.tensor_add(
                    xot[qi % 2][:, :], xb2[qi % 2][:, :], ytmp[:, :]))
                p.fence("vector")
                p.wait("sync", tres)
                if qi + 2 < QT:
                    t_xld[qi + 2] = p.dma("sync", xb2[qi % 2][:, :],
                                          x_in[(qi + 2) * 128:(qi + 3) * 128, :],
                                          stream=f"xb{qi % 2}")
                tst = p.dma("sync", x1h[qi * 128:(qi + 1) * 128, :], xot[qi % 2][:, :], stream=f"xo{qi % 2}")
                if dbg_x1:
                    tst = p.dma("sync", x1_out[qi * 128:(qi + 1) * 128, :], xot[qi % 2][:, :], stream=f"xo{qi % 2}")
                t_xst[qi % 2] = tst
            p.barrier()
        ph12.close()
        if phases <= 2:
            tcp = p.dma("sync", x2_out, x1h, stream="fin")
            p.barrier()
            p.emit()
            print("launch1 built (phase2)")
            return nc

        p.scope = "l1p3"
        if do_ffn:
            with ExitStack() as ph3:
                psl = [_sb(nc, ph3, f"psl{i}", [128, DC, 256], BF16) for i in range(2)]
                R2b = _sb(nc, ph3, "R2b", [128, DC, 8], BF16)
                WOb = _sb(nc, ph3, "WOb", [128, DC, D], BF16)
                UTfb = [_sb(nc, ph3, f"UTfb{i}", [128, 4, 256], BF16) for i in range(2)]
                UTib = [_sb(nc, ph3, f"UTib{i}", [128, 4, 256], BF16) for i in range(2)]
                bfb = _sb(nc, ph3, "bfb", [8, D], BF16)
                bib = _sb(nc, ph3, "bib", [8, D], BF16)
                Eb = _sb(nc, ph3, "Eb", [8, 512], BF16)
                trb1 = _sb(nc, ph3, "trb1", [128, 8], F32)
                tr2eb = _sb(nc, ph3, "tr2eb", [1, 8], F32)
                tr2ebb = _sb(nc, ph3, "tr2ebb", [1, 8], BF16)
                tr2a = _sb(nc, ph3, "tr2a", [1, 8], F32)
                tr2b_ = _sb(nc, ph3, "tr2b_", [1, 8], F32)
                tonesr = _sb(nc, ph3, "tonesr", [1, 128], BF16)
                to512 = _sb(nc, ph3, "to512", [128, 512], F32)
                ones512b = _sb(nc, ph3, "ones512b", [1, 512], BF16)
                alc = _sb(nc, ph3, "alc", [128, 1], F32)
                hfm3 = _sb(nc, ph3, "hfm3", [128, DC, 512], BF16)
                htm3 = _sb(nc, ph3, "htm3", [128, D], BF16)
                xtb3 = [_sb(nc, ph3, f"xtb3{i}", [128, 4, D], F32) for i in range(2)]
                Qfm = _sb(nc, ph3, "Qfm", [128, DC, 512], BF16)
                Kfm3 = _sb(nc, ph3, "Kfm3", [128, DC, 512], BF16)
                iQfm = _sb(nc, ph3, "iQfm", [128, DC, 512], BF16)
                iKfm = _sb(nc, ph3, "iKfm", [128, DC, 512], BF16)
                xVf = _sb(nc, ph3, "xVf", [128, 4, 512], BF16)
                xVi = _sb(nc, ph3, "xVi", [128, 4, 512], BF16)
                cumf = _sb(nc, ph3, "cumf", [128, 4, 512], F32)
                cumi = _sb(nc, ph3, "cumi", [128, 4, 512], F32)
                carryf = _sb(nc, ph3, "carryf", [128, 4], F32)
                carryi = _sb(nc, ph3, "carryi", [128, 4], F32)
                hrfm = [_sb(nc, ph3, f"hrfm{i}", [128, DC, 512], BF16) for i in range(2)]
                lgfm = [_sb(nc, ph3, f"lgfm{i}", [8, 512], F32) for i in range(2)]
                lgt = [_sb(nc, ph3, f"lgt{i}", [128, 4, 8], F32) for i in range(2)]
                lmx = [_sb(nc, ph3, f"lmx{i}", [128, 4], F32) for i in range(2)]
                lsm = [_sb(nc, ph3, f"lsm{i}", [128, 4], F32) for i in range(2)]
                wgtb = [_sb(nc, ph3, f"wgtb{i}", [128, 32], BF16) for i in range(2)]
                wgtTf = _sb(nc, ph3, "wgtTf", [8, 512], BF16)
                wgtTi = _sb(nc, ph3, "wgtTi", [8, 512], BF16)
                mtmp = _sb(nc, ph3, "mtmp", [128, 512], F32)
                mf = _sb(nc, ph3, "mf", [128, 4, 512], BF16)
                mi = _sb(nc, ph3, "mi", [128, 4, 512], BF16)
                outpre = _sb(nc, ph3, "outpre", [128, DC, 512], BF16)
                yo = _sb(nc, ph3, "yo", [128, D], F32)
                yt2 = _sb(nc, ph3, "yt2", [128, D], F32)
                xo3 = [_sb(nc, ph3, f"xo3{i}", [128, D], F32) for i in range(2)]
                psP = [_psum(nc, ph3, f"psP{i}", [128, 512], F32) for i in range(2)]
                psLog = _psum(nc, ph3, "psLog", [8, 512], F32)
                psWx = _psum(nc, ph3, "psWx", [128, 512], F32)
                psTr3 = _psum(nc, ph3, "psTr3", [128, D], BF16)
                psO2 = _psum(nc, ph3, "psO2", [128, D], F32)
                psOP = _psum(nc, ph3, "psOP", [128, 512], F32)

                WR = {nm: ffn_bf[nm][0].rearrange("(c p) n -> p c n", p=128)
                      for nm in ("WQT", "WKT", "WinvT", "rw1T", "Varr_f", "Warr_f",
                                 "Varr_i", "Warr_i", "UT_f", "UT_i", "WOT", "rw2T")}
                ldt = []
                ldt.append(p.dma("sync", WOb[:, :, :], WR["WOT"], stream="wld"))
                ldt.append(p.dma("sync", R2b[:, :, :], WR["rw2T"], stream="wld"))
                ldt.append(p.dma("sync", bfb[:, :], ffn_bf["b_f"][0], stream="wld"))
                ldt.append(p.dma("sync", bib[:, :], ffn_bf["b_i"][0], stream="wld"))
                ldt.append(p.dma("sync", Eb[:, :], Ecat, stream="wld"))
                ldt.append(p.dma("sync", trb1[:, :], rb1_cols, stream="wld"))
                ldt.append(p.dma("sync", tr2a[:, :], rb2, stream="wld"))
                ldt.append(p.dma("sync", tr2b_[:, :], eb, stream="wld"))
                ldt.append(p.dma("sync", alc[:, :], alpha_in, stream="wld"))
                ldt.append(p.dma("sync", tonesr[:, :], onesrow, stream="wld"))
                ldt.append(p.dma("sync", to512[:, :], ones512, stream="wld"))
                p.wait("vector", *ldt)
                p.op("vector", lambda h: h.memset(ones512b[:, :], 1.0))
                p.op("vector", lambda h: h.tensor_add(tr2eb[:, :], tr2a[:, :], tr2b_[:, :]))
                p.fence("vector")
                p.op("vector", lambda h: h.tensor_copy(tr2ebb[:, :], tr2eb[:, :]))
                p.op("vector", lambda h: h.memset(carryf[:, :], 0.0))
                tsetup = p.op("vector", lambda h: h.memset(carryi[:, :], 0.0))
                p.fence("vector")
                ta2 = p.op("vector", lambda h: h.tensor_scalar_mul(
                    bib[:, :], bib[:, :], alc[0:8, :]))
                p.fence("vector")

                t_ut_rd = [None, None]
                t_x3 = {}
                t_x3[0] = p.dma("sync", xtb3[0][:, :, :],
                                x1h[0:512, :].rearrange("(a p) d -> p a d", p=128),
                                stream="x30")
                t_ps = [None, None]
                t_psOP = None
                t_psO2 = None
                t_xo = [None, None]
                t_hfm_rd = None
                t_q_rd = None
                t_lgev = [None]
                t_u_mm = [None]
                t_m_done = [None, None]
                t_htr3g = [None]
                t_thr = [None, None]
                t_tle = [None, None]
                t_twT = [None, None]
                t_wx_ev = [None]
                t_tr3_ev = [None]

                t_psl_rd = [None, None]
                slice_ctr = [0]

                def slice_load(wname, bi, kch):
                    j = slice_ctr[0] % 2
                    slice_ctr[0] += 1
                    p.wait("sync", t_psl_rd[j])
                    tok = p.dma("sync", psl[j][:, 0:kch, :],
                                WR[wname][:, 0:kch, bi * 256:(bi + 1) * 256],
                                stream=f"pw{j}")
                    return j, tok

                def stream_mm(wname, n_out, kch, src_ap_fn, evac_fn, toks_wait):
                    nb = (n_out + 1) // 2
                    pend = [slice_load(wname, bi, kch) for bi in range(min(2, nb))]
                    for oi in range(n_out):
                        bi, sub = oi // 2, oi % 2
                        j, tld = pend[bi % 2]
                        slot = oi % 2
                        p.wait("tensor", t_ps[slot], tld, *toks_wait)
                        for dc in range(kch):
                            p.op("tensor", lambda h, slot=slot, j=j, dc=dc, sub=sub, kch=kch:
                                 h.matmul(psP[slot][:, :],
                                          psl[j][:, dc, sub * 128:(sub + 1) * 128],
                                          src_ap_fn(dc), start=(dc == 0), stop=(dc == kch - 1)))
                        tmm = p.last("tensor")
                        if sub == 1 or oi == n_out - 1:
                            t_psl_rd[j] = tmm
                            if bi + 2 < nb:
                                pend[bi % 2] = slice_load(wname, bi + 2, kch)
                        t_ps[slot] = evac_fn(oi, psP[slot], tmm)

                def proj(dst, wname, src_ap_fn, toks_wait, name):
                    def ev(oi, ps, tmm):
                        p.wait("vector", tmm)
                        return p.op("vector", lambda h, oi=oi, ps=ps: h.tensor_copy(
                            dst[:, oi, :], ps[:, :]))
                    stream_mm(wname, DC, DC, src_ap_fn, ev, toks_wait)
                    return t_ps[(DC - 1) % 2]

                for tc in range(TC):
                    p.wait("scalar", t_x3[tc])
                    p.wait("vector", t_x3[tc])
                    ttrs = []
                    for st_ in range(4):
                        xa = xtb3[tc % 2][:, st_, :]
                        negm, rstd, tln = ln_stats(p, s, xa, st_ % 2)
                        p.wait("vector", t_htr3g[0])
                        tok = p.op("vector", lambda h, xa=xa, negm=negm, rstd=rstd:
                                   h.tensor_scalar(out=htm3[:, :], in0=xa,
                                                   scalar1=negm[:, :], scalar2=rstd[:, :],
                                                   op0=ALU.add, op1=ALU.mult))
                        p.fence("vector")
                        p.wait("tensor", tok, t_hfm_rd)
                        if ttrs:
                            p.wait("tensor", ttrs[-1])
                        for dc in range(DC):
                            p.op("tensor", lambda h, dc=dc: h.transpose(
                                psTr3[:, dc * 128:(dc + 1) * 128],
                                htm3[:, dc * 128:(dc + 1) * 128], tident[:, :]))
                        t_htr3g[0] = p.last("tensor")
                        ttr = p.last("tensor")
                        p.wait("vector", ttr)
                        tev = p.op("vector", lambda h, st_=st_: h.tensor_copy(
                            hfm3[:, :, st_ * 128:(st_ + 1) * 128],
                            psTr3[:, :].rearrange("p (c t) -> p c t", c=DC)))
                        ttrs.append(tev)
                    t_h3 = ttrs[-1]
                    if tc + 1 < TC:
                        p.wait("sync", t_h3)
                        t_x3[tc + 1] = p.dma(
                            "sync", xtb3[(tc + 1) % 2][:, :, :],
                            x1h[(tc + 1) * 512:(tc + 2) * 512, :]
                            .rearrange("(a p) d -> p a d", p=128),
                            stream=f"x3{(tc + 1) % 2}")

                    tQ = proj(Qfm, "WQT", lambda dc: hfm3[:, dc, :], [t_h3], "Q")
                    tK = proj(Kfm3, "WKT", lambda dc: hfm3[:, dc, :], [t_h3], "K")
                    tiQ = proj(iQfm, "WinvT", lambda dc: Qfm[:, dc, :], [tQ], "iQ")
                    tiK = proj(iKfm, "WinvT", lambda dc: Kfm3[:, dc, :], [tK], "iK")
                    t_hfm_rd = p.last("tensor")

                    BRS = (
                        ("Varr_f", "Warr_f", Qfm, Kfm3, tQ, tK, cumf, carryf, xVf, wgtTf, mf),
                        ("Varr_i", "Warr_i", iQfm, iKfm, tiQ, tiK, cumi, carryi, xVi, wgtTi, mi),
                    )

                    def br_xv(br):
                        (Vnm, Wnm, Qs, Ks, tQs, tKs, cum, carry, xVb, wgtT, mb) = BRS[br]
                        def ev_xv(oi, ps, tmm, xVb=xVb):
                            p.wait("vector", tmm)
                            return p.op("vector", lambda h, oi=oi, ps=ps: h.tensor_copy(
                                xVb[:, oi, :], ps[:, :]))
                        stream_mm(Vnm, 4, DC, lambda dc, Qs=Qs: Qs[:, dc, :], ev_xv, [tQs])

                    def br_yw(br):
                        (Vnm, Wnm, Qs, Ks, tQs, tKs, cum, carry, xVb, wgtT, mb) = BRS[br]
                        def ev_yw(oi, ps, tmm, cum=cum, carry=carry):
                            p.wait("vector", tmm)
                            tcp = p.op("vector", lambda h, oi=oi, ps=ps, cum=cum, carry=carry:
                                       h.tensor_tensor_scan(
                                           out=cum[:, oi, :], data0=to512[:, :], data1=ps[:, :],
                                           initial=carry[:, oi:oi + 1], op0=ALU.mult, op1=ALU.add))
                            p.fence("vector")
                            p.op("vector", lambda h, oi=oi, cum=cum, carry=carry: h.tensor_copy(
                                carry[:, oi:oi + 1], cum[:, oi, 511:512]))
                            p.fence("vector")
                            return tcp
                        stream_mm(Wnm, 4, DC, lambda dc, Ks=Ks: Ks[:, dc, :], ev_yw, [tKs])

                    def br_router(br):
                        (Vnm, Wnm, Qs, Ks, tQs, tKs, cum, carry, xVb, wgtT, mb) = BRS[br]
                        hrb = hrfm[br]
                        def ev_hr(oi, ps, tmm, hrb=hrb):
                            p.wait("scalar", tmm)
                            return p.op("scalar", lambda h, oi=oi, ps=ps, hrb=hrb: h.activation(
                                out=hrb[:, oi, :], in_=ps[:, :], func=AF.Gelu,
                                bias=trb1[:, oi:oi + 1]))
                        stream_mm("rw1T", DC, DC, lambda dc, Qs=Qs: Qs[:, dc, :], ev_hr, [tQs])
                        t_thr[br] = t_ps[(DC - 1) % 2]

                    def br_logits(br):
                        hrb = hrfm[br]
                        p.wait("tensor", t_thr[br], t_lgev[0])
                        for rhc in range(DC):
                            p.op("tensor", lambda h, rhc=rhc, hrb=hrb: h.matmul(
                                psLog[:, :], R2b[:, rhc, :], hrb[:, rhc, :],
                                start=(rhc == 0), stop=False))
                        p.wait("tensor", ta2)
                        p.op("tensor", lambda h: h.matmul(
                            psLog[:, :], tr2ebb[:, :],
                            ones512b[:, :], start=False, stop=True))
                        tlg = p.last("tensor")
                        p.wait("vector", tlg)
                        tle = p.op("vector", lambda h, br=br: h.tensor_copy(
                            lgfm[br][:, :], psLog[:, :]))
                        t_lgev[0] = tle
                        t_tle[br] = tle

                    def br_softmax(br):
                        (Vnm, Wnm, Qs, Ks, tQs, tKs, cum, carry, xVb, wgtT, mb) = BRS[br]
                        p.wait("tensor", t_tle[br], t_wx_ev[0])
                        for j in range(4):
                            p.op("tensor", lambda h, j=j, br=br: h.transpose(
                                psWx[:, j * 8:(j + 1) * 8], lgfm[br][:, j * 128:(j + 1) * 128],
                                tidentf[0:8, 0:8]))
                        tlt = p.last("tensor")
                        p.wait("vector", tlt)
                        twx = p.op("vector", lambda h, br=br: h.tensor_copy(
                            lgt[br][:, :, :], psWx[:, 0:32].rearrange("p (a b) -> p a b", a=4)))
                        t_wx_ev[0] = twx
                        p.fence("vector")
                        p.op("vector", lambda h, br=br: h.reduce_max(
                            lmx[br][:, :], lgt[br][:, :, :], axis=AX.X, negate=True))
                        p.fence("vector")
                        p.op("vector", lambda h, br=br: h.tensor_tensor(
                            lgt[br][:, :, :], lgt[br][:, :, :],
                            lmx[br][:, :].unsqueeze(2).broadcast_to([128, 4, 8]), op=ALU.add))
                        p.fence("vector")
                        tsm = p.op("vector", lambda h: h.engine_nop())
                        p.wait("scalar", tsm)
                        tex = p.op("scalar", lambda h, br=br: h.activation(
                            out=lgt[br][:, :, :], in_=lgt[br][:, :, :], func=AF.Exp))
                        p.wait("vector", tex)
                        p.op("vector", lambda h, br=br: h.reduce_sum(
                            lsm[br][:, :], lgt[br][:, :, :], axis=AX.X))
                        p.fence("vector")
                        p.op("vector", lambda h, br=br: h.reciprocal(lsm[br][:, :], lsm[br][:, :]))
                        p.fence("vector")
                        p.op("vector", lambda h, br=br: h.tensor_tensor(
                            wgtb[br][:, :].rearrange("p (a b) -> p a b", a=4), lgt[br][:, :, :],
                            lsm[br][:, :].unsqueeze(2).broadcast_to([128, 4, 8]), op=ALU.mult))
                        twg = p.op("vector", lambda h: h.engine_nop())
                        p.fence("vector")
                        p.wait("tensor", twg, t_tr3_ev[0])
                        for j in range(4):
                            p.op("tensor", lambda h, j=j, br=br: h.transpose(
                                psTr3[0:8, j * 128:(j + 1) * 128], wgtb[br][:, j * 8:(j + 1) * 8],
                                tident[:, :]))
                        twt = p.last("tensor")
                        p.wait("vector", twt)
                        p.op("vector", lambda h, wgtT=wgtT: h.tensor_copy(
                            wgtT[:, :], psTr3[0:8, 0:512]))
                        twT = p.op("vector", lambda h: h.engine_nop())
                        t_tr3_ev[0] = twT
                        t_twT[br] = twT
                        p.fence("vector")

                    def br_m(br):
                        (Vnm, Wnm, Qs, Ks, tQs, tKs, cum, carry, xVb, wgtT, mb) = BRS[br]
                        p.wait("vector", t_u_mm[0])
                        for krc in range(4):
                            p.wait("tensor", t_twT[br], t_wx_ev[0])
                            tEx = p.op("tensor", lambda h, krc=krc, wgtT=wgtT: h.matmul(
                                psWx[:, :], Eb[:, krc * 128:(krc + 1) * 128], wgtT[:, :],
                                start=True, stop=True))
                            p.wait("vector", tEx)
                            p.op("vector", lambda h, krc=krc, cum=cum: h.tensor_mul(
                                mtmp[:, :], cum[:, krc, :], psWx[:, :]))
                            p.fence("vector")
                            if br == 1:
                                p.op("vector", lambda h: h.tensor_scalar_mul(
                                    mtmp[:, :], mtmp[:, :], alc[:, :]))
                                p.fence("vector")
                            tm = p.op("vector", lambda h, krc=krc, xVb=xVb, mb=mb: h.tensor_mul(
                                mb[:, krc, :], mtmp[:, :], xVb[:, krc, :]))
                            t_wx_ev[0] = tm
                            t_m_done[br] = tm
                            p.fence("vector")

                    for stage in (br_xv, br_yw, br_router, br_logits, br_softmax, br_m):
                        stage(0)
                        stage(1)
                    p.wait("tensor", t_m_done[0], t_m_done[1])
                    t_ut_ld = {}

                    def ut_load(pair):
                        j = pair % 2
                        p.wait("sync", t_ut_rd[j])
                        ta = p.dma("sync", UTfb[j][:, :, :],
                                   WR["UT_f"][:, 0:4, pair * 256:(pair + 1) * 256],
                                   stream=f"utf{j}")
                        tb = p.dma("sync", UTib[j][:, :, :],
                                   WR["UT_i"][:, 0:4, pair * 256:(pair + 1) * 256],
                                   stream=f"uti{j}")
                        t_ut_ld[pair] = (ta, tb)

                    ut_load(0)
                    ut_load(1)
                    for dc in range(DC):
                        pair, sub = dc // 2, dc % 2
                        j = pair % 2
                        p.wait("tensor", t_psOP, *t_ut_ld[pair])
                        first = True
                        for UTa, mb in ((UTfb[j], mf), (UTib[j], mi)):
                            for krc in range(4):
                                p.op("tensor", lambda h, sub=sub, krc=krc, UTa=UTa, mb=mb, first=first:
                                     h.matmul(psOP[:, :],
                                              UTa[:, krc, sub * 128:(sub + 1) * 128],
                                              mb[:, krc, :], start=first, stop=False))
                                first = False
                        p.op("tensor", lambda h, dc=dc: h.matmul(
                            psOP[:, :], bfb[:, dc * 128:(dc + 1) * 128], wgtTf[:, :],
                            start=False, stop=False))
                        p.op("tensor", lambda h, dc=dc: h.matmul(
                            psOP[:, :], bib[:, dc * 128:(dc + 1) * 128], wgtTi[:, :],
                            start=False, stop=True))
                        tmm = p.last("tensor")
                        t_u_mm[0] = tmm
                        if sub == 1:
                            t_ut_rd[j] = tmm
                            if pair + 2 < 4:
                                ut_load(pair + 2)
                        p.wait("vector", tmm)
                        t_psOP = p.op("vector", lambda h, dc=dc: h.tensor_copy(
                            outpre[:, dc, :], psOP[:, :]))
                    t_op = t_psOP

                    for st_ in range(4):
                        p.wait("tensor", t_op, t_psO2)
                        for dc in range(DC):
                            p.op("tensor", lambda h, st_=st_, dc=dc: h.matmul(
                                psO2[:, 0:512], outpre[:, dc, st_ * 128:(st_ + 1) * 128],
                                WOb[:, dc, 0:512], start=(dc == 0), stop=(dc == DC - 1)))
                            p.op("tensor", lambda h, st_=st_, dc=dc: h.matmul(
                                psO2[:, 512:1024], outpre[:, dc, st_ * 128:(st_ + 1) * 128],
                                WOb[:, dc, 512:1024], start=(dc == 0), stop=(dc == DC - 1)))
                        tmm = p.last("tensor")
                        p.wait("vector", tmm)
                        ty2 = p.op("vector", lambda h: h.tensor_copy(yo[:, :], psO2[:, :]))
                        t_psO2 = ty2
                        p.fence("vector")
                        p.wait("scalar", ty2)
                        p.wait("vector", t_rn)
                        negm, rstd, tln = ln_stats(p, s, yo[:, :], 1,
                                                   rl=trn[:, tc * 4 + st_:tc * 4 + st_ + 1])
                        p.op("vector", lambda h, negm=negm, rstd=rstd: h.tensor_scalar(
                            out=yt2[:, :], in0=yo[:, :], scalar1=negm[:, :],
                            scalar2=rstd[:, :], op0=ALU.add, op1=ALU.mult))
                        p.fence("vector")
                        p.wait("vector", t_xo[st_ % 2])
                        tres = p.op("vector", lambda h, tc=tc, st_=st_: h.tensor_add(
                            xo3[st_ % 2][:, :], xtb3[tc % 2][:, st_, :], yt2[:, :]))
                        p.fence("vector")
                        p.wait("sync", tres)
                        t_xo[st_ % 2] = p.dma(
                            "sync", x2_out[tc * 512 + st_ * 128: tc * 512 + (st_ + 1) * 128, :],
                            xo3[st_ % 2][:, :], stream=f"x2o{st_ % 2}")
                    t_q_rd = t_hfm_rd
            p.barrier()
        else:
            tcp = p.dma("sync", x2_out, x1h, stream="fin")
            p.wait("sync", tcp)

        p.barrier()
        p.emit()
    print(f"launch1 built: ops={ {k: len(v) for k, v in p.items.items()} }")
    return nc


def build_launch2(R=4096, do_ffn=True, dbg_x3=False):
    G = R // 128
    TC = R // 512
    nc = bass.Bass("TRN2", target_bir_lowering=False)

    def din(name, shape, dt=F32):
        return nc.dram_tensor(name, shape, dt, kind="ExternalInput").ap()

    def dout(name, shape, dt=F32):
        return nc.dram_tensor(name, shape, dt, kind="ExternalOutput").ap()

    def dint(name, shape, dt):
        return nc.dram_tensor(name, shape, dt, kind="Internal").ap()

    x_in = din("x", [R, D])
    Wqkv2 = din("Wqkv2", [D, 3 * D])
    bqkv2 = din("bqkv2", [1, 3 * D])
    Wwr = din("Wwr", [D, 32])
    bwr = din("bwr", [1, 32])
    WoT = din("WoT", [D, D])
    bo = din("bo", [1, D])
    bdmask = din("bdmask", [128, 128], BF16)
    identb = din("identb", [128, 128], BF16)
    identf = din("identf", [128, 128], F32)
    onesrow = din("onesrow", [1, 128], BF16)
    w1n = din("w1n", [4 * D, D])
    w1T = din("w1T", [D, 4 * D])
    w2n = din("w2n", [D, 4 * D])
    w2T = din("w2T", [4 * D, D])
    b1_cols = din("b1_cols", [128, 32])
    b2_cols = din("b2_cols", [128, 8])
    y_out = dout("y", [R, D])
    x3_out = dout("x3dbg", [R, D]) if dbg_x3 else None

    Wqkv2_bf = dint("Wqkv2_bf", [D, 3 * D], BF16)
    Wwr_bf = dint("Wwr_bf", [D, 32], BF16)
    WoT_bf = dint("WoT_bf", [D, D], BF16)
    w1T_bf = dint("w1T_bf", [D, 4 * D], BF16)
    w2T_bf = dint("w2T_bf", [4 * D, D], BF16)
    sign1_sp = dint("sign1_sp", [D, 4 * D], BF16)
    bw2_sp = dint("bw2_sp", [4 * D, D], BF16)
    x3h = dint("x3h", [R, D], F32)

    stack = ExitStack()
    with stack:
        p = Prog(nc, stack)
        tident = _sb(nc, stack, "tident", [128, 128], BF16)
        tidentf = _sb(nc, stack, "tidentf", [128, 128], F32)
        tmaskbd = _sb(nc, stack, "tmaskbd", [128, 128], BF16)
        tonesr = _sb(nc, stack, "tonesr", [1, 128], BF16)
        a1c = _sb(nc, stack, "a1c", [128, 32], F32)
        s = {}
        alloc_ln_scratch(nc, stack, s)
        p.op("vector", lambda h: h.memset(s["cInvD"][:, :], 1.0 / D))
        t_lnc = p.op("vector", lambda h: h.memset(s["cEps"][:, :], EPS))
        p.fence("vector")
        s[("g", 0)] = t_lnc
        s[("g", 1)] = t_lnc

        p.scope = "l2p0"
        t_ic = []
        t_ic.append(p.dma("sync", tident[:, :], identb, stream="ic0"))
        t_ic.append(p.dma("sync", tidentf[:, :], identf, stream="ic1"))
        t_ic.append(p.dma("sync", tmaskbd[:, :], bdmask, stream="ic2"))
        t_ic.append(p.dma("sync", tonesr[:, :], onesrow, stream="ic3"))
        t_qkv = p.dma("gpsimd", Wqkv2_bf, Wqkv2, stream="gpq")
        t_wr = p.dma("gpsimd", Wwr_bf, Wwr, stream="gpq")
        t_wo = p.dma("gpsimd", WoT_bf, WoT, stream="gpq")
        t_w1c = p.dma("gpsimd", w1T_bf, w1T, stream="gpw")
        t_w2c = p.dma("gpsimd", w2T_bf, w2T, stream="gpw")

        p.scope = "l2pa"
        with ExitStack() as pha:
            Wq2 = _sb(nc, pha, "Wq2", [128, DC, 3 * D], BF16)
            Wwr2 = _sb(nc, pha, "Wwr2", [128, DC, 32], BF16)
            Wo2 = _sb(nc, pha, "Wo2", [128, DC, D], BF16)
            bq2 = _sb(nc, pha, "bq2", [1, 3 * D], BF16)
            bwr2 = _sb(nc, pha, "bwr2", [1, 32], BF16)
            bo2 = _sb(nc, pha, "bo2", [1, D], BF16)
            ones512b2 = _sb(nc, pha, "ones512b2", [1, 512], BF16)
            xg = [_sb(nc, pha, f"xg{i}", [128, D], F32) for i in range(3)]
            htm2 = [_sb(nc, pha, f"htm2{i}", [128, D], BF16) for i in range(2)]
            hfmg = _sb(nc, pha, "hfmg", [128, DC, 128], BF16)
            qtm = _sb(nc, pha, "qtm", [128, D], F32)
            ktm = _sb(nc, pha, "ktm", [128, D], F32)
            vtm = _sb(nc, pha, "vtm", [128, D], F32)
            wrtm = _sb(nc, pha, "wrtm", [128, 32], F32)
            qb16 = [_sb(nc, pha, f"qb16{i}", [128, D], BF16) for i in range(2)]
            kb16 = [_sb(nc, pha, f"kb16{i}", [128, D], BF16) for i in range(2)]
            qfmg = _sb(nc, pha, "qfmg", [128, DC, 128], BF16)
            kzg = _sb(nc, pha, "kzg", [128, 16, 128], BF16)
            vw = [_sb(nc, pha, f"vw{i}", [128, D], BF16) for i in range(2)]
            Gsb16 = _sb(nc, pha, "Gsb16", [128, 2048], BF16)
            sqn = _sb(nc, pha, "sqn", [128, D], F32)
            ssq = _sb(nc, pha, "ssq", [128, 16], F32)
            rq = _sb(nc, pha, "rq", [128, 16], F32)
            rk = _sb(nc, pha, "rk", [128, 16], F32)
            wsig = _sb(nc, pha, "wsig", [128, 16], F32)
            rsig = _sb(nc, pha, "rsig", [128, 16], F32)
            oscale = [_sb(nc, pha, f"oscale{i}", [128, 16], F32) for i in range(2)]
            ob16 = _sb(nc, pha, "ob16", [128, D], BF16)
            ofmg = _sb(nc, pha, "ofmg", [128, DC, 128], BF16)
            yo2 = _sb(nc, pha, "yo2", [128, D], F32)
            yt3 = _sb(nc, pha, "yt3", [128, D], F32)
            xo2 = [_sb(nc, pha, f"xo2{i}", [128, D], F32) for i in range(2)]
            psA = _psum(nc, pha, "psA", [128, 2048], F32)
            psO1 = _psum(nc, pha, "psO1", [128, D], F32)
            psTrX = _psum(nc, pha, "psTrX", [128, D], BF16)

            t_w = []
            p.wait("sync", t_qkv, t_wr, t_wo)
            t_w.append(p.dma("sync", Wq2[:, :, :],
                             Wqkv2_bf.rearrange("(c p) n -> p c n", p=128), stream="wl2"))
            t_w.append(p.dma("sync", Wwr2[:, :, :],
                             Wwr_bf.rearrange("(c p) n -> p c n", p=128), stream="wl2"))
            t_w.append(p.dma("sync", Wo2[:, :, :],
                             WoT_bf.rearrange("(c p) n -> p c n", p=128), stream="wl2"))
            t_w.append(p.dma("gpsimd", bq2[:, :], bqkv2, stream="wl3"))
            t_w.append(p.dma("gpsimd", bwr2[:, :], bwr, stream="wl3"))
            t_w.append(p.dma("gpsimd", bo2[:, :], bo, stream="wl3"))
            p.wait("vector", *t_w)
            p.op("vector", lambda h: h.memset(ones512b2[:, :], 1.0))
            t_setup = p.last("vector")
            p.fence("vector")
            p.wait("tensor", *t_w)

            wrow = [_sb(nc, pha, "wrow0", [128, 4 * D], F32)] * 2
            a2c = _sb(nc, pha, "a2c", [128, 8], F32)
            a2row = _sb(nc, pha, "a2row", [1, D], F32)
            a2rowb = _sb(nc, pha, "a2rowb", [1, D], BF16)
            a2bcb = _sb(nc, pha, "a2bcb", [128, D], BF16)
            sgbuf = [_sb(nc, pha, f"sgbuf{i}", [128, D], BF16) for i in range(2)]
            sobuf = [_sb(nc, pha, f"sobuf{i}", [128, D], BF16) for i in range(2)]
            psR = _psum(nc, pha, "psR", [128, 512], F32)
            prep_state = {"t_a": [None, None], "t_sg": [None, None], "t_so": [None, None],
                          "ta2": None, "tbc": None}

            def prep_alpha1(i0_):
                for i in range(i0_, i0_ + 4):
                    j = 0
                    p.wait("sync", *prep_state["t_a"])
                    tw = p.dma("sync", wrow[j][:, 0:1024],
                               w1n[i * 128:(i + 1) * 128, :], stream=f"wr{j}")
                    p.wait("vector", tw)
                    prep_state["t_a"][0] = p.op("vector", lambda h, i=i, j=j: h.reduce_sum(
                        a1c[:, i:i + 1], wrow[j][:, 0:1024], axis=AX.X,
                        apply_absolute_value=True))
                    p.fence("vector")

            def prep_alpha2(i0_):
                for i in range(i0_, i0_ + 4):
                    j = 0
                    p.wait("sync", *prep_state["t_a"])
                    tw = p.dma("sync", wrow[j][:, :], w2n[i * 128:(i + 1) * 128, :],
                               stream=f"wr{j}")
                    p.wait("vector", tw)
                    prep_state["t_a"][0] = p.op("vector", lambda h, i=i, j=j: h.reduce_sum(
                        a2c[:, i:i + 1], wrow[j][:, :], axis=AX.X,
                        apply_absolute_value=True))
                    p.fence("vector")

            def prep_fin():
                p.op("vector", lambda h: h.tensor_scalar_mul(a1c[:, :], a1c[:, :], 1.0 / D))
                ta2_ = p.op("vector", lambda h: h.tensor_scalar_mul(
                    a2c[:, :], a2c[:, :], 1.0 / (4 * D)))
                p.fence("vector")
                p.wait("tensor", ta2_)
                tprev = None
                for dc in range(DC):
                    p.wait("tensor", tprev)
                    p.op("tensor", lambda h, dc=dc: h.transpose(
                        psR[0:1, 0:128], a2c[:, dc:dc + 1], tidentf[:, :]))
                    ttr = p.last("tensor")
                    p.wait("vector", ttr)
                    tprev = p.op("vector", lambda h, dc=dc: h.tensor_copy(
                        a2row[:, dc * 128:(dc + 1) * 128], psR[0:1, 0:128]))
                p.fence("vector")
                tcpy = p.op("vector", lambda h: h.tensor_copy(a2rowb[:, :], a2row[:, :]))
                p.fence("vector")
                p.wait("tensor", tcpy)
                for hh in range(2):
                    p.op("tensor", lambda h, hh=hh: h.matmul(
                        psR[:, :], tonesr[:, :], a2rowb[:, hh * 512:(hh + 1) * 512],
                        start=True, stop=True))
                    tb_ = p.last("tensor")
                    p.wait("vector", tb_)
                    tbe = p.op("vector", lambda h, hh=hh: h.tensor_copy(
                        a2bcb[:, hh * 512:(hh + 1) * 512], psR[:, :]))
                    p.wait("tensor", tbe)
                p.fence("vector")

            def prep_sign1(it0):
                p.wait("sync", t_w1c, t_w2c)
                for it in range(it0, it0 + 4):
                    i, hh = it // 4, it % 4
                    j = it % 2
                    p.wait("sync", prep_state["t_sg"][j], prep_state["t_so"][j])
                    tw = p.dma("sync", sgbuf[j][:, :],
                               w1T_bf[i * 128:(i + 1) * 128, hh * 1024:(hh + 1) * 1024],
                               stream=f"sg{j}")
                    p.wait("scalar", tw)
                    tsg = p.op("scalar", lambda h, j=j: h.activation(
                        out=sobuf[j][:, :], in_=sgbuf[j][:, :], func=AF.Sign))
                    prep_state["t_sg"][j] = tsg
                    p.fence("scalar")
                    prep_state["t_so"][j] = p.dma(
                        "scalar",
                        sign1_sp[i * 128:(i + 1) * 128, hh * 1024:(hh + 1) * 1024],
                        sobuf[j][:, :], stream=f"so{j}")

            def prep_bw2(i0_):
                p.wait("sync", t_w2c)
                for i in range(i0_, i0_ + 4):
                    j = i % 2
                    p.wait("sync", prep_state["t_sg"][j], prep_state["t_so"][j])
                    tw = p.dma("sync", sgbuf[j][:, :],
                               w2T_bf[i * 128:(i + 1) * 128, :], stream=f"sg{j}")
                    p.wait("scalar", tw)
                    tsg = p.op("scalar", lambda h, j=j: h.activation(
                        out=sobuf[j][:, :], in_=sgbuf[j][:, :], func=AF.Sign))
                    p.wait("vector", tsg)
                    tmu = p.op("vector", lambda h, j=j: h.tensor_mul(
                        sobuf[j][:, :], sobuf[j][:, :], a2bcb[:, :]))
                    prep_state["t_sg"][j] = tmu
                    p.fence("vector")
                    p.wait("scalar", tmu)
                    prep_state["t_so"][j] = p.dma(
                        "scalar", bw2_sp[i * 128:(i + 1) * 128, :],
                        sobuf[j][:, :], stream=f"so{j}")

            prep_chunks = (
                [lambda i=i: prep_alpha1(i * 4) for i in range(8)]
                + [lambda i=i: prep_alpha2(i * 4) for i in range(2)]
                + [prep_fin]
                + [lambda i=i: prep_sign1(i * 4) for i in range(8)]
                + [lambda i=i: prep_bw2(i * 4) for i in range(8)]
            )

            t_xld = {i: p.dma("sync", xg[i][:, :], x_in[i * 128:(i + 1) * 128, :],
                              stream=f"xg{i}") for i in range(min(3, G))}
            st_ = {
                "t_htr2": None, "t_hread2": None, "t_psA_ev": None,
                "t_psO1_rd": None, "t_xo2": [None, None], "t_otr": None,
                "tok_ln": {}, "t_wave_ev": {}, "t_bpre": None, "t_qkrd": None,
                "t_kzev": {}, "t_oscale": {},
            }

            def emit_A_ln(g):
                p.wait("scalar", t_xld[g])
                p.wait("vector", t_xld[g])
                xa = xg[g % 3][:, :]
                negm, rstd, tln = ln_stats(p, s, xa, g % 2)
                p.wait("vector", st_["t_htr2"])
                tok = p.op("vector", lambda h, xa=xa, negm=negm, rstd=rstd:
                           h.tensor_scalar(out=htm2[g % 2][:, :], in0=xa,
                                           scalar1=negm[:, :], scalar2=rstd[:, :],
                                           op0=ALU.add, op1=ALU.mult))
                p.fence("vector")
                st_["tok_ln"][g] = tok

            def emit_A_rest(g):
                tok = st_["tok_ln"][g]
                p.wait("tensor", tok, *t_ic)
                p.wait("tensor", st_["t_hread2"])
                for dc in range(DC):
                    p.op("tensor", lambda h, dc=dc, g=g: h.transpose(
                        psTrX[:, dc * 128:(dc + 1) * 128],
                        htm2[g % 2][:, dc * 128:(dc + 1) * 128], tident[:, :]))
                ttr = p.last("tensor")
                st_["t_htr2"] = ttr
                p.wait("vector", ttr)
                thev = p.op("vector", lambda h: h.tensor_copy(
                    hfmg[:, :, :], psTrX[:, :].rearrange("p (c t) -> p c t", c=DC)))
                p.fence("vector")

                p.wait("tensor", thev, t_setup)
                for wave in range(2):
                    p.wait("tensor", st_["t_psA_ev"])
                    for dc in range(DC):
                        for j in range(4):
                            col = wave * 2048 + j * 512
                            if col < 3 * D:
                                p.op("tensor", lambda h, dc=dc, j=j, col=col: h.matmul(
                                    psA[:, j * 512:(j + 1) * 512],
                                    hfmg[:, dc, :], Wq2[:, dc, col:col + 512],
                                    start=(dc == 0), stop=False))
                            elif col == 3 * D:
                                p.op("tensor", lambda h, dc=dc, j=j: h.matmul(
                                    psA[:, j * 512:j * 512 + 32],
                                    hfmg[:, dc, :], Wwr2[:, dc, :],
                                    start=(dc == 0), stop=False))
                    for j in range(4):
                        col = wave * 2048 + j * 512
                        if col < 3 * D:
                            p.op("tensor", lambda h, j=j, col=col: h.matmul(
                                psA[:, j * 512:(j + 1) * 512],
                                tonesr[:, :], bq2[:, col:col + 512],
                                start=False, stop=True))
                        elif col == 3 * D:
                            p.op("tensor", lambda h, j=j: h.matmul(
                                psA[:, j * 512:j * 512 + 32],
                                tonesr[:, :], bwr2[:, :],
                                start=False, stop=True))
                    tmmA = p.last("tensor")
                    if wave == 1:
                        st_["t_hread2"] = tmmA
                    p.wait("scalar", tmmA, st_["t_bpre"])
                    if wave == 0:
                        p.op("scalar", lambda h: h.activation(
                            out=qtm[:, :], in_=psA[:, 0:1024], func=AF.Identity))
                        st_["t_psA_ev"] = p.op("scalar", lambda h: h.activation(
                            out=ktm[:, :], in_=psA[:, 1024:2048], func=AF.Identity))
                    else:
                        p.op("scalar", lambda h: h.activation(
                            out=vtm[:, :], in_=psA[:, 0:1024], func=AF.Identity))
                        st_["t_psA_ev"] = p.op("scalar", lambda h: h.activation(
                            out=wrtm[:, :], in_=psA[:, 1024:1056], func=AF.Identity))
                    p.fence("scalar")
                st_["t_wave_ev"][g] = st_["t_psA_ev"]

            def emit_B_pre(g):
                p.wait("scalar", st_["t_wave_ev"][g])
                for src_t, dst in ((qtm, rq), (ktm, rk)):
                    tsq = p.op("scalar", lambda h, src_t=src_t: h.activation(
                        out=sqn[:, :], in_=src_t[:, :], func=AF.Square))
                    p.wait("vector", tsq)
                    p.op("vector", lambda h: h.reduce_sum(
                        ssq[:, :], sqn[:, :].rearrange("p (n e) -> p n e", e=64),
                        axis=AX.X))
                    p.fence("vector")
                    tss = p.last("vector")
                    p.wait("scalar", tss)
                    tsq2 = p.op("scalar", lambda h: h.activation(
                        out=ssq[:, :], in_=ssq[:, :], func=AF.Sqrt))
                    p.wait("vector", tsq2)
                    p.op("vector", lambda h: h.tensor_scalar_max(ssq[:, :], ssq[:, :], 1e-12))
                    p.fence("vector")
                    p.op("vector", lambda h, dst=dst: h.reciprocal(dst[:, :], ssq[:, :]))
                    p.fence("vector")
                twr = p.op("scalar", lambda h: h.activation(
                    out=wsig[:, :], in_=wrtm[:, 0:16], func=AF.Sigmoid))
                twr2 = p.op("scalar", lambda h: h.activation(
                    out=rsig[:, :], in_=wrtm[:, 16:32], func=AF.Sigmoid))
                st_["t_qkrd"] = twr2
                p.wait("vector", twr, twr2, st_["t_oscale"].get(g - 2))
                p.op("vector", lambda h, g=g: h.tensor_tensor(
                    vw[g % 2][:, :].rearrange("p (n e) -> p n e", e=64),
                    vtm[:, :].rearrange("p (n e) -> p n e", e=64),
                    wsig[:, :].unsqueeze(2).broadcast_to([128, 16, 64]), op=ALU.mult))
                p.fence("vector")
                p.op("vector", lambda h, g=g: h.tensor_mul(
                    oscale[g % 2][:, :], rq[:, :], rsig[:, :]))
                p.fence("vector")
                p.op("vector", lambda h, g=g: h.tensor_copy(qb16[g % 2][:, :], qtm[:, :]))
                tqb = p.op("vector", lambda h, g=g: h.tensor_tensor(
                    kb16[g % 2][:, :].rearrange("p (n e) -> p n e", e=64),
                    ktm[:, :].rearrange("p (n e) -> p n e", e=64),
                    rk[:, :].unsqueeze(2).broadcast_to([128, 16, 64]), op=ALU.mult))
                p.fence("vector")
                st_["t_bpre"] = tqb
                st_["t_kzev"][g] = tqb

            def emit_C(g):
                tqb = st_["t_kzev"][g]
                p.wait("tensor", tqb)
                for dc in range(DC):
                    p.op("tensor", lambda h, dc=dc, g=g: h.transpose(
                        psTrX[:, dc * 128:(dc + 1) * 128],
                        qb16[g % 2][:, dc * 128:(dc + 1) * 128], tident[:, :]))
                ttq = p.last("tensor")
                st_["t_htr2"] = ttq
                p.wait("vector", ttq)
                tevq = p.op("vector", lambda h: h.tensor_copy(
                    qfmg[:, :, :], psTrX[:, :].rearrange("p (c t) -> p c t", c=DC)))
                p.wait("tensor", tevq)
                for dc in range(DC):
                    p.op("tensor", lambda h, dc=dc, g=g: h.transpose(
                        psTrX[:, dc * 128:(dc + 1) * 128],
                        kb16[g % 2][:, dc * 128:(dc + 1) * 128], tident[:, :]))
                ttk = p.last("tensor")
                st_["t_htr2"] = ttk
                p.wait("vector", ttk)
                if g == 0:
                    p.op("vector", lambda h: h.memset(kzg[:, :, :], 0.0))
                    p.fence("vector")
                p.op("vector", lambda h: h.tensor_copy(
                    kzg[0:64, :, :][0:64, 0::2, :],
                    psTrX[0:64, :].rearrange("p (c t) -> p c t", c=DC)))
                p.fence("vector")
                tevk = p.op("vector", lambda h: h.tensor_copy(
                    kzg[64:128, :, :][0:64, 1::2, :],
                    psTrX[64:128, :].rearrange("p (c t) -> p c t", c=DC)))
                p.fence("vector")

                p.wait("tensor", tevk, st_["t_psO1_rd"], st_["t_psA_ev"])
                for n in range(16):
                    c = n // 2
                    p.op("tensor", lambda h, n=n, c=c: h.matmul(
                        psA[:, n * 128:(n + 1) * 128],
                        kzg[:, n, :], qfmg[:, c, :],
                        start=True, stop=True))
                tg = p.last("tensor")
                p.wait("vector", tg)
                tge = p.op("vector", lambda h: h.tensor_tensor(
                    Gsb16[:, :].rearrange("p (n t) -> p n t", n=16),
                    psA[:, :].rearrange("p (n t) -> p n t", n=16),
                    tmaskbd[:, :].unsqueeze(1).broadcast_to([128, 16, 128]),
                    op=ALU.mult))
                st_["t_psA_ev"] = tge
                p.fence("vector")
                p.wait("tensor", tge)
                for n in range(16):
                    p.op("tensor", lambda h, n=n, g=g: h.matmul(
                        psO1[:, n * 64:(n + 1) * 64], Gsb16[:, n * 128:(n + 1) * 128],
                        vw[g % 2][:, n * 64:(n + 1) * 64], start=True, stop=True))
                tapp = p.last("tensor")
                p.wait("vector", tapp)
                tob = p.op("vector", lambda h, g=g: h.tensor_tensor(
                    ob16[:, :].rearrange("p (n e) -> p n e", e=64),
                    psO1[:, :].rearrange("p (n e) -> p n e", e=64),
                    oscale[g % 2][:, :].unsqueeze(2).broadcast_to([128, 16, 64]), op=ALU.mult))
                st_["t_oscale"][g] = tob
                st_["t_psO1_rd"] = tob
                p.fence("vector")
                p.wait("tensor", tob, st_["t_otr"])
                for dc in range(DC):
                    p.op("tensor", lambda h, dc=dc: h.transpose(
                        psTrX[:, dc * 128:(dc + 1) * 128],
                        ob16[:, dc * 128:(dc + 1) * 128], tident[:, :]))
                tto = p.last("tensor")
                st_["t_htr2"] = tto
                p.wait("vector", tto)
                tevo = p.op("vector", lambda h: h.tensor_copy(
                    ofmg[:, :, :], psTrX[:, :].rearrange("p (c t) -> p c t", c=DC)))
                st_["t_otr"] = tevo
                p.fence("vector")
                p.wait("tensor", tevo)
                for dc in range(DC):
                    p.op("tensor", lambda h, dc=dc: h.matmul(
                        psO1[:, 0:512], ofmg[:, dc, :], Wo2[:, dc, 0:512],
                        start=(dc == 0), stop=False))
                    p.op("tensor", lambda h, dc=dc: h.matmul(
                        psO1[:, 512:1024], ofmg[:, dc, :], Wo2[:, dc, 512:1024],
                        start=(dc == 0), stop=False))
                p.op("tensor", lambda h: h.matmul(
                    psO1[:, 0:512], tonesr[:, :], bo2[:, 0:512], start=False, stop=True))
                p.op("tensor", lambda h: h.matmul(
                    psO1[:, 512:1024], tonesr[:, :], bo2[:, 512:1024], start=False, stop=True))
                tWo = p.last("tensor")
                p.wait("scalar", tWo)
                ty3 = p.op("scalar", lambda h: h.activation(
                    out=yo2[:, :], in_=psO1[:, :], func=AF.Identity))
                st_["t_psO1_rd"] = ty3
                p.fence("scalar")
                p.wait("vector", ty3)
                negm, rstd, tln = ln_stats(p, s, yo2[:, :], g % 2)
                p.op("vector", lambda h, negm=negm, rstd=rstd: h.tensor_scalar(
                    out=yt3[:, :], in0=yo2[:, :], scalar1=negm[:, :],
                    scalar2=rstd[:, :], op0=ALU.add, op1=ALU.mult))
                p.fence("vector")
                p.wait("vector", st_["t_xo2"][g % 2])
                tres = p.op("vector", lambda h, g=g: h.tensor_add(
                    xo2[g % 2][:, :], xg[g % 3][:, :], yt3[:, :]))
                p.fence("vector")
                p.wait("sync", tres)
                if g + 3 < G:
                    t_xld[g + 3] = p.dma("sync", xg[(g + 3) % 3][:, :],
                                         x_in[(g + 3) * 128:(g + 4) * 128, :],
                                         stream=f"xg{(g + 3) % 3}")
                tst = p.dma("sync", x3h[g * 128:(g + 1) * 128, :], xo2[g % 2][:, :],
                            stream=f"xs{g % 2}")
                if dbg_x3:
                    tst = p.dma("sync", x3_out[g * 128:(g + 1) * 128, :], xo2[g % 2][:, :],
                                stream=f"xs{g % 2}")
                st_["t_xo2"][g % 2] = tst

            for g in range(G):
                emit_A_ln(g)
                if g >= 1:
                    emit_B_pre(g - 1)
                if g >= 2:
                    emit_C(g - 2)
                emit_A_rest(g)
                if g - 2 >= 0 and g - 2 < len(prep_chunks):
                    prep_chunks[g - 2]()
            for ch in prep_chunks[max(0, G - 2):]:
                ch()
            emit_B_pre(G - 1)
            emit_C(G - 2)
            emit_C(G - 1)
            p.barrier()

        if not do_ffn:
            tcp = p.dma("sync", y_out, x3h, stream="fin")
            p.barrier()
            p.emit()
            print("launch2 built (attn only)")
            return nc

        p.scope = "l2prep"
        with ExitStack() as phc:
            xb3 = [_sb(nc, phc, f"xb3{i}", [128, 4, D], F32) for i in range(2)]
            htm3b = _sb(nc, phc, "htm3b", [128, D], BF16)
            hfm3b = [_sb(nc, phc, f"hfm3b{i}", [128, DC, 512], BF16) for i in range(2)]
            sl1 = [_sb(nc, phc, f"sl1{i}", [128, DC, 256], BF16) for i in range(3)]
            hmid = _sb(nc, phc, "hmid", [128, 32, 512], BF16)
            w2sl = [_sb(nc, phc, f"w2sl{i}", [128, 32, 128], BF16) for i in range(2)]
            o2fm = _sb(nc, phc, "o2fm", [128, DC, 512], BF16)
            tb1c = _sb(nc, phc, "tb1c", [128, 32], F32)
            tb2c = _sb(nc, phc, "tb2c", [128, 8], F32)
            y2tm = _sb(nc, phc, "y2tm", [128, D], F32)
            yt4 = _sb(nc, phc, "yt4", [128, D], F32)
            yo4 = [_sb(nc, phc, f"yo4{i}", [128, D], F32) for i in range(2)]
            psM1 = [_psum(nc, phc, f"psM1{i}", [128, 512], F32) for i in range(2)]
            psFM = [_psum(nc, phc, f"psFM{i}", [128, 512], F32) for i in range(2)]
            psTrF = _psum(nc, phc, "psTrF", [128, D], BF16)

            tb = [p.dma("sync", tb1c[:, :], b1_cols, stream="ic5"),
                  p.dma("sync", tb2c[:, :], b2_cols, stream="ic6")]
            sign1R = sign1_sp.rearrange("(c p) n -> p c n", p=128)
            bw2R = bw2_sp.rearrange("(c p) n -> p c n", p=128)

            t_x3ld = {}
            for i in range(min(2, TC)):
                t_x3ld[i] = p.dma(
                    "sync", xb3[i % 2][:, :, :],
                    x3h[i * 512:(i + 1) * 512, :].rearrange("(a p) d -> p a d", p=128),
                    stream=f"x3{'a' if i % 2 == 0 else 'b'}")
            t_htr3 = [None]
            t_h3rd = None
            t_hmid_rdx = [None]
            t_o2rdx = [None]
            t_trF = [None]
            t_th3 = {}
            t_sl = [None, None]
            t_slld = {}
            t_w2l = [None, None]
            t_w2ld = {}
            t_psM1ev = [None, None]
            t_psFMev = [None, None]

            t_yo = [None, None]

            def pf_front(tc):
                ttrs = []
                for st_ in range(4):
                    xa = xb3[tc % 2][:, st_, :]
                    p.wait("scalar", t_x3ld[tc])
                    p.wait("vector", t_x3ld[tc])
                    negm, rstd, tln = ln_stats(p, s, xa, st_ % 2)
                    p.wait("vector", t_htr3[0])
                    tok = p.op("vector", lambda h, xa=xa, negm=negm, rstd=rstd:
                               h.tensor_scalar(out=htm3b[:, :], in0=xa,
                                               scalar1=negm[:, :], scalar2=rstd[:, :],
                                               op0=ALU.add, op1=ALU.mult))
                    p.fence("vector")
                    p.wait("tensor", tok, t_hmid_rdx[0], t_trF[0])
                    if ttrs:
                        p.wait("tensor", ttrs[-1])
                    p.wait("tensor", t_h3rd)
                    for dc in range(DC):
                        p.op("tensor", lambda h, dc=dc: h.transpose(
                            psTrF[:, dc * 128:(dc + 1) * 128],
                            htm3b[:, dc * 128:(dc + 1) * 128], tident[:, :]))
                    ttr = p.last("tensor")
                    t_htr3[0] = ttr
                    p.wait("vector", ttr)
                    tev = p.op("vector", lambda h, tc=tc, st_=st_: h.tensor_copy(
                        hfm3b[tc % 2][:, :, st_ * 128:(st_ + 1) * 128],
                        psTrF[:, :].rearrange("p (c t) -> p c t", c=DC)))
                    t_trF[0] = tev
                    ttrs.append(tev)
                t_th3[tc] = ttrs[-1]

            def pf_body(tc):
                th3 = t_th3[tc]
                p.wait("tensor", th3, t_hmid_rdx[0])
                if tc == 0:
                    for oo in range(3):
                        t_slld[oo] = (oo, p.dma("sync", sl1[oo][:, :, :],
                                                sign1R[:, :, oo * 256:(oo + 1) * 256],
                                                stream=f"sl{oo}"))
                for oc in range(32):
                    bi, sub = oc // 2, oc % 2
                    slot = oc % 2
                    j, tld = t_slld[bi]
                    p.wait("tensor", tld, t_psM1ev[slot])
                    for dc in range(DC):
                        p.op("tensor", lambda h, j=j, slot=slot, dc=dc, sub=sub: h.matmul(
                            psM1[slot][:, :], sl1[j][:, dc, sub * 128:(sub + 1) * 128],
                            hfm3b[tc % 2][:, dc, :],
                            start=(dc == 0), stop=(dc == DC - 1)))
                    tmm = p.last("tensor")
                    if sub == 1:
                        nxt = bi + 3 if bi + 3 < 16 else (bi + 3 - 16 if tc + 1 < TC else None)
                        if nxt is not None:
                            p.wait("sync", tmm)
                            t_slld[nxt] = (j, p.dma("sync", sl1[j][:, :, :],
                                                    sign1R[:, :, nxt * 256:(nxt + 1) * 256],
                                                    stream=f"sl{j}"))
                    p.wait("scalar", tmm, *tb)
                    tev = p.op("scalar", lambda h, slot=slot, oc=oc: h.activation(
                        out=hmid[:, oc, :], in_=psM1[slot][:, :], func=AF.Gelu,
                        bias=tb1c[:, oc:oc + 1], scale=a1c[:, oc:oc + 1]))
                    t_psM1ev[slot] = tev
                thm = t_psM1ev[(31) % 2]

                p.wait("tensor", thm, t_o2rdx[0])
                if tc == 0:
                    for oo in range(2):
                        t_w2ld[oo] = p.dma("sync", w2sl[oo][:, :, :],
                                           bw2R[:, :, oo * 128:(oo + 1) * 128],
                                           stream=f"w2{oo}")
                for dc in range(DC):
                    j = dc % 2
                    slot = dc % 2
                    p.wait("tensor", t_w2ld[dc], t_psFMev[slot])
                    for hc in range(32):
                        p.op("tensor", lambda h, j=j, slot=slot, hc=hc: h.matmul(
                            psFM[slot][:, :], w2sl[j][:, hc, :], hmid[:, hc, :],
                            start=(hc == 0), stop=(hc == 31)))
                    tmm = p.last("tensor")
                    nxt = dc + 2 if dc + 2 < DC else (dc + 2 - DC if tc + 1 < TC else None)
                    if nxt is not None:
                        p.wait("sync", tmm)
                        t_w2ld[nxt] = p.dma("sync", w2sl[j][:, :, :],
                                            bw2R[:, :, nxt * 128:(nxt + 1) * 128],
                                            stream=f"w2{j}")
                    p.wait("scalar", tmm)
                    tev = p.op("scalar", lambda h, slot=slot, dc=dc: h.activation(
                        out=o2fm[:, dc, :], in_=psFM[slot][:, :], func=AF.Identity,
                        bias=tb2c[:, dc:dc + 1]))
                    t_psFMev[slot] = tev
                to2 = t_psFMev[(DC - 1) % 2]
                t_hmid_rdx[0] = p.last("tensor")

                p.wait("tensor", to2)
                for st_ in range(4):
                    gti = tc * 4 + st_
                    p.wait("tensor", t_o2rdx[0], t_trF[0])
                    for dc in range(DC):
                        p.op("tensor", lambda h, dc=dc, st_=st_: h.transpose(
                            psTrF[:, dc * 128:(dc + 1) * 128],
                            o2fm[:, dc, st_ * 128:(st_ + 1) * 128], tident[:, :]))
                    ttro = p.last("tensor")
                    p.wait("vector", ttro)
                    ty4 = p.op("vector", lambda h: h.tensor_copy(y2tm[:, :], psTrF[:, :]))
                    t_o2rdx[0] = ty4
                    t_trF[0] = ty4
                    p.fence("vector")
                    p.wait("scalar", ty4)
                    negm, rstd, tln = ln_stats(p, s, y2tm[:, :], st_ % 2)
                    p.op("vector", lambda h, negm=negm, rstd=rstd: h.tensor_scalar(
                        out=yt4[:, :], in0=y2tm[:, :], scalar1=negm[:, :],
                        scalar2=rstd[:, :], op0=ALU.add, op1=ALU.mult))
                    p.fence("vector")
                    p.wait("vector", t_yo[st_ % 2])
                    tres = p.op("vector", lambda h, tc=tc, st_=st_: h.tensor_add(
                        yo4[st_ % 2][:, :], xb3[tc % 2][:, st_, :], yt4[:, :]))
                    p.fence("vector")
                    p.wait("sync", tres)
                    t_yo[st_ % 2] = p.dma(
                        "sync", y_out[gti * 128:(gti + 1) * 128, :], yo4[st_ % 2][:, :],
                        stream=f"yo{st_ % 2}")
                    if st_ == 3 and tc + 2 < TC:
                        t_x3ld[tc + 2] = p.dma(
                            "sync", xb3[tc % 2][:, :, :],
                            x3h[(tc + 2) * 512:(tc + 3) * 512, :]
                            .rearrange("(a p) d -> p a d", p=128),
                            stream=f"x3{'a' if tc % 2 == 0 else 'b'}")

            for tc in range(TC):
                pf_front(tc)
                if tc >= 1:
                    pf_body(tc - 1)
            pf_body(TC - 1)
            p.barrier()
        p.barrier()
        p.emit()
    print(f"launch2 built: ops={ {k: len(v) for k, v in p.items.items()} }")
    return nc


def _f32(a):
    return np.ascontiguousarray(np.asarray(a, np.float32))


def _prep_launch1(ins, c, T=4096):
    m = {}
    m["x"] = _f32(ins["L0_attn_x"][c][:T])
    m["rn_cols"] = np.ascontiguousarray(
        (1.0 / np.arange(1, T + 1, dtype=np.float64)).astype(np.float32)
        .reshape(T // 128, 128).T)
    Wq_ = np.asarray(ins["L0_attn_W_qkv"], np.float32)
    m["Wqkv"] = _f32(np.concatenate([Wq_[:, 2048:3072], Wq_[:, 1024:2048], Wq_[:, 0:1024]], axis=1))
    bq_ = np.asarray(ins["L0_attn_b_qkv"], np.float32)
    m["bq_cols"] = _f32(np.concatenate([bq_[2048:3072], bq_[1024:2048], bq_[0:1024]]).reshape(24, 128).T)
    Wv = np.asarray(ins["L0_attn_W_vconv"], np.float32)
    m["wv_cols"] = _f32(np.concatenate(
        [Wv[:, j].reshape(8, 128).T for j in range(3)], axis=1))
    m["identb"] = np.eye(128, dtype=BF)
    m["identf"] = np.eye(128, dtype=np.float32)
    i_ = np.arange(128)
    m["maskadd"] = np.where(i_[None, :] <= i_[:, None], 0.0, -1e30).astype(np.float32)
    m["WQT"] = _f32(np.asarray(ins["L0_ffn_W_Q"], np.float32).T)
    m["WKT"] = _f32(np.asarray(ins["L0_ffn_W_K"], np.float32).T)
    m["WinvT"] = _f32(np.asarray(ins["L0_ffn_W_inv"], np.float32).T)
    m["rw1T"] = _f32(np.asarray(ins["L0_ffn_router_w1"], np.float32).T)
    m["rw2T"] = _f32(np.asarray(ins["L0_ffn_router_w2"], np.float32).T)
    m["WOT"] = _f32(np.asarray(ins["L0_ffn_W_O"], np.float32).T)
    m["Varr_f"] = _f32(np.asarray(ins["L0_ffn_V_fwd"], np.float32).transpose(1, 0, 2).reshape(1024, 512))
    m["Warr_f"] = _f32(np.asarray(ins["L0_ffn_W_fwd"], np.float32).transpose(1, 0, 2).reshape(1024, 512))
    m["Varr_i"] = _f32(np.asarray(ins["L0_ffn_W_inv_exp"], np.float32).transpose(1, 0, 2).reshape(1024, 512))
    m["Warr_i"] = _f32(np.asarray(ins["L0_ffn_V_inv"], np.float32).transpose(1, 0, 2).reshape(1024, 512))
    m["UT_f"] = _f32(np.asarray(ins["L0_ffn_U_fwd"], np.float32).transpose(0, 2, 1).reshape(512, 1024))
    m["UT_i"] = _f32(np.asarray(ins["L0_ffn_U_inv"], np.float32).transpose(0, 2, 1).reshape(512, 1024))
    m["b_f"] = _f32(ins["L0_ffn_b_fwd"])
    m["b_i"] = _f32(ins["L0_ffn_b_inv"])
    m["rb1_cols"] = _f32(np.asarray(ins["L0_ffn_router_b1"], np.float32).reshape(8, 128).T)
    m["rb2"] = _f32(np.asarray(ins["L0_ffn_router_b2"], np.float32).reshape(1, 8))
    m["eb"] = _f32(np.asarray(ins["L0_ffn_expert_bias"], np.float32).reshape(1, 8))
    m["alpha"] = _f32(np.broadcast_to(np.asarray(ins["L0_ffn_alpha_bi"], np.float32), (128, 1)))
    E = np.zeros((8, 512), dtype=BF)
    for cc in range(4):
        for pp in range(128):
            E[2 * cc + pp // 64, cc * 128 + pp] = 1.0
    m["Ecat"] = E
    m["onesrow"] = np.ones((1, 128), dtype=BF)
    m["ones512"] = np.ones((128, 512), np.float32)
    return m


def _il_order(x):
    S, Tw, d = x.shape
    return np.ascontiguousarray(
        x.reshape(8, Tw // 16, 16, d).transpose(1, 2, 0, 3).reshape(8 * Tw, d))


def _il_unorder(rows, Tw):
    d = rows.shape[1]
    return np.ascontiguousarray(
        rows.reshape(Tw // 16, 16, 8, d).transpose(2, 0, 1, 3).reshape(8, Tw, d))


def _prep_launch2(xwin, ins):
    m = {}
    m["x"] = _f32(_il_order(xwin))
    m["Wqkv2"] = _f32(np.concatenate([
        np.asarray(ins["L1_attn_Wq_w"], np.float32).T,
        np.asarray(ins["L1_attn_Wk_w"], np.float32).T,
        np.asarray(ins["L1_attn_Wv_w"], np.float32).T], axis=1))
    m["bqkv2"] = _f32(np.concatenate([
        ins["L1_attn_Wq_b"], ins["L1_attn_Wk_b"], ins["L1_attn_Wv_b"]]).reshape(1, -1))
    m["Wwr"] = _f32(np.concatenate([
        np.asarray(ins["L1_attn_Ww_w"], np.float32).T,
        np.asarray(ins["L1_attn_Wr_w"], np.float32).T], axis=1))
    m["bwr"] = _f32(np.concatenate([ins["L1_attn_Ww_b"], ins["L1_attn_Wr_b"]]).reshape(1, -1))
    m["WoT"] = _f32(np.asarray(ins["L1_attn_Wo_w"], np.float32).T)
    m["bo"] = _f32(np.asarray(ins["L1_attn_Wo_b"], np.float32).reshape(1, -1))
    i_ = np.arange(128)
    m["bdmask"] = ((i_[:, None] // 8) == (i_[None, :] // 8)).astype(BF)
    m["identb"] = np.eye(128, dtype=BF)
    m["identf"] = np.eye(128, dtype=np.float32)
    m["onesrow"] = np.ones((1, 128), dtype=BF)
    m["w1n"] = _f32(ins["L1_ffn_w1"])
    m["w1T"] = _f32(np.asarray(ins["L1_ffn_w1"], np.float32).T)
    m["w2n"] = _f32(ins["L1_ffn_w2"])
    m["w2T"] = _f32(np.asarray(ins["L1_ffn_w2"], np.float32).T)
    m["b1_cols"] = _f32(np.asarray(ins["L1_ffn_b1"], np.float32).reshape(32, 128).T)
    m["b2_cols"] = _f32(np.asarray(ins["L1_ffn_b2"], np.float32).reshape(8, 128).T)
    return m


_BUILD_CACHE = {}


def kernel(**inputs):
    T = 4096
    if "l1" not in _BUILD_CACHE:
        _BUILD_CACHE["l1"] = build_launch1(T=T)
        _BUILD_CACHE["l2"] = build_launch2(R=T)
    nc1, nc2 = _BUILD_CACHE["l1"], _BUILD_CACHE["l2"]

    in_maps1 = [_prep_launch1(inputs, c, T) for c in range(8)]
    res1 = run_bass_kernel_spmd(nc1, in_maps1, core_ids=list(range(8)))
    x2 = np.stack([np.asarray(res1.results[c]["x2"]) for c in range(8)], axis=0)

    Tw = T // 8
    in_maps2 = [_prep_launch2(x2[:, c * Tw:(c + 1) * Tw, :], inputs) for c in range(8)]
    res2 = run_bass_kernel_spmd(nc2, in_maps2, core_ids=list(range(8)))
    out = np.concatenate(
        [_il_unorder(np.asarray(res2.results[c]["y"]), Tw) for c in range(8)], axis=1)
    return out.astype(np.float32)


# version 07
# speedup vs baseline: 1.1461x; 1.1461x over previous
import numpy as np
import ml_dtypes
from contextlib import ExitStack

import concourse.bass as bass
import concourse.mybir as mybir
from concourse.bass_utils import run_bass_kernel_spmd

F32 = mybir.dt.float32
BF16 = mybir.dt.bfloat16
AF = mybir.ActivationFunctionType
ALU = mybir.AluOpType
AX = mybir.AxisListType
BF = ml_dtypes.bfloat16
DS = bass.ds

D = 1024
DC = 8
EPS = 1e-5

ENGS = ("tensor", "vector", "scalar", "gpsimd", "sync")


class Prog:

    def __init__(self, nc, stack):
        self.nc = nc
        self.stack = stack
        self.sems = {}
        for name in ("tensor", "vector", "scalar", "gpsimd"):
            self.sems[name] = stack.enter_context(nc.semaphore("sem_" + name))
        self.cnt = {k: 0 for k in self.sems}
        self.items = {e: [] for e in ENGS}
        self.scope = None

    def _stream_sem(self, stream):
        if stream not in self.sems:
            self.sems[stream] = self.stack.enter_context(
                self.nc.semaphore("sem_" + stream))
            self.cnt[stream] = 0
        return self.sems[stream]

    def op(self, eng, fn):
        self.cnt[eng] += 1
        self.items[eng].append(("op_mark", fn, self.sems[eng], 1, self.scope))
        return (eng, self.cnt[eng])

    def dma(self, eng, out, in_, stream="misc", **kw):
        sem = self._stream_sem("dq_" + stream)
        self.cnt["dq_" + stream] += 16
        self.items[eng].append(
            ("op_mark", lambda h, o=out, i=in_, k=kw: h.dma_start(out=o, in_=i, **k),
             sem, 16, self.scope))
        return ("dq_" + stream, self.cnt["dq_" + stream])

    def wait(self, eng, *toks):
        best = {}
        for t in toks:
            if t is None:
                continue
            best[t[0]] = max(best.get(t[0], 0), t[1])
        for name, val in best.items():
            self.items[eng].append(("wait", self.sems[name], val))

    def fence(self, eng):
        self.items[eng].append(("wait", self.sems[eng], self.cnt[eng]))

    def last(self, eng):
        return (eng, self.cnt[eng])

    def barrier(self):
        snap = dict(self.cnt)
        for e in ENGS:
            for name, val in snap.items():
                if val > 0:
                    self.items[e].append(("wait", self.sems[name], val))

    def emit(self):
        block = self.stack.enter_context(self.nc.Block())

        def runner(name):
            def run(handle):
                cur = [None, None]
                for item in self.items[name]:
                    if item[0] == "op_mark":
                        sc = item[4]
                        if sc != cur[0]:
                            if cur[0] is not None:
                                self.nc.leave_named_scope(cur[0], cur[1], False)
                            if sc is not None:
                                cur[1], _ = self.nc.enter_named_scope(sc, False)
                            cur[0] = sc
                        inst = item[1](handle)
                        inst.then_inc(item[2], item[3])
                    else:
                        handle.wait_ge(item[1], item[2])
                if cur[0] is not None:
                    self.nc.leave_named_scope(cur[0], cur[1], False)
            return run

        block.tensor(runner("tensor"))
        block.vector(runner("vector"))
        block.scalar(runner("scalar"))
        block.gpsimd(runner("gpsimd"))
        block.sync(runner("sync"))


def ln_stats(p, s, xa_ap, j, rl=None):
    sq, s1, s2 = s[f"sq{j}"], s[f"s1{j}"], s[f"s2{j}"]
    var, sd = s[f"var{j}"], s[f"sd{j}"]
    negm, rstd, m2n = s[f"negm{j}"], s[f"rstd{j}"], s[f"m2n{j}"]
    p.wait("scalar", s.get(("g", j)))
    p.op("scalar", lambda h: h.activation(out=sq[:, :], in_=xa_ap,
                                          func=AF.Square, accum_out=s2[:, :]))
    p.fence("scalar")
    p.op("vector", lambda h: h.reduce_sum(s1[:, :], xa_ap, axis=AX.X))
    p.fence("vector")
    p.op("vector", lambda h: h.tensor_scalar_mul(negm[:, :], s1[:, :], -1.0 / D))
    p.fence("vector")
    tm2 = p.op("vector", lambda h: h.scalar_tensor_tensor(
        out=m2n[:, :], in0=negm[:, :], scalar=-1.0, in1=negm[:, :],
        op0=ALU.mult, op1=ALU.mult))
    p.fence("vector")
    p.wait("scalar", tm2)
    p.op("scalar", lambda h: h.mul(var[:, :], s2[:, :], s["cInvD"][:, :]))
    p.fence("scalar")
    p.op("scalar", lambda h: h.add(var[:, :], var[:, :], m2n[:, :]))
    p.fence("scalar")
    if rl is not None:
        p.op("scalar", lambda h: h.mul(var[:, :], var[:, :], rl))
        p.fence("scalar")
        p.op("scalar", lambda h: h.mul(var[:, :], var[:, :], rl))
        p.fence("scalar")
    p.op("scalar", lambda h: h.add(var[:, :], var[:, :], s["cEps"][:, :]))
    p.fence("scalar")
    tsd = p.op("scalar", lambda h: h.activation(out=sd[:, :], in_=var[:, :],
                                                func=AF.Sqrt))
    p.wait("vector", tsd)
    p.op("vector", lambda h: h.reciprocal(rstd[:, :], sd[:, :]))
    p.fence("vector")
    if rl is not None:
        p.op("vector", lambda h: h.tensor_mul(rstd[:, :], rstd[:, :], rl))
        p.fence("vector")
    tok = p.op("vector", lambda h: h.engine_nop())
    s[("g", j)] = tok
    return negm, rstd, tok


def alloc_ln_scratch(nc, stack, s, tag=""):
    for j in (0, 1):
        s[f"sq{j}"] = _sb(nc, stack, f"lnsq{tag}{j}", [128, D], F32)
        s[f"s1{j}"] = _sb(nc, stack, f"lns1{tag}{j}", [128, 1], F32)
        s[f"s2{j}"] = _sb(nc, stack, f"lns2{tag}{j}", [128, 1], F32)
        s[f"var{j}"] = _sb(nc, stack, f"lnvar{tag}{j}", [128, 1], F32)
        s[f"sd{j}"] = _sb(nc, stack, f"lnsd{tag}{j}", [128, 1], F32)
        s[f"negm{j}"] = _sb(nc, stack, f"lnnegm{tag}{j}", [128, 1], F32)
        s[f"rstd{j}"] = _sb(nc, stack, f"lnrstd{tag}{j}", [128, 1], F32)
        s[f"m2n{j}"] = _sb(nc, stack, f"lnm2n{tag}{j}", [128, 1], F32)
    s["cInvD"] = _sb(nc, stack, f"lncid{tag}", [128, 1], F32)
    s["cEps"] = _sb(nc, stack, f"lnceps{tag}", [128, 1], F32)


def _sb(nc, stack, name, shape, dtype):
    return stack.enter_context(nc.sbuf_tensor(name, shape, dtype))


def _psum(nc, stack, name, shape, dtype):
    return stack.enter_context(nc.psum_tensor(name, shape, dtype))


def build_launch1(T=4096, do_ffn=True, dbg_x1=False, phases=3, p1_parts=("ln", "qkv", "conv")):
    TC = T // 512
    QT = T // 128
    nc = bass.Bass("TRN2", target_bir_lowering=False)

    def din(name, shape, dt=F32):
        return nc.dram_tensor(name, shape, dt, kind="ExternalInput").ap()

    def dout(name, shape, dt=F32):
        return nc.dram_tensor(name, shape, dt, kind="ExternalOutput").ap()

    def dint(name, shape, dt):
        return nc.dram_tensor(name, shape, dt, kind="Internal").ap()

    x_in = din("x", [T, D])
    rn_cols = din("rn_cols", [128, T // 128])
    Wqkv = din("Wqkv", [D, 3 * D])
    bq_cols = din("bq_cols", [128, 24])
    wv_cols = din("wv_cols", [128, 24])
    identb = din("identb", [128, 128], BF16)
    identf = din("identf", [128, 128], F32)
    maskadd = din("maskadd", [128, 128])
    x2_out = dout("x2", [T, D])
    x1_out = dout("x1dbg", [T, D]) if dbg_x1 else None

    Wqkv_bf = dint("Wqkv_bf", [D, 3 * D], BF16)
    q_sp = dint("q_sp", [DC, 128, T], BF16)
    wvn_sp = dint("wvn_sp", [QT, 128, D], BF16)
    x1h = dint("x1h", [T, D], F32)

    if do_ffn:
        WQT = din("WQT", [D, D]); WKT = din("WKT", [D, D])
        WinvT = din("WinvT", [D, D]); rw1T = din("rw1T", [D, D])
        rw2T = din("rw2T", [D, 8]); WOT = din("WOT", [D, D])
        Varr_f = din("Varr_f", [D, 512]); Warr_f = din("Warr_f", [D, 512])
        Varr_i = din("Varr_i", [D, 512]); Warr_i = din("Warr_i", [D, 512])
        UT_f = din("UT_f", [512, D]); UT_i = din("UT_i", [512, D])
        b_f = din("b_f", [8, D]); b_i = din("b_i", [8, D])
        rb1_cols = din("rb1_cols", [128, 8])
        rb2 = din("rb2", [1, 8]); eb = din("eb", [1, 8])
        alpha_in = din("alpha", [128, 1])
        Ecat = din("Ecat", [8, 512], BF16)
        onesrow = din("onesrow", [1, 128], BF16)
        ones512 = din("ones512", [128, 512])
        ffn_bf = {}
        for nm, src, shp in (
            ("WQT", WQT, [D, D]), ("WKT", WKT, [D, D]), ("WinvT", WinvT, [D, D]),
            ("rw1T", rw1T, [D, D]), ("rw2T", rw2T, [D, 8]), ("WOT", WOT, [D, D]),
            ("Varr_f", Varr_f, [D, 512]), ("Warr_f", Warr_f, [D, 512]),
            ("Varr_i", Varr_i, [D, 512]), ("Warr_i", Warr_i, [D, 512]),
            ("UT_f", UT_f, [512, D]), ("UT_i", UT_i, [512, D]),
            ("b_f", b_f, [8, D]), ("b_i", b_i, [8, D]),
        ):
            ffn_bf[nm] = (dint(nm + "_bf", shp, BF16), src)

    stack = ExitStack()
    with stack:
        p = Prog(nc, stack)

        tident = _sb(nc, stack, "tident", [128, 128], BF16)
        tidentf = _sb(nc, stack, "tidentf", [128, 128], F32)
        tmask = _sb(nc, stack, "tmask", [128, 128], F32)
        tbqc = _sb(nc, stack, "tbqc", [128, 24], F32)
        tbqs = _sb(nc, stack, "tbqs", [128, 8], F32)
        twvc = _sb(nc, stack, "twvc", [128, 24], F32)
        c32 = _sb(nc, stack, "c32", [128, 1], F32)
        trn = _sb(nc, stack, "trn", [128, T // 128], F32)
        s = {}
        alloc_ln_scratch(nc, stack, s)
        p.op("vector", lambda h: h.memset(s["cInvD"][:, :], 1.0 / D))
        t_lnc = p.op("vector", lambda h: h.memset(s["cEps"][:, :], EPS))
        p.fence("vector")
        s[("g", 0)] = t_lnc
        s[("g", 1)] = t_lnc

        p.scope = "l1p0"
        t_ident = p.dma("sync", tident[:, :], identb, stream="ic0")
        p.dma("sync", tidentf[:, :], identf, stream="ic1")
        t_mask = p.dma("sync", tmask[:, :], maskadd, stream="ic2")
        t_bqc = p.dma("sync", tbqc[:, :], bq_cols, stream="ic3")
        t_wvc = p.dma("sync", twvc[:, :], wv_cols, stream="ic4")
        t_rn = p.dma("sync", trn[:, :], rn_cols, stream="ic5")
        t_Wq = p.dma("gpsimd", Wqkv_bf, Wqkv, stream="gpq")
        if do_ffn:
            for nm, (dst, src) in ffn_bf.items():
                p.dma("gpsimd", dst, src, stream="gpc")
        p.wait("vector", t_bqc)
        p.op("vector", lambda h: h.tensor_scalar_mul(tbqs[:, :], tbqc[:, 16:24], 1.0 / 32.0))
        p.op("vector", lambda h: h.memset(c32[:, :], 1.0 / 32.0))
        p.fence("vector")

        if phases == 0:
            tcp = p.dma("sync", x2_out, x_in, stream="fin")
            if dbg_x1:
                p.dma("sync", x1_out, x_in, stream="fin")
            p.barrier()
            p.emit()
            print("launch1 built (phase0)")
            return nc
        p.scope = "l1p1"
        ph12 = ExitStack()
        kfm = _sb(nc, ph12, "kfm", [128, DC, T], BF16)
        Ctm = _sb(nc, ph12, "Ctm", [128, QT, D], BF16)
        with ExitStack() as ph1:
            xtb = [_sb(nc, ph1, f"xtb{i}", [128, D], F32) for i in range(2)]
            htm = _sb(nc, ph1, "htm", [128, D], BF16)
            hfm = _sb(nc, ph1, "hfm", [128, DC, 512], BF16)
            Wb = [_sb(nc, ph1, f"Wb{i}", [128, DC, 384], BF16) for i in range(2)]
            vbuf = [_sb(nc, ph1, f"vbuf{i}", [128, DC, 512], BF16) for i in range(2)]
            vlast = _sb(nc, ph1, "vlast", [128, DC, TC], BF16)
            qsb = [_sb(nc, ph1, f"qsb{i}", [128, 1024], BF16) for i in range(2)]
            Cfm = _sb(nc, ph1, "Cfm", [128, DC, 512], BF16)
            wvfm = _sb(nc, ph1, "wvfm", [128, DC, 512], BF16)
            wsb = [_sb(nc, ph1, "wsb0", [128, D], BF16)] * 2
            psQ = [_psum(nc, ph1, f"psQ{i}", [128, 1024], F32) for i in range(2)]
            psTrH = _psum(nc, ph1, "psTrH", [128, D], BF16)
            psTrC = _psum(nc, ph1, "psTrC", [128, D], BF16)

            WqkvR = Wqkv_bf.rearrange("(c p) n -> p c n", p=128)
            t_vev = {}
            t_conv = {}
            t_psev = [None, None]
            t_qspill = [None, None]
            t_wspill = [None, None]
            t_hread = {}

            def emit_conv(ct, last):
                vb = vbuf[ct % 2]
                vnb = vbuf[(ct + 1) % 2]
                p.wait("vector", t_vev[ct], t_wvc)
                if not last:
                    p.wait("vector", t_vev[ct + 1])
                for dc in range(DC):
                    w0 = twvc[:, dc:dc + 1]
                    w1 = twvc[:, 8 + dc:9 + dc]
                    w2 = twvc[:, 16 + dc:17 + dc]
                    p.op("vector", lambda h, dc=dc, w1=w1, vb=vb: h.tensor_scalar_mul(
                        Cfm[:, dc, :], vb[:, dc, :], w1))
                    p.fence("vector")
                    p.op("vector", lambda h, dc=dc, w0=w0, vb=vb: h.scalar_tensor_tensor(
                        out=Cfm[:, dc, 1:512], in0=vb[:, dc, 0:511], scalar=w0,
                        in1=Cfm[:, dc, 1:512], op0=ALU.mult, op1=ALU.add))
                    p.fence("vector")
                    if ct > 0:
                        p.op("vector", lambda h, dc=dc, w0=w0, ct=ct: h.scalar_tensor_tensor(
                            out=Cfm[:, dc, 0:1], in0=vlast[:, dc, ct - 1:ct],
                            scalar=w0, in1=Cfm[:, dc, 0:1], op0=ALU.mult, op1=ALU.add))
                        p.fence("vector")
                    p.op("vector", lambda h, dc=dc, w2=w2, vb=vb: h.scalar_tensor_tensor(
                        out=Cfm[:, dc, 0:511], in0=vb[:, dc, 1:512], scalar=w2,
                        in1=Cfm[:, dc, 0:511], op0=ALU.mult, op1=ALU.add))
                    p.fence("vector")
                    p.op("vector", lambda h, dc=dc, w2=w2, vb=vb: h.tensor_scalar_mul(
                        wvfm[:, dc, 0:511], vb[:, dc, 1:512], w2))
                    p.fence("vector")
                    if not last:
                        p.op("vector", lambda h, dc=dc, w2=w2, vnb=vnb: h.scalar_tensor_tensor(
                            out=Cfm[:, dc, 511:512], in0=vnb[:, dc, 0:1], scalar=w2,
                            in1=Cfm[:, dc, 511:512], op0=ALU.mult, op1=ALU.add))
                        p.fence("vector")
                        p.op("vector", lambda h, dc=dc, w2=w2, vnb=vnb: h.tensor_scalar_mul(
                            wvfm[:, dc, 511:512], vnb[:, dc, 0:1], w2))
                    else:
                        p.op("vector", lambda h, dc=dc: h.memset(wvfm[:, dc, 511:512], 0.0))
                    p.fence("vector")
                tcv = p.op("vector", lambda h: h.tensor_copy(Cfm[:, 0, 0:1], Cfm[:, 0, 0:1]))
                t_conv[ct] = tcv
                t_prev_ev = tcv
                for kcl in range(4):
                    kc = ct * 4 + kcl
                    p.wait("tensor", t_prev_ev, t_ident)
                    for dc in range(DC):
                        p.op("tensor", lambda h, dc=dc, kcl=kcl: h.transpose(
                            psTrC[:, dc * 128:(dc + 1) * 128],
                            Cfm[:, dc, kcl * 128:(kcl + 1) * 128], tident[:, :]))
                    tt = p.last("tensor")
                    p.wait("vector", tt)
                    tctm = p.op("vector", lambda h, kc=kc: h.tensor_copy(
                        Ctm[:, kc, :], psTrC[:, :]))
                    p.wait("tensor", tctm)
                    for dc in range(DC):
                        p.op("tensor", lambda h, dc=dc, kcl=kcl: h.transpose(
                            psTrC[:, dc * 128:(dc + 1) * 128],
                            wvfm[:, dc, kcl * 128:(kcl + 1) * 128], tident[:, :]))
                    tt = p.last("tensor")
                    p.wait("vector", tt, t_wspill[0], t_wspill[1])
                    tw = p.op("vector", lambda h, kcl=kcl: h.tensor_copy(
                        wsb[kcl % 2][:, :], psTrC[:, :]))
                    t_prev_ev = tw
                    p.wait("sync", tw)
                    t_wspill[kcl % 2] = p.dma("sync", wvn_sp[kc, :, :], wsb[kcl % 2][:, :], stream=f"wv{kcl % 2}")

            OGS = [(tc, og) for tc in range(TC) for og in range(8)]
            t_wdma = {}
            p.wait("sync", t_Wq)
            for j in (range(2) if "qkv" in p1_parts else range(0)):
                tcj, ogj = OGS[j]
                t_wdma[OGS[j]] = p.dma("sync", Wb[ogj % 2][:, :, :],
                                       WqkvR[:, :, ogj * 384:(ogj + 1) * 384],
                                       stream=f"wb{ogj % 2}")

            t_xtb_rd = [None, None]
            t_htr = [None]
            for tc in range(TC):
                ttrs = []
                for st_ in range(4):
                    g = (tc * 4 + st_) % 2
                    r0 = tc * 512 + st_ * 128
                    p.wait("sync", t_xtb_rd[g])
                    tx = p.dma("sync", xtb[g][:, :], x_in[r0:r0 + 128, :],
                               stream=f"xt{g}")
                    xa = xtb[g][:, :]
                    p.wait("scalar", tx)
                    p.wait("vector", tx)
                    negm, rstd, tln = ln_stats(p, s, xa, st_ % 2)
                    p.wait("vector", t_htr[0])
                    tok = p.op("vector", lambda h, xa=xa, negm=negm, rstd=rstd:
                               h.tensor_scalar(out=htm[:, :], in0=xa,
                                               scalar1=negm[:, :], scalar2=rstd[:, :],
                                               op0=ALU.add, op1=ALU.mult))
                    p.fence("vector")
                    p.wait("tensor", tok, t_ident)
                    if tc > 0:
                        p.wait("tensor", t_hread[tc - 1])
                    if ttrs:
                        p.wait("tensor", ttrs[-1])
                    for dc in range(DC):
                        p.op("tensor", lambda h, dc=dc: h.transpose(
                            psTrH[:, dc * 128:(dc + 1) * 128],
                            htm[:, dc * 128:(dc + 1) * 128], tident[:, :]))
                    ttr = p.last("tensor")
                    t_htr[0] = ttr
                    p.wait("vector", ttr)
                    tev = p.op("vector", lambda h, st_=st_: h.tensor_copy(
                        hfm[:, :, st_ * 128:(st_ + 1) * 128],
                        psTrH[:, :].rearrange("p (c t) -> p c t", c=DC)))
                    ttrs.append(tev)
                    t_xtb_rd[g] = tok
                t_hfm_done = ttrs[-1]

                p.wait("tensor", t_hfm_done)
                t_read_og = None
                for og in (range(8) if "qkv" in p1_parts else range(0)):
                    gi = tc * 8 + og
                    p.wait("tensor", t_wdma[OGS[gi]], t_Wq)
                    for oc in range(3):
                        ocg = og * 3 + oc
                        slot = (ocg // 2) % 2
                        half = ocg % 2
                        if half == 0:
                            p.wait("tensor", t_psev[slot])
                        for dc in range(DC):
                            p.op("tensor", lambda h, slot=slot, half=half, og=og, oc=oc, dc=dc:
                                 h.matmul(psQ[slot][:, half * 512:(half + 1) * 512],
                                          Wb[og % 2][:, dc, oc * 128:(oc + 1) * 128],
                                          hfm[:, dc, :],
                                          start=(dc == 0), stop=(dc == DC - 1)))
                        tmm = p.last("tensor")
                        if half == 0:
                            continue
                        pair = ocg // 2
                        p.wait("scalar", tmm)
                        if pair * 2 >= 16:
                            p.wait("scalar", t_qspill[slot])
                        if pair * 2 < 8 and tc >= 2:
                            p.wait("scalar", t_conv[tc - 2])
                        for hh in range(2):
                            oo = pair * 2 + hh
                            if oo >= 16:
                                tev = p.op("scalar", lambda h, slot=slot, hh=hh, oo=oo: h.activation(
                                    out=qsb[slot][:, hh * 512:(hh + 1) * 512],
                                    in_=psQ[slot][:, hh * 512:(hh + 1) * 512], func=AF.Identity,
                                    bias=tbqs[:, oo - 16:oo - 15], scale=c32[:, :]))
                            elif oo >= 8:
                                tev = p.op("scalar", lambda h, slot=slot, hh=hh, oo=oo, tc=tc: h.activation(
                                    out=kfm[:, oo - 8, tc * 512:(tc + 1) * 512],
                                    in_=psQ[slot][:, hh * 512:(hh + 1) * 512], func=AF.Identity,
                                    bias=tbqc[:, oo:oo + 1]))
                            else:
                                tev = p.op("scalar", lambda h, slot=slot, hh=hh, oo=oo, tc=tc: h.activation(
                                    out=vbuf[tc % 2][:, oo, :],
                                    in_=psQ[slot][:, hh * 512:(hh + 1) * 512], func=AF.Identity,
                                    bias=tbqc[:, oo:oo + 1]))
                                t_vev[tc] = tev
                        t_psev[slot] = tev
                        if pair * 2 >= 16:
                            p.fence("scalar")
                            t_qspill[slot] = p.dma(
                                "scalar",
                                q_sp.transpose([1, 0, 2])[:, pair * 2 - 16:pair * 2 - 14,
                                                          tc * 512:(tc + 1) * 512],
                                qsb[slot][:, :].rearrange("p (a t) -> p a t", a=2),
                                stream=f"qs{slot}")
                    t_read_og = p.last("tensor")
                    if gi + 2 < len(OGS):
                        tcn, ogn = OGS[gi + 2]
                        p.wait("sync", t_read_og)
                        t_wdma[OGS[gi + 2]] = p.dma(
                            "sync", Wb[ogn % 2][:, :, :],
                            WqkvR[:, :, ogn * 384:(ogn + 1) * 384],
                            stream=f"wb{ogn % 2}")
                    if og == 2 and "conv" in p1_parts:
                        p.wait("vector", t_vev[tc])
                        p.op("vector", lambda h, tc=tc: h.tensor_copy(
                            vlast[:, :, tc:tc + 1], vbuf[tc % 2][:, :, 511:512]))
                        p.fence("vector")
                        if tc >= 1:
                            emit_conv(tc - 1, last=False)
                t_hread[tc] = t_read_og
            if "conv" in p1_parts:
                emit_conv(TC - 1, last=True)
            p.barrier()
        if phases <= 1:
            ph12.close()
            tcp = p.dma("sync", x2_out, x_in, stream="fin")
            if dbg_x1:
                p.dma("sync", x1_out, x_in, stream="fin")
            p.barrier()
            p.emit()
            print("launch1 built (phase1)")
            return nc

        p.scope = "l1p2"
        with ExitStack() as ph2:
            Srow = _sb(nc, ph2, "Srow", [128, T], F32)
            Prow = _sb(nc, ph2, "Prow", [128, T], BF16)
            PTb = _sb(nc, ph2, "PTb", [128, T], BF16)
            qb = [_sb(nc, ph2, f"qb{i}", [128, DC, 128], BF16) for i in range(2)]
            wvt = [_sb(nc, ph2, f"wvt{i}", [128, D], BF16) for i in range(2)]
            xb2 = [_sb(nc, ph2, f"xb2{i}", [128, D], F32) for i in range(2)]
            ytile = _sb(nc, ph2, "ytile", [128, D], F32)
            ytmp = _sb(nc, ph2, "ytmp", [128, D], F32)
            xot = [_sb(nc, ph2, f"xot{i}", [128, D], F32) for i in range(2)]
            negmS = _sb(nc, ph2, "negmS", [128, 1], F32)
            sumP = _sb(nc, ph2, "sumP", [128, 1], F32)
            rlP = _sb(nc, ph2, "rlP", [128, 1], F32)
            dtmp = _sb(nc, ph2, "dtmp", [128, 128], F32)
            diagP = _sb(nc, ph2, "diagP", [128, 1], F32)
            negdiag = _sb(nc, ph2, "negdiag", [128, 1], F32)
            psS = [_psum(nc, ph2, f"psS{i}", [128, 512], F32) for i in range(4)]
            psO = _psum(nc, ph2, "psO", [128, D], F32)
            psTrP = _psum(nc, ph2, "psTrP", [128, 512], BF16)

            q_spR = q_sp.transpose([1, 0, 2])
            t_qld = {}
            t_wld = {}
            t_xld = {}
            for qi in range(min(2, QT)):
                t_qld[qi] = p.dma("sync", qb[qi % 2][:, :, :],
                                  q_spR[:, :, qi * 128:(qi + 1) * 128],
                                  stream=f"qb{qi % 2}")
                t_wld[qi] = p.dma("sync", wvt[qi % 2][:, :], wvn_sp[qi, :, :], stream=f"wt{qi % 2}")
                t_xld[qi] = p.dma("sync", xb2[qi % 2][:, :],
                                  x_in[qi * 128:(qi + 1) * 128, :],
                                  stream=f"xb{qi % 2}")
            t_sev = [None] * 4
            t_smm = {}
            t_y = None
            t_xst = [None, None]

            for qi in range(QT):
                width = (qi + 1) * 128
                nkb = (width + 511) // 512
                p.wait("tensor", t_qld[qi])
                for kb in range(nkb):
                    cols = min(512, width - kb * 512)
                    slot = kb % 4
                    p.wait("tensor", t_sev[slot])
                    for dc in range(DC):
                        p.op("tensor", lambda h, slot=slot, dc=dc, kb=kb, cols=cols, qi=qi:
                             h.matmul(psS[slot][:, 0:cols],
                                      qb[qi % 2][:, dc, :],
                                      kfm[:, dc, kb * 512:kb * 512 + cols],
                                      start=(dc == 0), stop=(dc == DC - 1)))
                    tmm = p.last("tensor")
                    p.wait("scalar", tmm)
                    t_sev[slot] = p.op("scalar", lambda h, slot=slot, kb=kb, cols=cols:
                                       h.activation(out=Srow[:, kb * 512:kb * 512 + cols],
                                                    in_=psS[slot][:, 0:cols],
                                                    func=AF.Identity))
                t_smm[qi] = tmm
                p.wait("vector", t_sev[slot], t_mask)
                p.op("vector", lambda h, qi=qi: h.tensor_add(
                    Srow[:, qi * 128:(qi + 1) * 128],
                    Srow[:, qi * 128:(qi + 1) * 128], tmask[:, :]))
                p.fence("vector")
                tmax = p.op("vector", lambda h, width=width: h.reduce_max(
                    negmS[:, :], Srow[:, 0:width], axis=AX.X, negate=True))
                p.fence("vector")
                p.wait("scalar", tmax)
                texp = p.op("scalar", lambda h, width=width: h.activation(
                    out=Prow[:, 0:width], in_=Srow[:, 0:width], func=AF.Exp,
                    bias=negmS[:, :], accum_out=sumP[:, :]))
                p.wait("vector", texp)
                p.op("vector", lambda h, qi=qi: h.tensor_mul(
                    dtmp[:, :], Prow[:, qi * 128:(qi + 1) * 128], tident[:, :]))
                p.fence("vector")
                p.op("vector", lambda h: h.reduce_sum(diagP[:, :], dtmp[:, :], axis=AX.X))
                p.fence("vector")
                tnd = p.op("vector", lambda h: h.tensor_scalar_mul(
                    negdiag[:, :], diagP[:, :], -1.0))
                p.fence("vector")
                t_ptev = {}
                ngrp = (qi + 1 + 3) // 4
                prev_ev = None
                for g in range(ngrp):
                    kcs = range(g * 4, min(qi + 1, g * 4 + 4))
                    p.wait("tensor", texp if g == 0 else None, prev_ev)
                    for j, kc in enumerate(kcs):
                        p.op("tensor", lambda h, j=j, kc=kc: h.transpose(
                            psTrP[:, j * 128:(j + 1) * 128],
                            Prow[:, kc * 128:(kc + 1) * 128], tident[:, :]))
                    tt = p.last("tensor")
                    p.wait("vector", tt)
                    prev_ev = p.op("vector", lambda h, g=g, n=len(list(kcs)): h.tensor_copy(
                        PTb[:, g * 512:g * 512 + n * 128], psTrP[:, 0:n * 128]))
                    t_ptev[g] = prev_ev
                p.wait("tensor", t_y, t_ptev[ngrp - 1])
                for kc in range(qi + 1):
                    p.op("tensor", lambda h, kc=kc, qi=qi: h.matmul(
                        psO[:, 0:512], PTb[:, kc * 128:(kc + 1) * 128],
                        Ctm[:, kc, 0:512], start=(kc == 0), stop=(kc == qi)))
                    p.op("tensor", lambda h, kc=kc, qi=qi: h.matmul(
                        psO[:, 512:1024], PTb[:, kc * 128:(kc + 1) * 128],
                        Ctm[:, kc, 512:1024], start=(kc == 0), stop=(kc == qi)))
                tpv = p.last("tensor")
                p.wait("vector", tpv, t_wld[qi], tnd)
                ty = p.op("vector", lambda h, qi=qi: h.scalar_tensor_tensor(
                    out=ytile[:, :], in0=wvt[qi % 2][:, :], scalar=negdiag[:, :],
                    in1=psO[:, :], op0=ALU.mult, op1=ALU.add))
                t_y = ty
                p.fence("vector")
                if qi + 2 < QT:
                    p.wait("sync", t_smm[qi])
                    t_qld[qi + 2] = p.dma("sync", qb[qi % 2][:, :, :],
                                          q_spR[:, :, (qi + 2) * 128:(qi + 3) * 128],
                                          stream=f"qb{qi % 2}")
                    p.wait("sync", ty)
                    t_wld[qi + 2] = p.dma("sync", wvt[qi % 2][:, :], wvn_sp[qi + 2, :, :], stream=f"wt{qi % 2}")
                p.op("vector", lambda h: h.reciprocal(rlP[:, :], sumP[:, :]))
                p.fence("vector")
                p.wait("scalar", ty)
                negm, rstd, tln = ln_stats(p, s, ytile[:, :], 0, rl=rlP[:, :])
                p.op("vector", lambda h, negm=negm, rstd=rstd: h.tensor_scalar(
                    out=ytmp[:, :], in0=ytile[:, :], scalar1=negm[:, :],
                    scalar2=rstd[:, :], op0=ALU.add, op1=ALU.mult))
                p.fence("vector")
                p.wait("vector", t_xld[qi], t_xst[qi % 2])
                tres = p.op("vector", lambda h, qi=qi: h.tensor_add(
                    xot[qi % 2][:, :], xb2[qi % 2][:, :], ytmp[:, :]))
                p.fence("vector")
                p.wait("sync", tres)
                if qi + 2 < QT:
                    t_xld[qi + 2] = p.dma("sync", xb2[qi % 2][:, :],
                                          x_in[(qi + 2) * 128:(qi + 3) * 128, :],
                                          stream=f"xb{qi % 2}")
                tst = p.dma("sync", x1h[qi * 128:(qi + 1) * 128, :], xot[qi % 2][:, :], stream=f"xo{qi % 2}")
                if dbg_x1:
                    tst = p.dma("sync", x1_out[qi * 128:(qi + 1) * 128, :], xot[qi % 2][:, :], stream=f"xo{qi % 2}")
                t_xst[qi % 2] = tst
            p.barrier()
        ph12.close()
        if phases <= 2:
            tcp = p.dma("sync", x2_out, x1h, stream="fin")
            p.barrier()
            p.emit()
            print("launch1 built (phase2)")
            return nc

        p.scope = "l1p3"
        if do_ffn:
            with ExitStack() as ph3:
                psl = [_sb(nc, ph3, f"psl{i}", [128, DC, 256], BF16) for i in range(2)]
                R2b = _sb(nc, ph3, "R2b", [128, DC, 8], BF16)
                WOb = _sb(nc, ph3, "WOb", [128, DC, D], BF16)
                UTfb = [_sb(nc, ph3, f"UTfb{i}", [128, 4, 256], BF16) for i in range(2)]
                UTib = [_sb(nc, ph3, f"UTib{i}", [128, 4, 256], BF16) for i in range(2)]
                bfb = _sb(nc, ph3, "bfb", [8, D], BF16)
                bib = _sb(nc, ph3, "bib", [8, D], BF16)
                Eb = _sb(nc, ph3, "Eb", [8, 512], BF16)
                trb1 = _sb(nc, ph3, "trb1", [128, 8], F32)
                tr2eb = _sb(nc, ph3, "tr2eb", [1, 8], F32)
                tr2ebb = _sb(nc, ph3, "tr2ebb", [1, 8], BF16)
                tr2a = _sb(nc, ph3, "tr2a", [1, 8], F32)
                tr2b_ = _sb(nc, ph3, "tr2b_", [1, 8], F32)
                tonesr = _sb(nc, ph3, "tonesr", [1, 128], BF16)
                to512 = _sb(nc, ph3, "to512", [128, 512], F32)
                ones512b = _sb(nc, ph3, "ones512b", [1, 512], BF16)
                alc = _sb(nc, ph3, "alc", [128, 1], F32)
                hfm3 = _sb(nc, ph3, "hfm3", [128, DC, 512], BF16)
                htm3 = _sb(nc, ph3, "htm3", [128, D], BF16)
                xtb3 = [_sb(nc, ph3, f"xtb3{i}", [128, 4, D], F32) for i in range(2)]
                Qfm = _sb(nc, ph3, "Qfm", [128, DC, 512], BF16)
                Kfm3 = _sb(nc, ph3, "Kfm3", [128, DC, 512], BF16)
                iQfm = _sb(nc, ph3, "iQfm", [128, DC, 512], BF16)
                iKfm = _sb(nc, ph3, "iKfm", [128, DC, 512], BF16)
                xVf = _sb(nc, ph3, "xVf", [128, 4, 512], BF16)
                xVi = _sb(nc, ph3, "xVi", [128, 4, 512], BF16)
                cumf = _sb(nc, ph3, "cumf", [128, 4, 512], F32)
                cumi = _sb(nc, ph3, "cumi", [128, 4, 512], F32)
                carryf = _sb(nc, ph3, "carryf", [128, 4], F32)
                carryi = _sb(nc, ph3, "carryi", [128, 4], F32)
                hrfm = [_sb(nc, ph3, f"hrfm{i}", [128, DC, 512], BF16) for i in range(2)]
                lgfm = [_sb(nc, ph3, f"lgfm{i}", [8, 512], F32) for i in range(2)]
                lgt = [_sb(nc, ph3, f"lgt{i}", [128, 4, 8], F32) for i in range(2)]
                lmx = [_sb(nc, ph3, f"lmx{i}", [128, 4], F32) for i in range(2)]
                lsm = [_sb(nc, ph3, f"lsm{i}", [128, 4], F32) for i in range(2)]
                wgtb = [_sb(nc, ph3, f"wgtb{i}", [128, 32], BF16) for i in range(2)]
                wgtTf = _sb(nc, ph3, "wgtTf", [8, 512], BF16)
                wgtTi = _sb(nc, ph3, "wgtTi", [8, 512], BF16)
                mtmp = _sb(nc, ph3, "mtmp", [128, 512], F32)
                mf = _sb(nc, ph3, "mf", [128, 4, 512], BF16)
                mi = _sb(nc, ph3, "mi", [128, 4, 512], BF16)
                outpre = _sb(nc, ph3, "outpre", [128, DC, 512], BF16)
                yo = _sb(nc, ph3, "yo", [128, D], F32)
                yt2 = _sb(nc, ph3, "yt2", [128, D], F32)
                xo3 = [_sb(nc, ph3, f"xo3{i}", [128, D], F32) for i in range(2)]
                psP = [_psum(nc, ph3, f"psP{i}", [128, 512], F32) for i in range(2)]
                psLog = _psum(nc, ph3, "psLog", [8, 512], F32)
                psWx = _psum(nc, ph3, "psWx", [128, 512], F32)
                psTr3 = _psum(nc, ph3, "psTr3", [128, D], BF16)
                psO2 = _psum(nc, ph3, "psO2", [128, D], F32)
                psOP = _psum(nc, ph3, "psOP", [128, 512], F32)

                WR = {nm: ffn_bf[nm][0].rearrange("(c p) n -> p c n", p=128)
                      for nm in ("WQT", "WKT", "WinvT", "rw1T", "Varr_f", "Warr_f",
                                 "Varr_i", "Warr_i", "UT_f", "UT_i", "WOT", "rw2T")}
                ldt = []
                ldt.append(p.dma("sync", WOb[:, :, :], WR["WOT"], stream="wld"))
                ldt.append(p.dma("sync", R2b[:, :, :], WR["rw2T"], stream="wld"))
                ldt.append(p.dma("sync", bfb[:, :], ffn_bf["b_f"][0], stream="wld"))
                ldt.append(p.dma("sync", bib[:, :], ffn_bf["b_i"][0], stream="wld"))
                ldt.append(p.dma("sync", Eb[:, :], Ecat, stream="wld"))
                ldt.append(p.dma("sync", trb1[:, :], rb1_cols, stream="wld"))
                ldt.append(p.dma("sync", tr2a[:, :], rb2, stream="wld"))
                ldt.append(p.dma("sync", tr2b_[:, :], eb, stream="wld"))
                ldt.append(p.dma("sync", alc[:, :], alpha_in, stream="wld"))
                ldt.append(p.dma("sync", tonesr[:, :], onesrow, stream="wld"))
                ldt.append(p.dma("sync", to512[:, :], ones512, stream="wld"))
                p.wait("vector", *ldt)
                p.op("vector", lambda h: h.memset(ones512b[:, :], 1.0))
                p.op("vector", lambda h: h.tensor_add(tr2eb[:, :], tr2a[:, :], tr2b_[:, :]))
                p.fence("vector")
                p.op("vector", lambda h: h.tensor_copy(tr2ebb[:, :], tr2eb[:, :]))
                p.op("vector", lambda h: h.memset(carryf[:, :], 0.0))
                tsetup = p.op("vector", lambda h: h.memset(carryi[:, :], 0.0))
                p.fence("vector")
                ta2 = p.op("vector", lambda h: h.tensor_scalar_mul(
                    bib[:, :], bib[:, :], alc[0:8, :]))
                p.fence("vector")

                t_ut_rd = [None, None]
                t_x3 = {}
                t_x3[0] = p.dma("sync", xtb3[0][:, :, :],
                                x1h[0:512, :].rearrange("(a p) d -> p a d", p=128),
                                stream="x30")
                t_ps = [None, None]
                t_psOP = None
                t_psO2 = None
                t_xo = [None, None]
                t_hfm_rd = None
                t_q_rd = None
                t_lgev = [None]
                t_u_mm = [None]
                t_m_done = [None, None]
                t_htr3g = [None]
                t_thr = [None, None]
                t_tle = [None, None]
                t_twT = [None, None]
                t_wx_ev = [None]
                t_tr3_ev = [None]

                t_psl_rd = [None, None]
                slice_ctr = [0]

                def slice_load(wname, bi, kch):
                    j = slice_ctr[0] % 2
                    slice_ctr[0] += 1
                    p.wait("sync", t_psl_rd[j])
                    tok = p.dma("sync", psl[j][:, 0:kch, :],
                                WR[wname][:, 0:kch, bi * 256:(bi + 1) * 256],
                                stream=f"pw{j}")
                    return j, tok

                def stream_mm(wname, n_out, kch, src_ap_fn, evac_fn, toks_wait):
                    nb = (n_out + 1) // 2
                    pend = [slice_load(wname, bi, kch) for bi in range(min(2, nb))]
                    for oi in range(n_out):
                        bi, sub = oi // 2, oi % 2
                        j, tld = pend[bi % 2]
                        slot = oi % 2
                        p.wait("tensor", t_ps[slot], tld, *toks_wait)
                        for dc in range(kch):
                            p.op("tensor", lambda h, slot=slot, j=j, dc=dc, sub=sub, kch=kch:
                                 h.matmul(psP[slot][:, :],
                                          psl[j][:, dc, sub * 128:(sub + 1) * 128],
                                          src_ap_fn(dc), start=(dc == 0), stop=(dc == kch - 1)))
                        tmm = p.last("tensor")
                        if sub == 1 or oi == n_out - 1:
                            t_psl_rd[j] = tmm
                            if bi + 2 < nb:
                                pend[bi % 2] = slice_load(wname, bi + 2, kch)
                        t_ps[slot] = evac_fn(oi, psP[slot], tmm)

                def proj(dst, wname, src_ap_fn, toks_wait, name):
                    def ev(oi, ps, tmm):
                        p.wait("vector", tmm)
                        return p.op("vector", lambda h, oi=oi, ps=ps: h.tensor_copy(
                            dst[:, oi, :], ps[:, :]))
                    stream_mm(wname, DC, DC, src_ap_fn, ev, toks_wait)
                    return t_ps[(DC - 1) % 2]

                for tc in range(TC):
                    p.wait("scalar", t_x3[tc])
                    p.wait("vector", t_x3[tc])
                    ttrs = []
                    for st_ in range(4):
                        xa = xtb3[tc % 2][:, st_, :]
                        negm, rstd, tln = ln_stats(p, s, xa, st_ % 2)
                        p.wait("vector", t_htr3g[0])
                        tok = p.op("vector", lambda h, xa=xa, negm=negm, rstd=rstd:
                                   h.tensor_scalar(out=htm3[:, :], in0=xa,
                                                   scalar1=negm[:, :], scalar2=rstd[:, :],
                                                   op0=ALU.add, op1=ALU.mult))
                        p.fence("vector")
                        p.wait("tensor", tok, t_hfm_rd)
                        if ttrs:
                            p.wait("tensor", ttrs[-1])
                        for dc in range(DC):
                            p.op("tensor", lambda h, dc=dc: h.transpose(
                                psTr3[:, dc * 128:(dc + 1) * 128],
                                htm3[:, dc * 128:(dc + 1) * 128], tident[:, :]))
                        t_htr3g[0] = p.last("tensor")
                        ttr = p.last("tensor")
                        p.wait("vector", ttr)
                        tev = p.op("vector", lambda h, st_=st_: h.tensor_copy(
                            hfm3[:, :, st_ * 128:(st_ + 1) * 128],
                            psTr3[:, :].rearrange("p (c t) -> p c t", c=DC)))
                        ttrs.append(tev)
                    t_h3 = ttrs[-1]
                    if tc + 1 < TC:
                        p.wait("sync", t_h3)
                        t_x3[tc + 1] = p.dma(
                            "sync", xtb3[(tc + 1) % 2][:, :, :],
                            x1h[(tc + 1) * 512:(tc + 2) * 512, :]
                            .rearrange("(a p) d -> p a d", p=128),
                            stream=f"x3{(tc + 1) % 2}")

                    tQ = proj(Qfm, "WQT", lambda dc: hfm3[:, dc, :], [t_h3], "Q")
                    tK = proj(Kfm3, "WKT", lambda dc: hfm3[:, dc, :], [t_h3], "K")
                    tiQ = proj(iQfm, "WinvT", lambda dc: Qfm[:, dc, :], [tQ], "iQ")
                    tiK = proj(iKfm, "WinvT", lambda dc: Kfm3[:, dc, :], [tK], "iK")
                    t_hfm_rd = p.last("tensor")

                    BRS = (
                        ("Varr_f", "Warr_f", Qfm, Kfm3, tQ, tK, cumf, carryf, xVf, wgtTf, mf),
                        ("Varr_i", "Warr_i", iQfm, iKfm, tiQ, tiK, cumi, carryi, xVi, wgtTi, mi),
                    )

                    def br_xv(br):
                        (Vnm, Wnm, Qs, Ks, tQs, tKs, cum, carry, xVb, wgtT, mb) = BRS[br]
                        def ev_xv(oi, ps, tmm, xVb=xVb):
                            p.wait("vector", tmm)
                            return p.op("vector", lambda h, oi=oi, ps=ps: h.tensor_copy(
                                xVb[:, oi, :], ps[:, :]))
                        stream_mm(Vnm, 4, DC, lambda dc, Qs=Qs: Qs[:, dc, :], ev_xv, [tQs])

                    def br_yw(br):
                        (Vnm, Wnm, Qs, Ks, tQs, tKs, cum, carry, xVb, wgtT, mb) = BRS[br]
                        def ev_yw(oi, ps, tmm, cum=cum, carry=carry):
                            p.wait("vector", tmm)
                            tcp = p.op("vector", lambda h, oi=oi, ps=ps, cum=cum, carry=carry:
                                       h.tensor_tensor_scan(
                                           out=cum[:, oi, :], data0=to512[:, :], data1=ps[:, :],
                                           initial=carry[:, oi:oi + 1], op0=ALU.mult, op1=ALU.add))
                            p.fence("vector")
                            p.op("vector", lambda h, oi=oi, cum=cum, carry=carry: h.tensor_copy(
                                carry[:, oi:oi + 1], cum[:, oi, 511:512]))
                            p.fence("vector")
                            return tcp
                        stream_mm(Wnm, 4, DC, lambda dc, Ks=Ks: Ks[:, dc, :], ev_yw, [tKs])

                    def br_router(br):
                        (Vnm, Wnm, Qs, Ks, tQs, tKs, cum, carry, xVb, wgtT, mb) = BRS[br]
                        hrb = hrfm[br]
                        def ev_hr(oi, ps, tmm, hrb=hrb):
                            p.wait("scalar", tmm)
                            return p.op("scalar", lambda h, oi=oi, ps=ps, hrb=hrb: h.activation(
                                out=hrb[:, oi, :], in_=ps[:, :], func=AF.Gelu,
                                bias=trb1[:, oi:oi + 1]))
                        stream_mm("rw1T", DC, DC, lambda dc, Qs=Qs: Qs[:, dc, :], ev_hr, [tQs])
                        t_thr[br] = t_ps[(DC - 1) % 2]

                    def br_logits(br):
                        hrb = hrfm[br]
                        p.wait("tensor", t_thr[br], t_lgev[0])
                        for rhc in range(DC):
                            p.op("tensor", lambda h, rhc=rhc, hrb=hrb: h.matmul(
                                psLog[:, :], R2b[:, rhc, :], hrb[:, rhc, :],
                                start=(rhc == 0), stop=False))
                        p.wait("tensor", ta2)
                        p.op("tensor", lambda h: h.matmul(
                            psLog[:, :], tr2ebb[:, :],
                            ones512b[:, :], start=False, stop=True))
                        tlg = p.last("tensor")
                        p.wait("vector", tlg)
                        tle = p.op("vector", lambda h, br=br: h.tensor_copy(
                            lgfm[br][:, :], psLog[:, :]))
                        t_lgev[0] = tle
                        t_tle[br] = tle

                    def br_softmax(br):
                        (Vnm, Wnm, Qs, Ks, tQs, tKs, cum, carry, xVb, wgtT, mb) = BRS[br]
                        p.wait("tensor", t_tle[br], t_wx_ev[0])
                        for j in range(4):
                            p.op("tensor", lambda h, j=j, br=br: h.transpose(
                                psWx[:, j * 8:(j + 1) * 8], lgfm[br][:, j * 128:(j + 1) * 128],
                                tidentf[0:8, 0:8]))
                        tlt = p.last("tensor")
                        p.wait("vector", tlt)
                        twx = p.op("vector", lambda h, br=br: h.tensor_copy(
                            lgt[br][:, :, :], psWx[:, 0:32].rearrange("p (a b) -> p a b", a=4)))
                        t_wx_ev[0] = twx
                        p.fence("vector")
                        p.op("vector", lambda h, br=br: h.reduce_max(
                            lmx[br][:, :], lgt[br][:, :, :], axis=AX.X, negate=True))
                        p.fence("vector")
                        p.op("vector", lambda h, br=br: h.tensor_tensor(
                            lgt[br][:, :, :], lgt[br][:, :, :],
                            lmx[br][:, :].unsqueeze(2).broadcast_to([128, 4, 8]), op=ALU.add))
                        p.fence("vector")
                        tsm = p.op("vector", lambda h: h.engine_nop())
                        p.wait("scalar", tsm)
                        tex = p.op("scalar", lambda h, br=br: h.activation(
                            out=lgt[br][:, :, :], in_=lgt[br][:, :, :], func=AF.Exp))
                        p.wait("vector", tex)
                        p.op("vector", lambda h, br=br: h.reduce_sum(
                            lsm[br][:, :], lgt[br][:, :, :], axis=AX.X))
                        p.fence("vector")
                        p.op("vector", lambda h, br=br: h.reciprocal(lsm[br][:, :], lsm[br][:, :]))
                        p.fence("vector")
                        p.op("vector", lambda h, br=br: h.tensor_tensor(
                            wgtb[br][:, :].rearrange("p (a b) -> p a b", a=4), lgt[br][:, :, :],
                            lsm[br][:, :].unsqueeze(2).broadcast_to([128, 4, 8]), op=ALU.mult))
                        twg = p.op("vector", lambda h: h.engine_nop())
                        p.fence("vector")
                        p.wait("tensor", twg, t_tr3_ev[0])
                        for j in range(4):
                            p.op("tensor", lambda h, j=j, br=br: h.transpose(
                                psTr3[0:8, j * 128:(j + 1) * 128], wgtb[br][:, j * 8:(j + 1) * 8],
                                tident[:, :]))
                        twt = p.last("tensor")
                        p.wait("vector", twt)
                        p.op("vector", lambda h, wgtT=wgtT: h.tensor_copy(
                            wgtT[:, :], psTr3[0:8, 0:512]))
                        twT = p.op("vector", lambda h: h.engine_nop())
                        t_tr3_ev[0] = twT
                        t_twT[br] = twT
                        p.fence("vector")

                    def br_m(br):
                        (Vnm, Wnm, Qs, Ks, tQs, tKs, cum, carry, xVb, wgtT, mb) = BRS[br]
                        p.wait("vector", t_u_mm[0])
                        for krc in range(4):
                            p.wait("tensor", t_twT[br], t_wx_ev[0])
                            tEx = p.op("tensor", lambda h, krc=krc, wgtT=wgtT: h.matmul(
                                psWx[:, :], Eb[:, krc * 128:(krc + 1) * 128], wgtT[:, :],
                                start=True, stop=True))
                            p.wait("vector", tEx)
                            p.op("vector", lambda h, krc=krc, cum=cum: h.tensor_mul(
                                mtmp[:, :], cum[:, krc, :], psWx[:, :]))
                            p.fence("vector")
                            if br == 1:
                                p.op("vector", lambda h: h.tensor_scalar_mul(
                                    mtmp[:, :], mtmp[:, :], alc[:, :]))
                                p.fence("vector")
                            tm = p.op("vector", lambda h, krc=krc, xVb=xVb, mb=mb: h.tensor_mul(
                                mb[:, krc, :], mtmp[:, :], xVb[:, krc, :]))
                            t_wx_ev[0] = tm
                            t_m_done[br] = tm
                            p.fence("vector")

                    for stage in (br_xv, br_yw, br_router, br_logits, br_softmax, br_m):
                        stage(0)
                        stage(1)
                    p.wait("tensor", t_m_done[0], t_m_done[1])
                    t_ut_ld = {}

                    def ut_load(pair):
                        j = pair % 2
                        p.wait("sync", t_ut_rd[j])
                        ta = p.dma("sync", UTfb[j][:, :, :],
                                   WR["UT_f"][:, 0:4, pair * 256:(pair + 1) * 256],
                                   stream=f"utf{j}")
                        tb = p.dma("sync", UTib[j][:, :, :],
                                   WR["UT_i"][:, 0:4, pair * 256:(pair + 1) * 256],
                                   stream=f"uti{j}")
                        t_ut_ld[pair] = (ta, tb)

                    ut_load(0)
                    ut_load(1)
                    for dc in range(DC):
                        pair, sub = dc // 2, dc % 2
                        j = pair % 2
                        p.wait("tensor", t_psOP, *t_ut_ld[pair])
                        first = True
                        for UTa, mb in ((UTfb[j], mf), (UTib[j], mi)):
                            for krc in range(4):
                                p.op("tensor", lambda h, sub=sub, krc=krc, UTa=UTa, mb=mb, first=first:
                                     h.matmul(psOP[:, :],
                                              UTa[:, krc, sub * 128:(sub + 1) * 128],
                                              mb[:, krc, :], start=first, stop=False))
                                first = False
                        p.op("tensor", lambda h, dc=dc: h.matmul(
                            psOP[:, :], bfb[:, dc * 128:(dc + 1) * 128], wgtTf[:, :],
                            start=False, stop=False))
                        p.op("tensor", lambda h, dc=dc: h.matmul(
                            psOP[:, :], bib[:, dc * 128:(dc + 1) * 128], wgtTi[:, :],
                            start=False, stop=True))
                        tmm = p.last("tensor")
                        t_u_mm[0] = tmm
                        if sub == 1:
                            t_ut_rd[j] = tmm
                            if pair + 2 < 4:
                                ut_load(pair + 2)
                        p.wait("vector", tmm)
                        t_psOP = p.op("vector", lambda h, dc=dc: h.tensor_copy(
                            outpre[:, dc, :], psOP[:, :]))
                    t_op = t_psOP

                    for st_ in range(4):
                        p.wait("tensor", t_op, t_psO2)
                        for dc in range(DC):
                            p.op("tensor", lambda h, st_=st_, dc=dc: h.matmul(
                                psO2[:, 0:512], outpre[:, dc, st_ * 128:(st_ + 1) * 128],
                                WOb[:, dc, 0:512], start=(dc == 0), stop=(dc == DC - 1)))
                            p.op("tensor", lambda h, st_=st_, dc=dc: h.matmul(
                                psO2[:, 512:1024], outpre[:, dc, st_ * 128:(st_ + 1) * 128],
                                WOb[:, dc, 512:1024], start=(dc == 0), stop=(dc == DC - 1)))
                        tmm = p.last("tensor")
                        p.wait("vector", tmm)
                        ty2 = p.op("vector", lambda h: h.tensor_copy(yo[:, :], psO2[:, :]))
                        t_psO2 = ty2
                        p.fence("vector")
                        p.wait("scalar", ty2)
                        p.wait("vector", t_rn)
                        negm, rstd, tln = ln_stats(p, s, yo[:, :], 1,
                                                   rl=trn[:, tc * 4 + st_:tc * 4 + st_ + 1])
                        p.op("vector", lambda h, negm=negm, rstd=rstd: h.tensor_scalar(
                            out=yt2[:, :], in0=yo[:, :], scalar1=negm[:, :],
                            scalar2=rstd[:, :], op0=ALU.add, op1=ALU.mult))
                        p.fence("vector")
                        p.wait("vector", t_xo[st_ % 2])
                        tres = p.op("vector", lambda h, tc=tc, st_=st_: h.tensor_add(
                            xo3[st_ % 2][:, :], xtb3[tc % 2][:, st_, :], yt2[:, :]))
                        p.fence("vector")
                        p.wait("sync", tres)
                        t_xo[st_ % 2] = p.dma(
                            "sync", x2_out[tc * 512 + st_ * 128: tc * 512 + (st_ + 1) * 128, :],
                            xo3[st_ % 2][:, :], stream=f"x2o{st_ % 2}")
                    t_q_rd = t_hfm_rd
            p.barrier()
        else:
            tcp = p.dma("sync", x2_out, x1h, stream="fin")
            p.wait("sync", tcp)

        p.barrier()
        p.emit()
    print(f"launch1 built: ops={ {k: len(v) for k, v in p.items.items()} }")
    return nc


def build_launch2(R=4096, do_ffn=True, dbg_x3=False):
    G = R // 128
    TC = R // 512
    nc = bass.Bass("TRN2", target_bir_lowering=False)

    def din(name, shape, dt=F32):
        return nc.dram_tensor(name, shape, dt, kind="ExternalInput").ap()

    def dout(name, shape, dt=F32):
        return nc.dram_tensor(name, shape, dt, kind="ExternalOutput").ap()

    def dint(name, shape, dt):
        return nc.dram_tensor(name, shape, dt, kind="Internal").ap()

    x_in = din("x", [R, D])
    Wqkv2 = din("Wqkv2", [D, 3 * D])
    bqkv2 = din("bqkv2", [1, 3 * D])
    Wwr = din("Wwr", [D, 32])
    bwr = din("bwr", [1, 32])
    WoT = din("WoT", [D, D])
    bo = din("bo", [1, D])
    bdmask = din("bdmask", [128, 128], BF16)
    identb = din("identb", [128, 128], BF16)
    identf = din("identf", [128, 128], F32)
    onesrow = din("onesrow", [1, 128], BF16)
    w1n = din("w1n", [4 * D, D])
    w1T = din("w1T", [D, 4 * D])
    w2n = din("w2n", [D, 4 * D])
    w2T = din("w2T", [4 * D, D])
    b1_cols = din("b1_cols", [128, 32])
    b2_cols = din("b2_cols", [128, 8])
    y_out = dout("y", [R, D])
    x3_out = dout("x3dbg", [R, D]) if dbg_x3 else None

    Wqkv2_bf = dint("Wqkv2_bf", [D, 3 * D], BF16)
    Wwr_bf = dint("Wwr_bf", [D, 32], BF16)
    WoT_bf = dint("WoT_bf", [D, D], BF16)
    w1T_bf = dint("w1T_bf", [D, 4 * D], BF16)
    w2T_bf = dint("w2T_bf", [4 * D, D], BF16)
    sign1_sp = dint("sign1_sp", [D, 4 * D], BF16)
    bw2_sp = dint("bw2_sp", [4 * D, D], BF16)
    x3h = dint("x3h", [R, D], F32)

    stack = ExitStack()
    with stack:
        p = Prog(nc, stack)
        tident = _sb(nc, stack, "tident", [128, 128], BF16)
        tidentf = _sb(nc, stack, "tidentf", [128, 128], F32)
        tmaskbd = _sb(nc, stack, "tmaskbd", [128, 128], BF16)
        tonesr = _sb(nc, stack, "tonesr", [1, 128], BF16)
        a1c = _sb(nc, stack, "a1c", [128, 32], F32)
        s = {}
        alloc_ln_scratch(nc, stack, s)
        p.op("vector", lambda h: h.memset(s["cInvD"][:, :], 1.0 / D))
        t_lnc = p.op("vector", lambda h: h.memset(s["cEps"][:, :], EPS))
        p.fence("vector")
        s[("g", 0)] = t_lnc
        s[("g", 1)] = t_lnc

        p.scope = "l2p0"
        t_ic = []
        t_ic.append(p.dma("sync", tident[:, :], identb, stream="ic0"))
        t_ic.append(p.dma("sync", tidentf[:, :], identf, stream="ic1"))
        t_ic.append(p.dma("sync", tmaskbd[:, :], bdmask, stream="ic2"))
        t_ic.append(p.dma("sync", tonesr[:, :], onesrow, stream="ic3"))
        t_qkv = p.dma("gpsimd", Wqkv2_bf, Wqkv2, stream="gpq")
        t_wr = p.dma("gpsimd", Wwr_bf, Wwr, stream="gpq")
        t_wo = p.dma("gpsimd", WoT_bf, WoT, stream="gpq")
        t_w1c = p.dma("gpsimd", w1T_bf, w1T, stream="gpw")
        t_w2c = p.dma("gpsimd", w2T_bf, w2T, stream="gpw")

        p.scope = "l2pa"
        with ExitStack() as pha:
            Wq2 = _sb(nc, pha, "Wq2", [128, DC, 3 * D], BF16)
            Wwr2 = _sb(nc, pha, "Wwr2", [128, DC, 32], BF16)
            Wo2 = _sb(nc, pha, "Wo2", [128, DC, D], BF16)
            bq2 = _sb(nc, pha, "bq2", [1, 3 * D], BF16)
            bwr2 = _sb(nc, pha, "bwr2", [1, 32], BF16)
            bo2 = _sb(nc, pha, "bo2", [1, D], BF16)
            ones512b2 = _sb(nc, pha, "ones512b2", [1, 512], BF16)
            xg = [_sb(nc, pha, f"xg{i}", [128, D], F32) for i in range(3)]
            htm2 = [_sb(nc, pha, f"htm2{i}", [128, D], BF16) for i in range(2)]
            hfmg = _sb(nc, pha, "hfmg", [128, DC, 128], BF16)
            qtm = _sb(nc, pha, "qtm", [128, D], F32)
            ktm = _sb(nc, pha, "ktm", [128, D], F32)
            vtm = _sb(nc, pha, "vtm", [128, D], F32)
            wrtm = _sb(nc, pha, "wrtm", [128, 32], F32)
            qb16 = [_sb(nc, pha, f"qb16{i}", [128, D], BF16) for i in range(2)]
            kb16 = [_sb(nc, pha, f"kb16{i}", [128, D], BF16) for i in range(2)]
            qfmg = _sb(nc, pha, "qfmg", [128, DC, 128], BF16)
            kzg = _sb(nc, pha, "kzg", [128, 16, 128], BF16)
            vw = [_sb(nc, pha, f"vw{i}", [128, D], BF16) for i in range(2)]
            Gsb16 = _sb(nc, pha, "Gsb16", [128, 2048], BF16)
            sqn = _sb(nc, pha, "sqn", [128, D], F32)
            ssq = _sb(nc, pha, "ssq", [128, 16], F32)
            rq = _sb(nc, pha, "rq", [128, 16], F32)
            rk = _sb(nc, pha, "rk", [128, 16], F32)
            wsig = _sb(nc, pha, "wsig", [128, 16], F32)
            rsig = _sb(nc, pha, "rsig", [128, 16], F32)
            oscale = [_sb(nc, pha, f"oscale{i}", [128, 16], F32) for i in range(2)]
            ob16 = _sb(nc, pha, "ob16", [128, D], BF16)
            ofmg = _sb(nc, pha, "ofmg", [128, DC, 128], BF16)
            yo2 = _sb(nc, pha, "yo2", [128, D], F32)
            yt3 = _sb(nc, pha, "yt3", [128, D], F32)
            xo2 = [_sb(nc, pha, f"xo2{i}", [128, D], F32) for i in range(2)]
            psA = _psum(nc, pha, "psA", [128, 2048], F32)
            psO1 = _psum(nc, pha, "psO1", [128, D], F32)
            psTrX = _psum(nc, pha, "psTrX", [128, D], BF16)

            t_w = []
            p.wait("sync", t_qkv, t_wr, t_wo)
            t_w.append(p.dma("sync", Wq2[:, :, :],
                             Wqkv2_bf.rearrange("(c p) n -> p c n", p=128), stream="wl2"))
            t_w.append(p.dma("sync", Wwr2[:, :, :],
                             Wwr_bf.rearrange("(c p) n -> p c n", p=128), stream="wl2"))
            t_w.append(p.dma("sync", Wo2[:, :, :],
                             WoT_bf.rearrange("(c p) n -> p c n", p=128), stream="wl2"))
            t_w.append(p.dma("gpsimd", bq2[:, :], bqkv2, stream="wl3"))
            t_w.append(p.dma("gpsimd", bwr2[:, :], bwr, stream="wl3"))
            t_w.append(p.dma("gpsimd", bo2[:, :], bo, stream="wl3"))
            p.wait("vector", *t_w)
            p.op("vector", lambda h: h.memset(ones512b2[:, :], 1.0))
            t_setup = p.last("vector")
            p.fence("vector")
            p.wait("tensor", *t_w)

            wrow = [_sb(nc, pha, "wrow0", [128, 4 * D], F32)] * 2
            a2c = _sb(nc, pha, "a2c", [128, 8], F32)
            a2row = _sb(nc, pha, "a2row", [1, D], F32)
            a2rowb = _sb(nc, pha, "a2rowb", [1, D], BF16)
            a2bcb = _sb(nc, pha, "a2bcb", [128, D], BF16)
            sgbuf = [_sb(nc, pha, f"sgbuf{i}", [128, D], BF16) for i in range(2)]
            sobuf = [_sb(nc, pha, f"sobuf{i}", [128, D], BF16) for i in range(2)]
            psR = _psum(nc, pha, "psR", [128, 512], F32)
            prep_state = {"t_a": [None, None], "t_sg": [None, None], "t_so": [None, None],
                          "ta2": None, "tbc": None}

            def prep_alpha1(i0_):
                for i in range(i0_, i0_ + 4):
                    j = 0
                    p.wait("sync", *prep_state["t_a"])
                    tw = p.dma("sync", wrow[j][:, 0:1024],
                               w1n[i * 128:(i + 1) * 128, :], stream=f"wr{j}")
                    p.wait("vector", tw)
                    prep_state["t_a"][0] = p.op("vector", lambda h, i=i, j=j: h.reduce_sum(
                        a1c[:, i:i + 1], wrow[j][:, 0:1024], axis=AX.X,
                        apply_absolute_value=True))
                    p.fence("vector")

            def prep_alpha2(i0_):
                for i in range(i0_, i0_ + 4):
                    j = 0
                    p.wait("sync", *prep_state["t_a"])
                    tw = p.dma("sync", wrow[j][:, :], w2n[i * 128:(i + 1) * 128, :],
                               stream=f"wr{j}")
                    p.wait("vector", tw)
                    prep_state["t_a"][0] = p.op("vector", lambda h, i=i, j=j: h.reduce_sum(
                        a2c[:, i:i + 1], wrow[j][:, :], axis=AX.X,
                        apply_absolute_value=True))
                    p.fence("vector")

            def prep_fin():
                p.op("vector", lambda h: h.tensor_scalar_mul(a1c[:, :], a1c[:, :], 1.0 / D))
                ta2_ = p.op("vector", lambda h: h.tensor_scalar_mul(
                    a2c[:, :], a2c[:, :], 1.0 / (4 * D)))
                p.fence("vector")
                p.wait("tensor", ta2_)
                tprev = None
                for dc in range(DC):
                    p.wait("tensor", tprev)
                    p.op("tensor", lambda h, dc=dc: h.transpose(
                        psR[0:1, 0:128], a2c[:, dc:dc + 1], tidentf[:, :]))
                    ttr = p.last("tensor")
                    p.wait("vector", ttr)
                    tprev = p.op("vector", lambda h, dc=dc: h.tensor_copy(
                        a2row[:, dc * 128:(dc + 1) * 128], psR[0:1, 0:128]))
                p.fence("vector")
                tcpy = p.op("vector", lambda h: h.tensor_copy(a2rowb[:, :], a2row[:, :]))
                p.fence("vector")
                p.wait("tensor", tcpy)
                for hh in range(2):
                    p.op("tensor", lambda h, hh=hh: h.matmul(
                        psR[:, :], tonesr[:, :], a2rowb[:, hh * 512:(hh + 1) * 512],
                        start=True, stop=True))
                    tb_ = p.last("tensor")
                    p.wait("vector", tb_)
                    tbe = p.op("vector", lambda h, hh=hh: h.tensor_copy(
                        a2bcb[:, hh * 512:(hh + 1) * 512], psR[:, :]))
                    p.wait("tensor", tbe)
                p.fence("vector")

            def prep_sign1(it0):
                p.wait("sync", t_w1c, t_w2c)
                for it in range(it0, it0 + 4):
                    i, hh = it // 4, it % 4
                    j = it % 2
                    p.wait("sync", prep_state["t_sg"][j], prep_state["t_so"][j])
                    tw = p.dma("sync", sgbuf[j][:, :],
                               w1T_bf[i * 128:(i + 1) * 128, hh * 1024:(hh + 1) * 1024],
                               stream=f"sg{j}")
                    p.wait("scalar", tw)
                    tsg = p.op("scalar", lambda h, j=j: h.activation(
                        out=sobuf[j][:, :], in_=sgbuf[j][:, :], func=AF.Sign))
                    prep_state["t_sg"][j] = tsg
                    p.fence("scalar")
                    prep_state["t_so"][j] = p.dma(
                        "scalar",
                        sign1_sp[i * 128:(i + 1) * 128, hh * 1024:(hh + 1) * 1024],
                        sobuf[j][:, :], stream=f"so{j}")

            def prep_bw2(i0_):
                p.wait("sync", t_w2c)
                for i in range(i0_, i0_ + 4):
                    j = i % 2
                    p.wait("sync", prep_state["t_sg"][j], prep_state["t_so"][j])
                    tw = p.dma("sync", sgbuf[j][:, :],
                               w2T_bf[i * 128:(i + 1) * 128, :], stream=f"sg{j}")
                    p.wait("scalar", tw)
                    tsg = p.op("scalar", lambda h, j=j: h.activation(
                        out=sobuf[j][:, :], in_=sgbuf[j][:, :], func=AF.Sign))
                    p.wait("vector", tsg)
                    tmu = p.op("vector", lambda h, j=j: h.tensor_mul(
                        sobuf[j][:, :], sobuf[j][:, :], a2bcb[:, :]))
                    prep_state["t_sg"][j] = tmu
                    p.fence("vector")
                    p.wait("scalar", tmu)
                    prep_state["t_so"][j] = p.dma(
                        "scalar", bw2_sp[i * 128:(i + 1) * 128, :],
                        sobuf[j][:, :], stream=f"so{j}")

            prep_chunks = (
                [lambda i=i: prep_alpha1(i * 4) for i in range(8)]
                + [lambda i=i: prep_alpha2(i * 4) for i in range(2)]
                + [prep_fin]
                + [lambda i=i: prep_sign1(i * 4) for i in range(8)]
                + [lambda i=i: prep_bw2(i * 4) for i in range(8)]
            )

            t_xld = {i: p.dma("sync", xg[i][:, :], x_in[i * 128:(i + 1) * 128, :],
                              stream=f"xg{i}") for i in range(min(3, G))}
            st_ = {
                "t_htr2": None, "t_hread2": None, "t_psA_ev": None,
                "t_psO1_rd": None, "t_xo2": [None, None], "t_otr": None,
                "tok_ln": {}, "t_wave_ev": {}, "t_bpre": None, "t_qkrd": None,
                "t_kzev": {}, "t_oscale": {},
            }

            def emit_A_ln(g):
                p.wait("scalar", t_xld[g])
                p.wait("vector", t_xld[g])
                xa = xg[g % 3][:, :]
                negm, rstd, tln = ln_stats(p, s, xa, g % 2)
                p.wait("vector", st_["t_htr2"])
                tok = p.op("vector", lambda h, xa=xa, negm=negm, rstd=rstd:
                           h.tensor_scalar(out=htm2[g % 2][:, :], in0=xa,
                                           scalar1=negm[:, :], scalar2=rstd[:, :],
                                           op0=ALU.add, op1=ALU.mult))
                p.fence("vector")
                st_["tok_ln"][g] = tok

            def emit_A_rest(g):
                tok = st_["tok_ln"][g]
                p.wait("tensor", tok, *t_ic)
                p.wait("tensor", st_["t_hread2"])
                for dc in range(DC):
                    p.op("tensor", lambda h, dc=dc, g=g: h.transpose(
                        psTrX[:, dc * 128:(dc + 1) * 128],
                        htm2[g % 2][:, dc * 128:(dc + 1) * 128], tident[:, :]))
                ttr = p.last("tensor")
                st_["t_htr2"] = ttr
                p.wait("vector", ttr)
                thev = p.op("vector", lambda h: h.tensor_copy(
                    hfmg[:, :, :], psTrX[:, :].rearrange("p (c t) -> p c t", c=DC)))
                p.fence("vector")

                p.wait("tensor", thev, t_setup)
                for wave in range(2):
                    p.wait("tensor", st_["t_psA_ev"])
                    for dc in range(DC):
                        for j in range(4):
                            col = wave * 2048 + j * 512
                            if col < 3 * D:
                                p.op("tensor", lambda h, dc=dc, j=j, col=col: h.matmul(
                                    psA[:, j * 512:(j + 1) * 512],
                                    hfmg[:, dc, :], Wq2[:, dc, col:col + 512],
                                    start=(dc == 0), stop=False))
                            elif col == 3 * D:
                                p.op("tensor", lambda h, dc=dc, j=j: h.matmul(
                                    psA[:, j * 512:j * 512 + 32],
                                    hfmg[:, dc, :], Wwr2[:, dc, :],
                                    start=(dc == 0), stop=False))
                    for j in range(4):
                        col = wave * 2048 + j * 512
                        if col < 3 * D:
                            p.op("tensor", lambda h, j=j, col=col: h.matmul(
                                psA[:, j * 512:(j + 1) * 512],
                                tonesr[:, :], bq2[:, col:col + 512],
                                start=False, stop=True))
                        elif col == 3 * D:
                            p.op("tensor", lambda h, j=j: h.matmul(
                                psA[:, j * 512:j * 512 + 32],
                                tonesr[:, :], bwr2[:, :],
                                start=False, stop=True))
                    tmmA = p.last("tensor")
                    if wave == 1:
                        st_["t_hread2"] = tmmA
                    p.wait("scalar", tmmA, st_["t_bpre"])
                    if wave == 0:
                        p.op("scalar", lambda h: h.activation(
                            out=qtm[:, :], in_=psA[:, 0:1024], func=AF.Identity))
                        st_["t_psA_ev"] = p.op("scalar", lambda h: h.activation(
                            out=ktm[:, :], in_=psA[:, 1024:2048], func=AF.Identity))
                    else:
                        p.op("scalar", lambda h: h.activation(
                            out=vtm[:, :], in_=psA[:, 0:1024], func=AF.Identity))
                        st_["t_psA_ev"] = p.op("scalar", lambda h: h.activation(
                            out=wrtm[:, :], in_=psA[:, 1024:1056], func=AF.Identity))
                    p.fence("scalar")
                st_["t_wave_ev"][g] = st_["t_psA_ev"]

            def emit_B_pre(g):
                p.wait("scalar", st_["t_wave_ev"][g])
                for src_t, dst in ((qtm, rq), (ktm, rk)):
                    tsq = p.op("scalar", lambda h, src_t=src_t: h.activation(
                        out=sqn[:, :], in_=src_t[:, :], func=AF.Square))
                    p.wait("vector", tsq)
                    p.op("vector", lambda h: h.reduce_sum(
                        ssq[:, :], sqn[:, :].rearrange("p (n e) -> p n e", e=64),
                        axis=AX.X))
                    p.fence("vector")
                    tss = p.last("vector")
                    p.wait("scalar", tss)
                    tsq2 = p.op("scalar", lambda h: h.activation(
                        out=ssq[:, :], in_=ssq[:, :], func=AF.Sqrt))
                    p.wait("vector", tsq2)
                    p.op("vector", lambda h: h.tensor_scalar_max(ssq[:, :], ssq[:, :], 1e-12))
                    p.fence("vector")
                    p.op("vector", lambda h, dst=dst: h.reciprocal(dst[:, :], ssq[:, :]))
                    p.fence("vector")
                twr = p.op("scalar", lambda h: h.activation(
                    out=wsig[:, :], in_=wrtm[:, 0:16], func=AF.Sigmoid))
                twr2 = p.op("scalar", lambda h: h.activation(
                    out=rsig[:, :], in_=wrtm[:, 16:32], func=AF.Sigmoid))
                st_["t_qkrd"] = twr2
                p.wait("vector", twr, twr2, st_["t_oscale"].get(g - 2))
                p.op("vector", lambda h, g=g: h.tensor_tensor(
                    vw[g % 2][:, :].rearrange("p (n e) -> p n e", e=64),
                    vtm[:, :].rearrange("p (n e) -> p n e", e=64),
                    wsig[:, :].unsqueeze(2).broadcast_to([128, 16, 64]), op=ALU.mult))
                p.fence("vector")
                p.op("vector", lambda h, g=g: h.tensor_mul(
                    oscale[g % 2][:, :], rq[:, :], rsig[:, :]))
                p.fence("vector")
                p.op("vector", lambda h, g=g: h.tensor_copy(qb16[g % 2][:, :], qtm[:, :]))
                tqb = p.op("vector", lambda h, g=g: h.tensor_tensor(
                    kb16[g % 2][:, :].rearrange("p (n e) -> p n e", e=64),
                    ktm[:, :].rearrange("p (n e) -> p n e", e=64),
                    rk[:, :].unsqueeze(2).broadcast_to([128, 16, 64]), op=ALU.mult))
                p.fence("vector")
                st_["t_bpre"] = tqb
                st_["t_kzev"][g] = tqb

            def emit_C(g):
                tqb = st_["t_kzev"][g]
                p.wait("tensor", tqb)
                for dc in range(DC):
                    p.op("tensor", lambda h, dc=dc, g=g: h.transpose(
                        psTrX[:, dc * 128:(dc + 1) * 128],
                        qb16[g % 2][:, dc * 128:(dc + 1) * 128], tident[:, :]))
                ttq = p.last("tensor")
                st_["t_htr2"] = ttq
                p.wait("vector", ttq)
                tevq = p.op("vector", lambda h: h.tensor_copy(
                    qfmg[:, :, :], psTrX[:, :].rearrange("p (c t) -> p c t", c=DC)))
                p.wait("tensor", tevq)
                for dc in range(DC):
                    p.op("tensor", lambda h, dc=dc, g=g: h.transpose(
                        psTrX[:, dc * 128:(dc + 1) * 128],
                        kb16[g % 2][:, dc * 128:(dc + 1) * 128], tident[:, :]))
                ttk = p.last("tensor")
                st_["t_htr2"] = ttk
                p.wait("vector", ttk)
                if g == 0:
                    p.op("vector", lambda h: h.memset(kzg[:, :, :], 0.0))
                    p.fence("vector")
                p.op("vector", lambda h: h.tensor_copy(
                    kzg[0:64, :, :][0:64, 0::2, :],
                    psTrX[0:64, :].rearrange("p (c t) -> p c t", c=DC)))
                p.fence("vector")
                tevk = p.op("vector", lambda h: h.tensor_copy(
                    kzg[64:128, :, :][0:64, 1::2, :],
                    psTrX[64:128, :].rearrange("p (c t) -> p c t", c=DC)))
                p.fence("vector")

                p.wait("tensor", tevk, st_["t_psO1_rd"], st_["t_psA_ev"])
                for n in range(16):
                    c = n // 2
                    p.op("tensor", lambda h, n=n, c=c: h.matmul(
                        psA[:, n * 128:(n + 1) * 128],
                        kzg[:, n, :], qfmg[:, c, :],
                        start=True, stop=True))
                tg = p.last("tensor")
                p.wait("vector", tg)
                tge = p.op("vector", lambda h: h.tensor_tensor(
                    Gsb16[:, :].rearrange("p (n t) -> p n t", n=16),
                    psA[:, :].rearrange("p (n t) -> p n t", n=16),
                    tmaskbd[:, :].unsqueeze(1).broadcast_to([128, 16, 128]),
                    op=ALU.mult))
                st_["t_psA_ev"] = tge
                p.fence("vector")
                p.wait("tensor", tge)
                for n in range(16):
                    p.op("tensor", lambda h, n=n, g=g: h.matmul(
                        psO1[:, n * 64:(n + 1) * 64], Gsb16[:, n * 128:(n + 1) * 128],
                        vw[g % 2][:, n * 64:(n + 1) * 64], start=True, stop=True))
                tapp = p.last("tensor")
                p.wait("vector", tapp)
                tob = p.op("vector", lambda h, g=g: h.tensor_tensor(
                    ob16[:, :].rearrange("p (n e) -> p n e", e=64),
                    psO1[:, :].rearrange("p (n e) -> p n e", e=64),
                    oscale[g % 2][:, :].unsqueeze(2).broadcast_to([128, 16, 64]), op=ALU.mult))
                st_["t_oscale"][g] = tob
                st_["t_psO1_rd"] = tob
                p.fence("vector")
                p.wait("tensor", tob, st_["t_otr"])
                for dc in range(DC):
                    p.op("tensor", lambda h, dc=dc: h.transpose(
                        psTrX[:, dc * 128:(dc + 1) * 128],
                        ob16[:, dc * 128:(dc + 1) * 128], tident[:, :]))
                tto = p.last("tensor")
                st_["t_htr2"] = tto
                p.wait("vector", tto)
                tevo = p.op("vector", lambda h: h.tensor_copy(
                    ofmg[:, :, :], psTrX[:, :].rearrange("p (c t) -> p c t", c=DC)))
                st_["t_otr"] = tevo
                p.fence("vector")
                p.wait("tensor", tevo)
                for dc in range(DC):
                    p.op("tensor", lambda h, dc=dc: h.matmul(
                        psO1[:, 0:512], ofmg[:, dc, :], Wo2[:, dc, 0:512],
                        start=(dc == 0), stop=False))
                    p.op("tensor", lambda h, dc=dc: h.matmul(
                        psO1[:, 512:1024], ofmg[:, dc, :], Wo2[:, dc, 512:1024],
                        start=(dc == 0), stop=False))
                p.op("tensor", lambda h: h.matmul(
                    psO1[:, 0:512], tonesr[:, :], bo2[:, 0:512], start=False, stop=True))
                p.op("tensor", lambda h: h.matmul(
                    psO1[:, 512:1024], tonesr[:, :], bo2[:, 512:1024], start=False, stop=True))
                tWo = p.last("tensor")
                p.wait("scalar", tWo)
                ty3 = p.op("scalar", lambda h: h.activation(
                    out=yo2[:, :], in_=psO1[:, :], func=AF.Identity))
                st_["t_psO1_rd"] = ty3
                p.fence("scalar")
                p.wait("vector", ty3)
                negm, rstd, tln = ln_stats(p, s, yo2[:, :], g % 2)
                p.op("vector", lambda h, negm=negm, rstd=rstd: h.tensor_scalar(
                    out=yt3[:, :], in0=yo2[:, :], scalar1=negm[:, :],
                    scalar2=rstd[:, :], op0=ALU.add, op1=ALU.mult))
                p.fence("vector")
                p.wait("vector", st_["t_xo2"][g % 2])
                tres = p.op("vector", lambda h, g=g: h.tensor_add(
                    xo2[g % 2][:, :], xg[g % 3][:, :], yt3[:, :]))
                p.fence("vector")
                p.wait("sync", tres)
                if g + 3 < G:
                    t_xld[g + 3] = p.dma("sync", xg[(g + 3) % 3][:, :],
                                         x_in[(g + 3) * 128:(g + 4) * 128, :],
                                         stream=f"xg{(g + 3) % 3}")
                tst = p.dma("sync", x3h[g * 128:(g + 1) * 128, :], xo2[g % 2][:, :],
                            stream=f"xs{g % 2}")
                if dbg_x3:
                    tst = p.dma("sync", x3_out[g * 128:(g + 1) * 128, :], xo2[g % 2][:, :],
                                stream=f"xs{g % 2}")
                st_["t_xo2"][g % 2] = tst

            for g in range(G):
                emit_A_ln(g)
                if g >= 1:
                    emit_B_pre(g - 1)
                if g >= 2:
                    emit_C(g - 2)
                emit_A_rest(g)
                if g - 2 >= 0 and g - 2 < len(prep_chunks):
                    prep_chunks[g - 2]()
            for ch in prep_chunks[max(0, G - 2):]:
                ch()
            emit_B_pre(G - 1)
            emit_C(G - 2)
            emit_C(G - 1)
            p.barrier()

        if not do_ffn:
            tcp = p.dma("sync", y_out, x3h, stream="fin")
            p.barrier()
            p.emit()
            print("launch2 built (attn only)")
            return nc

        p.scope = "l2prep"
        with ExitStack() as phc:
            xb3 = [_sb(nc, phc, f"xb3{i}", [128, 4, D], F32) for i in range(2)]
            htm3b = _sb(nc, phc, "htm3b", [128, D], BF16)
            hfm3b = [_sb(nc, phc, f"hfm3b{i}", [128, DC, 512], BF16) for i in range(2)]
            sl1 = [_sb(nc, phc, f"sl1{i}", [128, DC, 256], BF16) for i in range(3)]
            hmid = _sb(nc, phc, "hmid", [128, 32, 512], BF16)
            w2sl = [_sb(nc, phc, f"w2sl{i}", [128, 32, 128], BF16) for i in range(2)]
            o2fm = _sb(nc, phc, "o2fm", [128, DC, 512], BF16)
            tb1c = _sb(nc, phc, "tb1c", [128, 32], F32)
            tb2c = _sb(nc, phc, "tb2c", [128, 8], F32)
            y2tm = _sb(nc, phc, "y2tm", [128, D], F32)
            yt4 = _sb(nc, phc, "yt4", [128, D], F32)
            yo4 = [_sb(nc, phc, f"yo4{i}", [128, D], F32) for i in range(2)]
            psM1 = [_psum(nc, phc, f"psM1{i}", [128, 512], F32) for i in range(2)]
            psFM = [_psum(nc, phc, f"psFM{i}", [128, 512], F32) for i in range(2)]
            psTrF = _psum(nc, phc, "psTrF", [128, D], BF16)

            tb = [p.dma("sync", tb1c[:, :], b1_cols, stream="ic5"),
                  p.dma("sync", tb2c[:, :], b2_cols, stream="ic6")]
            sign1R = sign1_sp.rearrange("(c p) n -> p c n", p=128)
            bw2R = bw2_sp.rearrange("(c p) n -> p c n", p=128)

            t_x3ld = {}
            for i in range(min(2, TC)):
                t_x3ld[i] = p.dma(
                    "sync", xb3[i % 2][:, :, :],
                    x3h[i * 512:(i + 1) * 512, :].rearrange("(a p) d -> p a d", p=128),
                    stream=f"x3{'a' if i % 2 == 0 else 'b'}")
            t_htr3 = [None]
            t_h3rd = None
            t_hmid_rdx = [None]
            t_o2rdx = [None]
            t_trF = [None]
            t_th3 = {}
            t_sl = [None, None]
            t_slld = {}
            t_w2l = [None, None]
            t_w2ld = {}
            t_psM1ev = [None, None]
            t_psFMev = [None, None]

            t_yo = [None, None]

            def pf_front(tc):
                ttrs = []
                for st_ in range(4):
                    xa = xb3[tc % 2][:, st_, :]
                    p.wait("scalar", t_x3ld[tc])
                    p.wait("vector", t_x3ld[tc])
                    negm, rstd, tln = ln_stats(p, s, xa, st_ % 2)
                    p.wait("vector", t_htr3[0])
                    tok = p.op("vector", lambda h, xa=xa, negm=negm, rstd=rstd:
                               h.tensor_scalar(out=htm3b[:, :], in0=xa,
                                               scalar1=negm[:, :], scalar2=rstd[:, :],
                                               op0=ALU.add, op1=ALU.mult))
                    p.fence("vector")
                    p.wait("tensor", tok, t_hmid_rdx[0], t_trF[0])
                    if ttrs:
                        p.wait("tensor", ttrs[-1])
                    p.wait("tensor", t_h3rd)
                    for dc in range(DC):
                        p.op("tensor", lambda h, dc=dc: h.transpose(
                            psTrF[:, dc * 128:(dc + 1) * 128],
                            htm3b[:, dc * 128:(dc + 1) * 128], tident[:, :]))
                    ttr = p.last("tensor")
                    t_htr3[0] = ttr
                    p.wait("vector", ttr)
                    tev = p.op("vector", lambda h, tc=tc, st_=st_: h.tensor_copy(
                        hfm3b[tc % 2][:, :, st_ * 128:(st_ + 1) * 128],
                        psTrF[:, :].rearrange("p (c t) -> p c t", c=DC)))
                    t_trF[0] = tev
                    ttrs.append(tev)
                t_th3[tc] = ttrs[-1]

            def pf_body(tc):
                th3 = t_th3[tc]
                p.wait("tensor", th3, t_hmid_rdx[0])
                if tc == 0:
                    for oo in range(3):
                        t_slld[oo] = (oo, p.dma("sync", sl1[oo][:, :, :],
                                                sign1R[:, :, oo * 256:(oo + 1) * 256],
                                                stream=f"sl{oo}"))
                for oc in range(32):
                    bi, sub = oc // 2, oc % 2
                    slot = oc % 2
                    j, tld = t_slld[bi]
                    p.wait("tensor", tld, t_psM1ev[slot])
                    for dc in range(DC):
                        p.op("tensor", lambda h, j=j, slot=slot, dc=dc, sub=sub: h.matmul(
                            psM1[slot][:, :], sl1[j][:, dc, sub * 128:(sub + 1) * 128],
                            hfm3b[tc % 2][:, dc, :],
                            start=(dc == 0), stop=(dc == DC - 1)))
                    tmm = p.last("tensor")
                    if sub == 1:
                        nxt = bi + 3 if bi + 3 < 16 else (bi + 3 - 16 if tc + 1 < TC else None)
                        if nxt is not None:
                            p.wait("sync", tmm)
                            t_slld[nxt] = (j, p.dma("sync", sl1[j][:, :, :],
                                                    sign1R[:, :, nxt * 256:(nxt + 1) * 256],
                                                    stream=f"sl{j}"))
                    p.wait("scalar", tmm, *tb)
                    tev = p.op("scalar", lambda h, slot=slot, oc=oc: h.activation(
                        out=hmid[:, oc, :], in_=psM1[slot][:, :], func=AF.Gelu,
                        bias=tb1c[:, oc:oc + 1], scale=a1c[:, oc:oc + 1]))
                    t_psM1ev[slot] = tev
                thm = t_psM1ev[(31) % 2]

                p.wait("tensor", thm, t_o2rdx[0])
                if tc == 0:
                    for oo in range(2):
                        t_w2ld[oo] = p.dma("sync", w2sl[oo][:, :, :],
                                           bw2R[:, :, oo * 128:(oo + 1) * 128],
                                           stream=f"w2{oo}")
                for dc in range(DC):
                    j = dc % 2
                    slot = dc % 2
                    p.wait("tensor", t_w2ld[dc], t_psFMev[slot])
                    for hc in range(32):
                        p.op("tensor", lambda h, j=j, slot=slot, hc=hc: h.matmul(
                            psFM[slot][:, :], w2sl[j][:, hc, :], hmid[:, hc, :],
                            start=(hc == 0), stop=(hc == 31)))
                    tmm = p.last("tensor")
                    nxt = dc + 2 if dc + 2 < DC else (dc + 2 - DC if tc + 1 < TC else None)
                    if nxt is not None:
                        p.wait("sync", tmm)
                        t_w2ld[nxt] = p.dma("sync", w2sl[j][:, :, :],
                                            bw2R[:, :, nxt * 128:(nxt + 1) * 128],
                                            stream=f"w2{j}")
                    p.wait("scalar", tmm)
                    tev = p.op("scalar", lambda h, slot=slot, dc=dc: h.activation(
                        out=o2fm[:, dc, :], in_=psFM[slot][:, :], func=AF.Identity,
                        bias=tb2c[:, dc:dc + 1]))
                    t_psFMev[slot] = tev
                to2 = t_psFMev[(DC - 1) % 2]
                t_hmid_rdx[0] = p.last("tensor")

                p.wait("tensor", to2)
                for st_ in range(4):
                    gti = tc * 4 + st_
                    p.wait("tensor", t_o2rdx[0], t_trF[0])
                    for dc in range(DC):
                        p.op("tensor", lambda h, dc=dc, st_=st_: h.transpose(
                            psTrF[:, dc * 128:(dc + 1) * 128],
                            o2fm[:, dc, st_ * 128:(st_ + 1) * 128], tident[:, :]))
                    ttro = p.last("tensor")
                    p.wait("vector", ttro)
                    ty4 = p.op("vector", lambda h: h.tensor_copy(y2tm[:, :], psTrF[:, :]))
                    t_o2rdx[0] = ty4
                    t_trF[0] = ty4
                    p.fence("vector")
                    p.wait("scalar", ty4)
                    negm, rstd, tln = ln_stats(p, s, y2tm[:, :], st_ % 2)
                    p.op("vector", lambda h, negm=negm, rstd=rstd: h.tensor_scalar(
                        out=yt4[:, :], in0=y2tm[:, :], scalar1=negm[:, :],
                        scalar2=rstd[:, :], op0=ALU.add, op1=ALU.mult))
                    p.fence("vector")
                    p.wait("vector", t_yo[st_ % 2])
                    tres = p.op("vector", lambda h, tc=tc, st_=st_: h.tensor_add(
                        yo4[st_ % 2][:, :], xb3[tc % 2][:, st_, :], yt4[:, :]))
                    p.fence("vector")
                    p.wait("sync", tres)
                    t_yo[st_ % 2] = p.dma(
                        "sync", y_out[gti * 128:(gti + 1) * 128, :], yo4[st_ % 2][:, :],
                        stream=f"yo{st_ % 2}")
                    if st_ == 3 and tc + 2 < TC:
                        t_x3ld[tc + 2] = p.dma(
                            "sync", xb3[tc % 2][:, :, :],
                            x3h[(tc + 2) * 512:(tc + 3) * 512, :]
                            .rearrange("(a p) d -> p a d", p=128),
                            stream=f"x3{'a' if tc % 2 == 0 else 'b'}")

            for tc in range(TC):
                pf_front(tc)
                if tc >= 1:
                    pf_body(tc - 1)
            pf_body(TC - 1)
            p.barrier()
        p.barrier()
        p.emit()
    print(f"launch2 built: ops={ {k: len(v) for k, v in p.items.items()} }")
    return nc


def _f32(a):
    return np.ascontiguousarray(np.asarray(a, np.float32))


def _prep_launch1(ins, c, T=4096):
    m = {}
    m["x"] = _f32(ins["L0_attn_x"][c][:T])
    m["rn_cols"] = np.ascontiguousarray(
        (1.0 / np.arange(1, T + 1, dtype=np.float64)).astype(np.float32)
        .reshape(T // 128, 128).T)
    Wq_ = np.asarray(ins["L0_attn_W_qkv"], np.float32)
    m["Wqkv"] = _f32(np.concatenate([Wq_[:, 2048:3072], Wq_[:, 1024:2048], Wq_[:, 0:1024]], axis=1))
    bq_ = np.asarray(ins["L0_attn_b_qkv"], np.float32)
    m["bq_cols"] = _f32(np.concatenate([bq_[2048:3072], bq_[1024:2048], bq_[0:1024]]).reshape(24, 128).T)
    Wv = np.asarray(ins["L0_attn_W_vconv"], np.float32)
    m["wv_cols"] = _f32(np.concatenate(
        [Wv[:, j].reshape(8, 128).T for j in range(3)], axis=1))
    m["identb"] = np.eye(128, dtype=BF)
    m["identf"] = np.eye(128, dtype=np.float32)
    i_ = np.arange(128)
    m["maskadd"] = np.where(i_[None, :] <= i_[:, None], 0.0, -1e30).astype(np.float32)
    m["WQT"] = _f32(np.asarray(ins["L0_ffn_W_Q"], np.float32).T)
    m["WKT"] = _f32(np.asarray(ins["L0_ffn_W_K"], np.float32).T)
    m["WinvT"] = _f32(np.asarray(ins["L0_ffn_W_inv"], np.float32).T)
    m["rw1T"] = _f32(np.asarray(ins["L0_ffn_router_w1"], np.float32).T)
    m["rw2T"] = _f32(np.asarray(ins["L0_ffn_router_w2"], np.float32).T)
    m["WOT"] = _f32(np.asarray(ins["L0_ffn_W_O"], np.float32).T)
    m["Varr_f"] = _f32(np.asarray(ins["L0_ffn_V_fwd"], np.float32).transpose(1, 0, 2).reshape(1024, 512))
    m["Warr_f"] = _f32(np.asarray(ins["L0_ffn_W_fwd"], np.float32).transpose(1, 0, 2).reshape(1024, 512))
    m["Varr_i"] = _f32(np.asarray(ins["L0_ffn_W_inv_exp"], np.float32).transpose(1, 0, 2).reshape(1024, 512))
    m["Warr_i"] = _f32(np.asarray(ins["L0_ffn_V_inv"], np.float32).transpose(1, 0, 2).reshape(1024, 512))
    m["UT_f"] = _f32(np.asarray(ins["L0_ffn_U_fwd"], np.float32).transpose(0, 2, 1).reshape(512, 1024))
    m["UT_i"] = _f32(np.asarray(ins["L0_ffn_U_inv"], np.float32).transpose(0, 2, 1).reshape(512, 1024))
    m["b_f"] = _f32(ins["L0_ffn_b_fwd"])
    m["b_i"] = _f32(ins["L0_ffn_b_inv"])
    m["rb1_cols"] = _f32(np.asarray(ins["L0_ffn_router_b1"], np.float32).reshape(8, 128).T)
    m["rb2"] = _f32(np.asarray(ins["L0_ffn_router_b2"], np.float32).reshape(1, 8))
    m["eb"] = _f32(np.asarray(ins["L0_ffn_expert_bias"], np.float32).reshape(1, 8))
    m["alpha"] = _f32(np.broadcast_to(np.asarray(ins["L0_ffn_alpha_bi"], np.float32), (128, 1)))
    E = np.zeros((8, 512), dtype=BF)
    for cc in range(4):
        for pp in range(128):
            E[2 * cc + pp // 64, cc * 128 + pp] = 1.0
    m["Ecat"] = E
    m["onesrow"] = np.ones((1, 128), dtype=BF)
    m["ones512"] = np.ones((128, 512), np.float32)
    return m


def _il_order(x):
    S, Tw, d = x.shape
    return np.ascontiguousarray(
        x.reshape(8, Tw // 16, 16, d).transpose(1, 2, 0, 3).reshape(8 * Tw, d))


def _il_unorder(rows, Tw):
    d = rows.shape[1]
    return np.ascontiguousarray(
        rows.reshape(Tw // 16, 16, 8, d).transpose(2, 0, 1, 3).reshape(8, Tw, d))


def _prep_launch2(xwin, ins):
    m = {}
    m["x"] = _f32(_il_order(xwin))
    m["Wqkv2"] = _f32(np.concatenate([
        np.asarray(ins["L1_attn_Wq_w"], np.float32).T,
        np.asarray(ins["L1_attn_Wk_w"], np.float32).T,
        np.asarray(ins["L1_attn_Wv_w"], np.float32).T], axis=1))
    m["bqkv2"] = _f32(np.concatenate([
        ins["L1_attn_Wq_b"], ins["L1_attn_Wk_b"], ins["L1_attn_Wv_b"]]).reshape(1, -1))
    m["Wwr"] = _f32(np.concatenate([
        np.asarray(ins["L1_attn_Ww_w"], np.float32).T,
        np.asarray(ins["L1_attn_Wr_w"], np.float32).T], axis=1))
    m["bwr"] = _f32(np.concatenate([ins["L1_attn_Ww_b"], ins["L1_attn_Wr_b"]]).reshape(1, -1))
    m["WoT"] = _f32(np.asarray(ins["L1_attn_Wo_w"], np.float32).T)
    m["bo"] = _f32(np.asarray(ins["L1_attn_Wo_b"], np.float32).reshape(1, -1))
    i_ = np.arange(128)
    m["bdmask"] = ((i_[:, None] // 8) == (i_[None, :] // 8)).astype(BF)
    m["identb"] = np.eye(128, dtype=BF)
    m["identf"] = np.eye(128, dtype=np.float32)
    m["onesrow"] = np.ones((1, 128), dtype=BF)
    m["w1n"] = _f32(ins["L1_ffn_w1"])
    m["w1T"] = _f32(np.asarray(ins["L1_ffn_w1"], np.float32).T)
    m["w2n"] = _f32(ins["L1_ffn_w2"])
    m["w2T"] = _f32(np.asarray(ins["L1_ffn_w2"], np.float32).T)
    m["b1_cols"] = _f32(np.asarray(ins["L1_ffn_b1"], np.float32).reshape(32, 128).T)
    m["b2_cols"] = _f32(np.asarray(ins["L1_ffn_b2"], np.float32).reshape(8, 128).T)
    return m


_BUILD_CACHE = {}


def kernel(**inputs):
    T = 4096
    if "l1" not in _BUILD_CACHE:
        _BUILD_CACHE["l1"] = build_launch1(T=T)
        _BUILD_CACHE["l2"] = build_launch2(R=T)
    nc1, nc2 = _BUILD_CACHE["l1"], _BUILD_CACHE["l2"]

    in_maps1 = [_prep_launch1(inputs, c, T) for c in range(8)]
    res1 = run_bass_kernel_spmd(nc1, in_maps1, core_ids=list(range(8)))
    x2 = np.stack([np.asarray(res1.results[c]["x2"]) for c in range(8)], axis=0)

    Tw = T // 8
    in_maps2 = [_prep_launch2(x2[:, c * Tw:(c + 1) * Tw, :], inputs) for c in range(8)]
    res2 = run_bass_kernel_spmd(nc2, in_maps2, core_ids=list(range(8)))
    out = np.concatenate(
        [_il_unorder(np.asarray(res2.results[c]["y"]), Tw) for c in range(8)], axis=1)
    return out.astype(np.float32)
